# version 15
# baseline (speedup 1.0000x reference)
import numpy as np
import ml_dtypes
import concourse.bass as bass
import concourse.mybir as mybir
from concourse.bass_utils import run_bass_kernel_spmd

F32 = mybir.dt.float32
BF16 = mybir.dt.bfloat16
AF = mybir.ActivationFunctionType
OP = mybir.AluOpType
AX = mybir.AxisListType
BF16NP = ml_dtypes.bfloat16

NT = 17
TTOT = NT * 128
DM = 2048
DI = 4096
DH = 8192
NH = 64

ENGINES = ("sync", "scalar", "vector", "tensor", "gpsimd")


def bfree(ap2d, inner):
    return ap2d.unsqueeze(2).broadcast_to([ap2d.shape[0], ap2d.shape[1], inner])


def r3(ap2d, a, b):
    return ap2d.rearrange("p (a b) -> p a b", a=a, b=b)


class Task:
    __slots__ = ("eng", "emit", "reads", "writes", "dma", "seq", "done", "waits", "prewait")

    def __init__(self, eng, emit, reads, writes, dma):
        self.eng, self.emit = eng, emit
        self.reads, self.writes = list(reads), list(writes)
        self.dma = dma


class Phase:

    def __init__(self, C, name):
        self.C = C
        self.name = name
        self.tasks = []

    def add(self, eng, emit, r=(), w=(), dma=False):
        self.tasks.append(Task(eng, emit, r, w, dma))

    def t_dma(self, eng, out_ap, in_ap, r=(), w=(), **kw):
        def emit(e):
            return e.dma_start(out_ap, in_ap, **kw)
        self.add(eng, emit, r=r, w=w, dma=True)

    NDMA = 6

    def run(self):
        C = self.C
        nc = C.nc
        cnt = {}
        last_writer = C.last_writer
        readers = C.readers
        for t in self.tasks:
            if t.dma:
                k = C.dma_idx.get(t.eng, 0)
                C.dma_idx[t.eng] = k + 1
                slot = k % self.NDMA
                use = k // self.NDMA
                sname = f"{t.eng}D{slot}"
                t.prewait = (sname, 16 * use) if use > 0 else None
                t.done = (sname, 16 * (use + 1))
            else:
                key = (t.eng, t.dma)
                base = C.semval(self._sname(t.eng, t.dma))
                cnt[key] = cnt.get(key, 0) + 1
                t.seq = cnt[key]
                t.done = (self._sname(t.eng, t.dma), base + t.seq)
                t.prewait = None
        for t in self.tasks:
            waits = {}

            def need(dep, raw=False):
                if dep is None or dep is t:
                    return
                if (raw and dep.eng == t.eng and not dep.dma and not t.dma
                        and dep.eng != "gpsimd"):
                    return
                sname, val = dep.done
                waits[sname] = max(waits.get(sname, 0), val)

            for tok in t.reads:
                need(last_writer.get(tok), raw=True)
            for tok in t.writes:
                need(last_writer.get(tok))
                for rd in readers.get(tok, ()):
                    need(rd)
            t.waits = waits
            for tok in t.reads:
                readers.setdefault(tok, []).append(t)
            for tok in t.writes:
                last_writer[tok] = t
                readers[tok] = []
        with nc.Block(self.name) as blk:
            for eng in ENGINES:
                mytasks = [t for t in self.tasks if t.eng == eng]
                if not mytasks:
                    continue

                def body(e, mytasks=mytasks, eng=eng):
                    hwm = {}
                    for t in mytasks:
                        allw = dict(t.waits)
                        if t.prewait is not None:
                            sn, v = t.prewait
                            allw[sn] = max(allw.get(sn, 0), v)
                        for sname, val in sorted(allw.items()):
                            if hwm.get(sname, -1) >= val:
                                continue
                            e.wait_ge(C.sem(sname), val)
                            hwm[sname] = val
                        safe = (not t.dma) and (
                            C.race_detect or eng in ("vector", "scalar"))
                        ee = SafeEng(e, C, eng) if safe else e
                        ins = t.emit(ee)
                        sname, val = t.done
                        inc = 16 if t.dma else 1
                        if ins is None or safe:
                            e.sem_inc(C.sem(sname), inc)
                        else:
                            ins.then_inc(C.sem(sname), inc)
                        C.setsem(sname, val)

                getattr(blk, eng)(body)

    @staticmethod
    def _sname(eng, dma):
        return f"{eng}{'D' if dma else ''}"


class SafeEng:

    def __init__(self, eng, C, name):
        self._e = eng
        self._C = C
        self._n = name

    def __getattr__(self, name):
        attr = getattr(self._e, name)
        if not callable(attr) or name in ("wait_ge", "sem_inc", "nop",
                                          "dma_start", "load_library"):
            return attr

        def wrap(*a, **k):
            r = attr(*a, **k)
            C = self._C
            sn = "safe_" + self._n
            r.then_inc(C.sem(sn), 1)
            v = C.inc_safe(sn)
            self._e.wait_ge(C.sem(sn), v)
            return r
        return wrap


class Ctx:
    def __init__(self, dbg=(), race_detect=False):
        self.race_detect = race_detect
        self.dbg = set(dbg)
        self.nc = bass.Bass("TRN2", target_bir_lowering=False, debug=False,
                            detect_race_conditions=race_detect)
        self._sems = {}
        self._semn = {}
        self.dma_idx = {}
        self._safe = {}
        self.last_writer = {}
        self.readers = {}

    def inc_safe(self, name):
        self._safe[name] = self._safe.get(name, 0) + 1
        return self._safe[name]

    def sem(self, name):
        if name not in self._sems:
            self._sems[name] = self.nc.alloc_semaphore(name)
            self._semn[name] = 0
        return self._sems[name]

    def semval(self, name):
        self.sem(name)
        return self._semn[name]

    def setsem(self, name, v):
        self._semn[name] = max(self._semn[name], v)


def build_program(dbg=(), race_detect=False):
    C = Ctx(dbg, race_detect)
    nc = C.nc
    E = {}

    def dram(name, shape, dtype, kind="Internal"):
        k = "ExternalOutput" if name in C.dbg else kind
        E["d_" + name] = nc.dram_tensor(name, shape, dtype, kind=k).ap()

    for name, shape in [
        ("u_tok", [TTOT, DM]), ("w_in", [DM, 8384]),
        ("w_upT", [DM, DH]), ("w_dnT", [DH, DM]), ("w_out", [DI, DM]),
        ("cw", [128, 128]), ("cb", [128, 32]), ("nsc", [128, 32]),
        ("usc", [128, 64]), ("dsc", [128, 16]), ("vecs", [3, 64]),
        ("osc", [1, 1]), ("flag", [1, 1]),
    ]:
        E["d_" + name] = nc.dram_tensor(name, shape, F32, kind="ExternalInput").ap()
    E["d_out"] = nc.dram_tensor("out", [1024, DM], F32, kind="ExternalOutput").ap()

    dram("xnf", [16, 128, TTOT], BF16)
    dram("wb", [DM, 8384], BF16)
    dram("wub", [DM, DH], BF16)
    dram("wdb", [DH, DM], BF16)
    dram("wob", [DI, DM], BF16)
    dram("yg", [8, 128, DI], BF16)
    dram("zs", [8, 128, DI], BF16)
    dram("csd", [32, NH * 64], F32)
    dram("trow", [1, 2048], F32)
    dram("nrow", [1, 64], F32)
    if "stats" in C.dbg:
        E["d_pstat"] = nc.dram_tensor("pstat", [128, 6 * NT], F32,
                                      kind="ExternalOutput").ap()
    if "probe2" in C.dbg:
        for nm, shape in [("py", [128, DI]), ("patt", [128, NH * 128]),
                          ("pG", [128, 64]), ("pD", [128, NH * 64]),
                          ("phb", [64, DI]), ("pqs", [128, 64]),
                          ("pkd", [128, 128]), ("pvs", [128, DI])]:
            E["d_" + nm] = nc.dram_tensor(nm, shape, F32,
                                          kind="ExternalOutput").ap()
    if "probe" in C.dbg:
        E["d_ph0"] = nc.dram_tensor("ph0", [64, DI], F32,
                                    kind="ExternalOutput").ap()
        E["d_pdecp"] = nc.dram_tensor("pdecp", [128, 8 * 64], F32,
                                      kind="ExternalOutput").ap()
        E["d_pkq"] = nc.dram_tensor("pkq", [128, 2 * 16 * 64], F32,
                                    kind="ExternalOutput").ap()
        E["d_pvt"] = nc.dram_tensor("pvt", [128, 4 * DI], F32,
                                    kind="ExternalOutput").ap()

    A = nc.alloc_sbuf_tensor
    for nm, shape, dt in [
        ("causal", [128, 64], BF16), ("id_bf", [128, 128], BF16),
        ("id_f32", [128, 128], F32), ("cw_sb", [128, 128], F32),
        ("cb_sb", [128, 32], F32), ("nsc_sb", [128, 32], F32),
        ("usc_sb", [128, 64], F32), ("dsc_sb", [128, 16], F32),
        ("arow", [1, 64], F32), ("brow", [1, 64], F32), ("drow", [1, 64], F32),
        ("oscv", [1, 1], F32), ("flagv", [1, 1], F32), ("OSCB", [128, 1], F32),
        ("NEGA", [128, 64], F32), ("DTB", [128, 64], F32), ("DP", [128, 64], F32),
        ("FLG", [64, 1], F32),
        ("mu", [128, NT], F32), ("rstd", [128, NT], F32), ("nmr", [128, NT], F32),
        ("musum", [128, NT], F32), ("sum2", [128, NT], F32),
        ("stat_t", [128, NT], F32),
    ]:
        E[nm] = A(nm, shape, dt)

    import contextlib
    early_stack = contextlib.ExitStack()
    for nm, shape, dt in [
        ("ones", [128, 2048], F32), ("reset", [64, 2048], F32),
        ("dtv", [128, 16 * 64], F32), ("CS", [64, 2048], F32),
        ("GCS", [64, 1024], F32),
    ]:
        E[nm] = early_stack.enter_context(nc.sbuf_tensor(nm, shape, dt,
                                                         side="right"))

    _p0_consts(C, E)
    _p1_stats(C, E)
    _p2_xnf(C, E)

    ssd_stack = contextlib.ExitStack()
    for nm, shape, dt in [
        ("k_tok", [128, 16 * 64], BF16), ("q_tok", [128, 16 * 64], BF16),
        ("CS_T", [128, 16 * 64], F32), ("ECS_T", [128, 16 * 64], F32),
        ("DEC_T", [128, 16 * 64], F32), ("DECP_T", [128, 8 * 64], F32),
        ("DBC64", [64, 2048], F32), ("h_st", [64, DI], F32),
        ("vt", [128, 4 * DI], BF16),
        ("xtail", [128, 32 * 12], F32), ("sqs_cv", [128, 512], F32),
    ]:
        E[nm] = ssd_stack.enter_context(nc.sbuf_tensor(nm, shape, dt))

    _p3_bcdt(C, E)
    _p4_csfam(C, E)
    early_stack.close()
    _p5_z(C, E)
    _p6_main(C, E)
    ssd_stack.close()
    _p7_ffn_out(C, E)
    return nc


def _p0_consts(C, E):
    nc = C.nc
    P = Phase(C, "p0")
    for dst, src in [
        (E["cw_sb"][:, :], E["d_cw"]), (E["cb_sb"][:, :], E["d_cb"]),
        (E["nsc_sb"][:, :], E["d_nsc"]), (E["usc_sb"][:, :], E["d_usc"]),
        (E["dsc_sb"][:, :], E["d_dsc"]), (E["oscv"][:, :], E["d_osc"]),
        (E["flagv"][:, :], E["d_flag"]),
        (E["arow"][:, :], E["d_vecs"][0:1, :]),
        (E["brow"][:, :], E["d_vecs"][1:2, :]),
        (E["drow"][:, :], E["d_vecs"][2:3, :]),
    ]:
        nm = src.name if hasattr(src, "name") else "x"
        P.t_dma("sync", dst, src, w=[id(dst.tensor) if hasattr(dst, "tensor") else nm])
    P.tasks = []
    loads = [
        ("cw_sb", E["d_cw"]), ("cb_sb", E["d_cb"]), ("nsc_sb", E["d_nsc"]),
        ("usc_sb", E["d_usc"]), ("dsc_sb", E["d_dsc"]), ("oscv", E["d_osc"]),
        ("flagv", E["d_flag"]),
    ]
    for nm, src in loads:
        P.t_dma("sync", E[nm][:, :], src, w=[nm])
    P.t_dma("sync", E["arow"][:, :], E["d_vecs"][0:1, :], w=["arow"])
    P.t_dma("sync", E["brow"][:, :], E["d_vecs"][1:2, :], w=["brow"])
    P.t_dma("sync", E["drow"][:, :], E["d_vecs"][2:3, :], w=["drow"])

    P.add("scalar", lambda e: e.activation(E["arow"][:, :], E["arow"][:, :], AF.Exp),
          r=["arow"], w=["arow"])
    P.add("vector", lambda e: e.tensor_scalar_mul(E["arow"][:, :], E["arow"][:, :], -1.0),
          r=["arow"], w=["arow"])
    P.t_dma("gpsimd", E["d_nrow"], E["arow"][:, :], r=["arow"],
            w=["nrow_dram"])

    P.add("gpsimd", lambda e: e.memset(E["ones"][:, :], 1.0), w=["ones"])
    P.add("gpsimd", lambda e: e.affine_select(
        E["reset"][:, :], E["ones"][0:64, :], [[0, 32], [1, 64]], OP.is_gt, 0.0,
        base=0, channel_multiplier=0), r=["ones"], w=["reset"])
    P.add("gpsimd", lambda e: e.affine_select(
        E["causal"][0:64, :], E["ones"][0:64, 0:64], [[1, 64]], OP.is_ge, 0.0,
        base=0, channel_multiplier=-1), r=["ones"], w=["causal"])
    P.add("gpsimd", lambda e: e.affine_select(
        E["causal"][64:128, :], E["ones"][0:64, 0:64], [[1, 64]], OP.is_ge, 0.0,
        base=0, channel_multiplier=-1), r=["ones"], w=["causal"])
    P.add("gpsimd", lambda e: e.affine_select(
        E["id_bf"][:, :], E["ones"][:, 0:128], [[-1, 128]], OP.is_equal, 0.0,
        base=0, channel_multiplier=1), r=["ones"], w=["id_bf"])
    P.add("gpsimd", lambda e: e.affine_select(
        E["id_f32"][:, :], E["ones"][:, 0:128], [[-1, 128]], OP.is_equal, 0.0,
        base=0, channel_multiplier=1), r=["ones"], w=["id_f32"])
    P.t_dma("gpsimd", E["NEGA"][:, :], E["d_nrow"].partition_broadcast(128),
            r=["nrow_dram"], w=["NEGA"])
    P.t_dma("gpsimd", E["DTB"][:, :],
            E["d_vecs"][1:2, :].partition_broadcast(128), w=["DTB"])
    P.t_dma("gpsimd", E["DP"][:, :],
            E["d_vecs"][2:3, :].partition_broadcast(128), w=["DP"])
    P.t_dma("gpsimd", E["FLG"][:, :], E["d_flag"].partition_broadcast(64),
            w=["FLG"])
    P.t_dma("gpsimd", E["OSCB"][:, :], E["d_osc"].partition_broadcast(128),
            w=["OSCB"])
    P.add("vector", lambda e: e.tensor_scalar_mul(
        E["dsc_sb"][:, :], E["dsc_sb"][:, :], E["OSCB"][:, 0:1]),
        r=["OSCB", "dsc_sb"], w=["dsc_sb"])
    P.t_dma("gpsimd", E["d_wb"][:, 8192:8384], E["d_w_in"][:, 8192:8384],
            w=[("wcv", "b")])
    P.run()


def _p1_stats(C, E):
    nc = C.nc
    with nc.sbuf_tensor("UT1", [128, 2 * DM], F32) as UT, \
         nc.sbuf_tensor("SQS1", [128, DM], F32) as SQS:
        P = Phase(C, "p1")
        for i in range(NT):
            buf = UT[:, (i % 2) * DM:(i % 2 + 1) * DM]
            P.t_dma("sync" if i % 2 == 0 else "scalar", buf,
                    E["d_u_tok"][i * 128:(i + 1) * 128, :],
                    w=[("ut", i % 2)])
            P.add("vector", lambda e, b=buf, i=i: e.reduce_sum(
                E["musum"][:, i:i + 1], b, axis=AX.X), r=[("ut", i % 2)],
                w=["musum"])
            P.add("scalar", lambda e, b=buf, i=i: e.activation(
                SQS[:, :], b, AF.Square, accum_out=E["sum2"][:, i:i + 1]),
                r=[("ut", i % 2)], w=["sum2"])
        P.run()

    P = Phase(C, "p1b")

    def v1(e):
        MU, RS, NM, T = E["mu"], E["rstd"], E["nmr"], E["stat_t"]
        e.tensor_scalar_mul(MU[:, :], E["musum"][:, :], 1.0 / DM)
        e.tensor_scalar_mul(T[:, :], E["sum2"][:, :], 1.0 / DM)
        e.tensor_mul(RS[:, :], MU[:, :], MU[:, :])
        e.tensor_sub(T[:, :], T[:, :], RS[:, :])
        return e.tensor_scalar_add(T[:, :], T[:, :], 1e-5)
    P.add("vector", v1, r=["musum", "sum2"], w=["mu", "vareps"])
    P.add("scalar", lambda e: e.activation(E["rstd"][:, :], E["stat_t"][:, :],
                                           AF.Sqrt), r=["vareps"], w=["sq"])

    def v2(e):
        RS, T, MU, NM = E["rstd"], E["stat_t"], E["mu"], E["nmr"]
        R0, S2 = E["musum"], E["sum2"]
        e.reciprocal(R0[:, :], RS[:, :])
        e.tensor_mul(S2[:, :], R0[:, :], R0[:, :])
        e.tensor_mul(S2[:, :], S2[:, :], T[:, :])
        e.tensor_scalar(S2[:, :], S2[:, :], -0.5, 1.5, OP.mult, OP.add)
        e.tensor_mul(RS[:, :], R0[:, :], S2[:, :])
        e.tensor_mul(NM[:, :], MU[:, :], RS[:, :])
        return e.tensor_scalar_mul(NM[:, :], NM[:, :], -1.0)
    P.add("vector", v2, r=["sq", "mu", "vareps"], w=["rstd", "musum", "sum2"])
    if "stats" in C.dbg:
        for i, nm in enumerate(["mu", "rstd", "nmr", "musum", "sum2",
                                "stat_t"]):
            P.t_dma("gpsimd", E["d_pstat"][:, i * NT:(i + 1) * NT],
                    E[nm][:, :], r=["rstd", "mu"], w=["pst"])
        P.add("sync", lambda e: None, r=["pst"])
    P.run()


def _p2_xnf(C, E):
    nc = C.nc
    groups = [(0, 4), (4, 4), (8, 4), (12, 4), (16, 1)]
    with nc.sbuf_tensor("UT2", [128, 2 * DM], F32) as UT, \
         nc.sbuf_tensor("XT2", [128, 2 * DM], BF16) as XT, \
         nc.sbuf_tensor("XSL2", [128, 2 * 16 * 512], BF16) as XSL, \
         nc.psum_tensor("ps2a", [128, 1024], BF16) as PSA, \
         nc.psum_tensor("ps2b", [128, 1024], BF16) as PSB:
        P = Phase(C, "p2")
        XSL4 = XSL.rearrange("p (r k t) -> p r k t", r=2, k=16)
        for gi, (t0, gn) in enumerate(groups):
            for j in range(gn):
                i = t0 + j
                n = gi * 4 + j
                pr = n % 2
                ub = UT[:, pr * DM:(pr + 1) * DM]
                xb = XT[:, pr * DM:(pr + 1) * DM]
                ps = PSA if pr == 0 else PSB
                P.t_dma("sync" if pr == 0 else "scalar", ub,
                        E["d_u_tok"][i * 128:(i + 1) * 128, :],
                        w=[("u2", pr)])
                P.add("scalar", lambda e, ub=ub, xb=xb, i=i: e.activation(
                    xb, ub, AF.Identity, bias=E["nmr"][:, i:i + 1],
                    scale=E["rstd"][:, i:i + 1]),
                    r=[("u2", pr), "rstd"], w=[("xn", pr)])
                for half in range(2):
                    def trh(e, xb=xb, ps=ps, half=half):
                        ins = None
                        for k in range(half * 8, half * 8 + 8):
                            ins = e.transpose(
                                ps[:, (k % 8) * 128:(k % 8) * 128 + 128],
                                xb[:, k * 128:(k + 1) * 128], E["id_bf"][:, :])
                        return ins
                    P.add("tensor", trh, r=[("xn", pr), "id_bf"],
                          w=[("tph", pr)])
                    dst = XSL4[:, gi % 2, half * 8:(half + 1) * 8,
                               j * 128:(j + 1) * 128]
                    P.add("vector", lambda e, dst=dst, ps=ps: e.tensor_copy(
                        dst, r3(ps[:, :], 8, 128)),
                        r=[("tph", pr)],
                        w=[("slab", gi % 2)])
            dst = E["d_xnf"].rearrange("k d t -> d k t"
                                       )[:, :, t0 * 128:t0 * 128 + gn * 128]
            P.t_dma("scalar", dst, XSL4[:, gi % 2, :, 0:gn * 128],
                    r=[("slab", gi % 2)], w=[("xnfd", gi), ("slab", gi % 2)])
            P2W = {0: [("z", 0), ("z", 1)], 1: [("z", 2), ("z", 3)],
                   2: [("z", 4), ("z", 5)], 3: [("z", 6), ("z", 7)],
                   4: [("x", 0), ("x", 1)]}
            for fam, ci in P2W.get(gi, []):
                off = 4096 if fam == "x" else 0
                sl = slice(off + ci * 512, off + (ci + 1) * 512)
                P.t_dma("gpsimd", E["d_wb"][:, sl], E["d_w_in"][:, sl],
                        r=[("xnfd", gi)], w=[("wcv", fam, ci)])
        P.add("sync", lambda e: None,
              r=[("xnfd", g2) for g2 in range(len(groups))], w=["xnf_ready"])
        P.run()


def _p3_bcdt(C, E):
    nc = C.nc
    with nc.sbuf_tensor("WBCDT", [128, 16 * 192], BF16) as W, \
         nc.sbuf_tensor("XST3", [128, 2 * 16 * 512], BF16) as XST, \
         nc.sbuf_tensor("p3T1", [128, 2 * 64], F32) as T1, \
         nc.sbuf_tensor("p3T2", [128, 2 * 64], F32) as T2, \
         nc.sbuf_tensor("p3T3", [128, 2 * 64], F32) as T3, \
         nc.sbuf_tensor("p3MSK", [128, 2 * 64], mybir.dt.int32) as MSK, \
         nc.psum_tensor("ps3a", [128, 192], F32) as PA, \
         nc.psum_tensor("ps3b", [128, 192], F32) as PB:
        P = Phase(C, "p3")
        W3 = r3(W[:, :], 16, 192)
        XST4 = XST.rearrange("p (r k t) -> p r k t", r=2, k=16)
        P.t_dma("sync", W3,
                E["d_wb"].rearrange("(k p) c -> p k c", p=128)[:, :, 8192:8384],
                r=[("wcv", "b")], w=["W3"])
        for g in range(4):
            src = E["d_xnf"].rearrange("k d t -> d k t"
                                       )[:, :, 128 + g * 512:128 + (g + 1) * 512]
            P.t_dma("sync", XST4[:, g % 2], src,
                    r=[("xst_used", g - 2), "xnf_ready"], w=[("xst", g)])
            if g >= 1:
                ci = g + 1
                sl = slice(4096 + ci * 512, 4096 + (ci + 1) * 512)
                P.t_dma("gpsimd", E["d_wb"][:, sl], E["d_w_in"][:, sl],
                        r=[("xst", g)], w=[("wcv", "x", ci)])
            for j in range(4):
                idx = g * 4 + j
                par = idx % 2
                ps = PA if par == 0 else PB

                def mm(e, g=g, j=j, ps=ps):
                    ins = None
                    for k in range(16):
                        ins = e.matmul(ps[:, :], XST4[:, g % 2, k,
                                                      j * 128:(j + 1) * 128],
                                       W3[:, k, :], start=(k == 0),
                                       stop=(k == 15))
                    return ins
                P.add("tensor", mm, r=[("xst", g), "W3", ("ps3r", par)],
                      w=[("ps3", par), ("xst_used", g) if j == 3 else ("xju", g, j)])

                for col, dst in ((0, "k_tok"), (64, "q_tok")):
                    P.add("vector", lambda e, ps=ps, col=col, par=par:
                          e.tensor_scalar_min(T1[:, par * 64:(par + 1) * 64],
                                              ps[:, col:col + 64], 0.0),
                          r=[("ps3", par)], w=[("T1", par)])
                    P.add("scalar", lambda e, par=par: e.activation(
                        T2[:, par * 64:(par + 1) * 64],
                        T1[:, par * 64:(par + 1) * 64], AF.Exp),
                        r=[("T1", par)], w=[("T2", par)])
                    P.add("vector", lambda e, ps=ps, col=col, par=par, dst=dst,
                          idx=idx: e.scalar_tensor_tensor(
                              E[dst][:, idx * 64:(idx + 1) * 64],
                              ps[:, col:col + 64], 0.0,
                              T2[:, par * 64:(par + 1) * 64], OP.max, OP.add),
                          r=[("ps3", par), ("T2", par)], w=[dst, ("T1", par)])

                def vdt(e, ps=ps, par=par):
                    e.tensor_add(T1[:, par * 64:(par + 1) * 64],
                                 ps[:, 128:192], E["DTB"][:, 0:64])
                    e.tensor_scalar_min(T3[:, par * 64:(par + 1) * 64],
                                        T1[:, par * 64:(par + 1) * 64], 20.0)
                    return e.tensor_single_scalar(
                        MSK[:, par * 64:(par + 1) * 64],
                        T1[:, par * 64:(par + 1) * 64], 20.0, OP.is_gt)
                P.add("vector", vdt, r=[("ps3", par), "DTB"],
                      w=[("T1", par), ("T3", par), ("MSK", par)])

                def sdt(e, par=par):
                    e.activation(T2[:, par * 64:(par + 1) * 64],
                                 T3[:, par * 64:(par + 1) * 64], AF.Exp)
                    return e.activation(T2[:, par * 64:(par + 1) * 64],
                                        T2[:, par * 64:(par + 1) * 64],
                                        AF.Ln, bias=1.0)
                P.add("scalar", sdt, r=[("T3", par)], w=[("T2", par)])
                P.add("vector", lambda e, par=par, idx=idx: e.select(
                    E["dtv"][:, idx * 64:(idx + 1) * 64],
                    MSK[:, par * 64:(par + 1) * 64],
                    T1[:, par * 64:(par + 1) * 64],
                    T2[:, par * 64:(par + 1) * 64], add_drain=True),
                    r=[("T2", par), ("T1", par), ("MSK", par)],
                    w=["dtv", ("ps3r", par)])
        P.run()


def _p4_csfam(C, E):
    nc = C.nc
    with nc.sbuf_tensor("p4XX", [64, 2048], F32) as XX, \
         nc.sbuf_tensor("p4YY", [64, 1024], F32) as YY, \
         nc.sbuf_tensor("p4TOT", [64, 32], F32) as TOT, \
         nc.sbuf_tensor("p4TOTT", [32, 64], F32) as TOTT, \
         nc.psum_tensor("ps4a", [128, 512], F32) as PSA, \
         nc.psum_tensor("ps4b", [128, 512], F32) as PSB:
        P = Phase(C, "p4")
        P.add("vector", lambda e: e.tensor_tensor(
            r3(E["dtv"][:, :], 16, 64), r3(E["dtv"][:, :], 16, 64),
            E["NEGA"][:, :].unsqueeze(1).broadcast_to([128, 16, 64]), OP.mult),
            r=["dtv", "NEGA"], w=["dtv"])
        for g in range(4):
            ps = PSA if g % 2 == 0 else PSB

            def trs(e, g=g, ps=ps):
                ins = None
                for i in range(4):
                    ins = e.transpose(ps[0:64, i * 128:(i + 1) * 128],
                                      E["dtv"][:, (g * 4 + i) * 64:(g * 4 + i + 1) * 64],
                                      E["id_f32"][:, :])
                return ins
            P.add("tensor", trs, r=["dtv", "id_f32", ("ps4r", g % 2)],
                  w=[("ps4", g % 2)])
            P.add("vector", lambda e, g=g, ps=ps: e.tensor_copy(
                XX[:, g * 512:(g + 1) * 512], ps[0:64, 0:512]),
                r=[("ps4", g % 2)], w=["LAH", ("ps4r", g % 2)])

        def scans(e):
            e.tensor_tensor_scan(E["CS"][:, :], E["reset"][:, :], XX[:, :],
                                 0.0, OP.mult, OP.add)
            e.tensor_tensor_scan(E["GCS"][:, :], E["ones"][0:64, 0:1024],
                                 XX[:, 0:1024], 0.0, OP.mult, OP.add)
            e.tensor_copy(TOT[:, :],
                          E["CS"].rearrange("h (c s) -> h c s", c=32)[:, :, 63])
            e.tensor_tensor(r3(XX[:, :], 32, 64), bfree(TOT[:, :], 64),
                            r3(E["CS"][:, :], 32, 64), OP.subtract)
            return e.tensor_tensor(
                YY[:, :].unsqueeze(1),
                bfree(E["GCS"][:, 1023:1024], 1024).transpose([0, 2, 1]),
                E["GCS"][:, :].unsqueeze(1), OP.subtract)
        P.add("vector", scans, r=["LAH", "reset", "ones"],
              w=["CS", "GCS", "TOT", "XXd", "YYd", "LAH"])

        def exps(e):
            e.activation(XX[:, :], XX[:, :], AF.Exp)
            e.activation(YY[:, :], YY[:, :], AF.Exp)
            return e.activation(TOT[:, :], TOT[:, :], AF.Exp)
        P.add("scalar", exps, r=["XXd", "YYd", "TOT"],
              w=["XXd", "YYd", "TOT"])

        jobs = [(E["CS"], "CS_T", 16, "CS"), (XX, "DEC_T", 16, "XXd"),
                (YY, "DECP_T", 8, "YYd")]
        gidx = 0
        for src, dstn, cnt, tok in jobs:
            for g in range((cnt + 3) // 4):
                take = min(4, cnt - g * 4)
                ps = PSA if gidx % 2 == 0 else PSB

                def trs2(e, src=src, g=g, take=take, ps=ps):
                    ins = None
                    for i in range(take):
                        ins = e.transpose(ps[:, i * 64:(i + 1) * 64],
                                          src[:, (g * 4 + i) * 128:(g * 4 + i + 1) * 128],
                                          E["id_f32"][0:64, 0:64])
                    return ins
                P.add("tensor", trs2, r=[tok, "id_f32", ("ps4r", gidx % 2)],
                      w=[("ps4", gidx % 2)])
                dst = E[dstn][:, (g * 4) * 64:(g * 4) * 64 + take * 64]
                P.add("vector", lambda e, dst=dst, ps=ps, take=take:
                      e.tensor_copy(dst, ps[:, 0:take * 64]),
                      r=[("ps4", gidx % 2)], w=[dstn, ("ps4r", gidx % 2)])
                gidx += 1
        P.add("tensor", lambda e: e.transpose(PSA[0:32, 0:64], TOT[:, :],
                                              E["id_f32"][0:64, 0:64]),
              r=["TOT", "id_f32", ("ps4r", 0)], w=[("ps4", 0)])
        P.add("vector", lambda e: e.tensor_copy(TOTT[:, :], PSA[0:32, 0:64]),
              r=[("ps4", 0)], w=["TOTT", ("ps4r", 0)])
        P.add("scalar", lambda e: e.activation(E["ECS_T"][:, :], E["CS_T"][:, :],
                                               AF.Exp), r=["CS_T"], w=["ECS_T"])
        for ci in (5, 6, 7):
            sl = slice(4096 + ci * 512, 4096 + (ci + 1) * 512)
            P.t_dma("gpsimd", E["d_wb"][:, sl], E["d_w_in"][:, sl],
                    r=["CS"], w=[("wcv", "x", ci)])
        P.t_dma("scalar", E["d_csd"].rearrange("c (h s) -> h c s", h=64),
                r3(E["CS"][:, :], 32, 64), r=["CS"], w=["csd_dram"])
        P.t_dma("scalar", E["d_trow"].rearrange("o (c h) -> (o c) h", c=32),
                TOTT[:, :], r=["TOTT"], w=["trow_dram"])
        P.t_dma("gpsimd", E["DBC64"][:, :],
                E["d_trow"].partition_broadcast(64), r=["trow_dram"],
                w=["DBC64"])
        P.add("sync", lambda e: None, r=["csd_dram"])
        P.run()


def _p5_z(C, E):
    nc = C.nc
    with nc.sbuf_tensor("WZ5", [128, 2 * 16 * 512], BF16) as WZ, \
         nc.sbuf_tensor("XST5", [128, 2 * 16 * 512], BF16) as XST, \
         nc.sbuf_tensor("ZR5", [128, 4 * 512], BF16) as ZR, \
         nc.psum_tensor("ps5a", [128, 512], F32) as PA, \
         nc.psum_tensor("ps5b", [128, 512], F32) as PB:
        P = Phase(C, "p5")
        WZ4 = WZ.rearrange("p (r k c) -> p r k c", r=2, k=16)
        XST4 = XST.rearrange("p (r k t) -> p r k t", r=2, k=16)
        for tg in range(2):
            src = E["d_xnf"].rearrange("k d t -> d k t"
                                       )[:, :, 1152 + tg * 512:1152 + (tg + 1) * 512]
            P.t_dma("sync", XST4[:, tg], src,
                    r=["xnf_ready"], w=[("xst5", tg)])
        for zb in range(8):
            P.t_dma("sync", WZ4[:, zb % 2],
                    E["d_wb"].rearrange("(k p) c -> p k c", p=128
                                        )[:, :, zb * 512:(zb + 1) * 512],
                    r=[("wzu", zb - 2), ("wcv", "z", zb)], w=[("wz", zb)])
            for tg in range(2):
                for j in range(4):
                    idx = zb * 8 + tg * 4 + j
                    par = idx % 2
                    ps = PA if par == 0 else PB

                    def mm(e, zb=zb, tg=tg, j=j, ps=ps):
                        ins = None
                        for k in range(16):
                            ins = e.matmul(ps[:, :],
                                           XST4[:, tg, k, j * 128:(j + 1) * 128],
                                           WZ4[:, zb % 2, k, :],
                                           start=(k == 0), stop=(k == 15))
                        return ins
                    P.add("tensor", mm,
                          r=[("xst5", tg), ("wz", zb), ("ps5r", par)],
                          w=[("ps5", par)] + ([("wzu", zb)] if (tg == 1 and j == 3) else []))
                    P.add("scalar", lambda e, ps=ps, idx=idx: e.activation(
                        ZR[:, (idx % 4) * 512:(idx % 4 + 1) * 512], ps[:, :],
                        AF.Silu),
                        r=[("ps5", par), ("zru", idx - 3)],
                        w=[("zr", idx), ("ps5r", par)])
                    ti = tg * 4 + j
                    P.t_dma("scalar", E["d_zs"][ti, :, zb * 512:(zb + 1) * 512],
                            ZR[:, (idx % 4) * 512:(idx % 4 + 1) * 512],
                            r=[("zr", idx)], w=[("zru", idx), ("zsd", zb, ti)])
        P.run()


CONV_LEVEL = 99
CONV_SKIP = ""
CONV_NMG = 8

XBLOCKS = [(0, 512, (1, 2, 3)), (512, 512, (4, 5, 6, 7)),
           (1024, 512, (8, 9, 10, 11)), (1536, 512, (12, 13, 14, 15)),
           (2048, 128, (16,))]


def _p6_conv_blk(C, E, bi):
    nc = C.nc
    a, w, tiles = XBLOCKS[bi]
    with nc.sbuf_tensor(f"XMV6_{bi}", [128, 16 * 512], BF16) as XMV, \
         nc.sbuf_tensor(f"WXG6_{bi}", [128, 2 * 16 * 512], BF16) as WXG, \
         nc.sbuf_tensor(f"XJ6_{bi}", [128, 2 * 4 * 516], F32) as XJ, \
         nc.psum_tensor(f"ps6xa_{bi}", [128, 512], F32) as PXA, \
         nc.psum_tensor(f"ps6xb_{bi}", [128, 512], F32) as PXB, \
         nc.psum_tensor(f"ps6ca_{bi}", [128, 512], F32) as PCA, \
         nc.psum_tensor(f"ps6cb_{bi}", [128, 512], F32) as PCB:
        P = Phase(C, f"cv{bi}")
        WXG4 = WXG.rearrange("p (r k c) -> p r k c", r=2, k=16)
        XJ4 = XJ.rearrange("p (r j c) -> p r j c", r=2, j=4)
        P.t_dma("sync", r3(XMV[:, 0:16 * w], 16, w),
                E["d_xnf"].rearrange("k d t -> d k t")[:, :, a:a + w],
                r=["xnf_ready"], w=["XMV"])
        CVW = {1: [("u", 0), ("u", 1), ("u", 2), ("u", 3)],
               2: [("u", 4), ("u", 5), ("u", 6), ("u", 7)],
               3: [("d", 3), ("d", 4), ("d", 5), ("d", 6)]}
        for j, (fam, ci) in enumerate(CVW.get(bi, [])):
            st_, dt_, rows = {"u": (E["d_w_upT"], E["d_wub"], 256),
                              "d": (E["d_w_dnT"], E["d_wdb"], 1024)}[fam]
            sl = slice(ci * rows, (ci + 1) * rows)
            P.t_dma("gpsimd", dt_[sl, :], st_[sl, :],
                    r=[("v", tiles[j])], w=[("wcv", fam, ci)])
        for mg in range(CONV_NMG):
            P.t_dma("sync", WXG4[:, mg % 2],
                    E["d_wb"].rearrange("(k p) c -> p k c", p=128
                                        )[:, :, 4096 + mg * 512:4096 + (mg + 1) * 512],
                    r=[("wcv", "x", mg)], w=[("wx", mg % 2)])
            for mm in range(4):
                m = mg * 4 + mm
                par = m % 2
                psx = PXA if par == 0 else PXB
                psc = PCA if par == 0 else PCB

                def mmx(e, mg=mg, mm=mm, psx=psx):
                    ins = None
                    for k in range(16):
                        ins = e.matmul(psx[:, 0:w], WXG4[:, mg % 2, k,
                                                        mm * 128:(mm + 1) * 128],
                                       r3(XMV[:, 0:16 * w], 16, w)[:, k, :],
                                       start=(k == 0), stop=(k == 15))
                    return ins
                P.add("tensor", mmx, r=[("wx", mg % 2), "XMV"],
                      w=[("psx", par)])

                if bi > 0:
                    P.add("vector", lambda e, m=m, par=par: e.tensor_copy(
                        XJ4[:, par, :, 0:3],
                        E["xtail"][:, m * 12:(m + 1) * 12].rearrange(
                            "p (j c) -> p j c", j=4)),
                        r=[("xtail", m)], w=[("xj0", par)])

                if CONV_LEVEL < 2:
                    continue

                def cpj01(e, m=m, par=par, psx=psx):
                    if CONV_SKIP == "act1":
                        return e.activation(XJ4[:, par, 0, 3:3 + w], psx[:, 0:w],
                                            AF.Identity)
                    e.activation(XJ4[:, par, 0, 3:3 + w], psx[:, 0:w], AF.Identity,
                                 bias=E["cb_sb"][:, m:m + 1],
                                 scale=E["cw_sb"][:, 4 * m:4 * m + 1])
                    return e.activation(XJ4[:, par, 1, 3:3 + w], psx[:, 0:w],
                                        AF.Identity,
                                        scale=E["cw_sb"][:, 4 * m + 1:4 * m + 2])
                P.add("scalar", cpj01, r=[("psx", par), "cw_sb", "cb_sb"],
                      w=[("xj1", par)])

                def cpj23(e, m=m, par=par, psx=psx):
                    if CONV_SKIP == "dve_scr":
                        e.tensor_tensor(
                            E["sqs_cv"][:, 0:w], psx[:, 0:w],
                            E["cw_sb"][:, 4 * m + 2:4 * m + 3
                                       ].broadcast_to([128, w]), OP.mult)
                        return e.tensor_tensor(
                            E["sqs_cv"][:, 0:w], psx[:, 0:w],
                            E["cw_sb"][:, 4 * m + 3:4 * m + 4
                                       ].broadcast_to([128, w]), OP.mult)
                    if CONV_SKIP == "dve_sbsrc":
                        e.tensor_tensor(
                            XJ4[:, par, 2, 3:3 + w], E["sqs_cv"][:, 0:w],
                            E["cw_sb"][:, 4 * m + 2:4 * m + 3
                                       ].broadcast_to([128, w]), OP.mult)
                        return e.tensor_tensor(
                            XJ4[:, par, 3, 3:3 + w], E["sqs_cv"][:, 0:w],
                            E["cw_sb"][:, 4 * m + 3:4 * m + 4
                                       ].broadcast_to([128, w]), OP.mult)
                    e.tensor_tensor(
                        XJ4[:, par, 2, 3:3 + w], psx[:, 0:w],
                        E["cw_sb"][:, 4 * m + 2:4 * m + 3
                                   ].broadcast_to([128, w]), OP.mult)
                    return e.tensor_tensor(
                        XJ4[:, par, 3, 3:3 + w], psx[:, 0:w],
                        E["cw_sb"][:, 4 * m + 3:4 * m + 4
                                   ].broadcast_to([128, w]), OP.mult)
                if CONV_SKIP != "dve":
                    P.add("vector", cpj23, r=[("psx", par), ("xj1", par),
                                              "cw_sb"],
                          w=[("xj2", par)])
                else:
                    P.add("vector", lambda e, par=par: e.engine_nop(),
                          w=[("xj2", par)])

                if CONV_LEVEL >= 3:
                    P.add("vector", lambda e, m=m, par=par: e.tensor_copy(
                        E["xtail"][:, m * 12:(m + 1) * 12].rearrange(
                            "p (j c) -> p j c", j=4), XJ4[:, par, :, w:w + 3]),
                        r=[("xj1", par), ("xj2", par)], w=[("xtail", m)])

                def trc(e, m=m, par=par, psc=psc, tiles=tiles, a=a):
                    ins = None
                    first = True
                    for wl, ti in enumerate(tiles):
                        for j in range(4):
                            col = 128 * ti + j - a
                            ins = e.matmul(psc[:, wl * 128:(wl + 1) * 128],
                                           XJ4[:, par, j, col:col + 128],
                                           E["id_f32"][:, :],
                                           start=first, stop=False,
                                           is_transpose=True,
                                           skip_group_check=True)
                            first = False
                    return ins
                rd = [("xj1", par), ("xj2", par), "id_f32"]
                if bi > 0:
                    rd.append(("xj0", par))
                if CONV_LEVEL >= 4:
                    P.add("tensor", trc, r=rd, w=[("psc", par)])

                def sil(e, m=m, par=par, psc=psc, tiles=tiles):
                    ins = None
                    for wl, ti in enumerate(tiles):
                        slot = ti % 4
                        ins = e.activation(
                            E["vt"][:, slot * DI + m * 128:slot * DI + (m + 1) * 128],
                            psc[:, wl * 128:(wl + 1) * 128], AF.Silu)
                    return ins
                if CONV_LEVEL >= 5:
                    P.add("scalar", sil, r=[("psc", par)],
                          w=[("v", ti) for ti in tiles])
        P.run()


def _p6_prefix(C, E, pairs, first, last):
    nc = C.nc
    with nc.psum_tensor(f"ps6h_{pairs[0]}", [64, DI], F32) as PH, \
         nc.sbuf_tensor(f"vscl_p{pairs[0]}", [128, DI], BF16) as VSCL:
        E["vscl"] = VSCL
        P = Phase(C, f"pre{pairs[0]}")
        if first:
            P.add("vector", lambda e: e.memset(E["h_st"][:, :], 0.0),
                  w=["h_st"])
        for p in pairs:
            ti = p + 1
            P.add("vector", lambda e, p=p, ti=ti: e.tensor_tensor(
                r3(E["vscl"][:, :], 64, 64),
                r3(E["vt"][:, (ti % 4) * DI:(ti % 4 + 1) * DI], 64, 64),
                bfree(E["DECP_T"][:, p * 64:(p + 1) * 64], 64), OP.mult),
                r=[("v", ti), "DECP_T"], w=["vscl"])
            for b in range(8):
                P.add("tensor", lambda e, p=p, b=b: e.matmul(
                    PH[:, b * 512:(b + 1) * 512],
                    E["k_tok"][:, p * 64:(p + 1) * 64],
                    E["vscl"][:, b * 512:(b + 1) * 512],
                    start=(p == pairs[0]), stop=(p == pairs[-1]),
                    skip_group_check=True),
                    r=["vscl", "k_tok"], w=[("ph", b)])
        for b in range(8):
            P.add("vector", lambda e, b=b: e.tensor_add(
                E["h_st"][:, b * 512:(b + 1) * 512],
                E["h_st"][:, b * 512:(b + 1) * 512], PH[:, b * 512:(b + 1) * 512]),
                r=[("ph", b), "h_st"], w=["h_st"])
        if last:
            P.add("vector", lambda e: e.tensor_scalar_mul(
                E["h_st"][:, :], E["h_st"][:, :], E["FLG"][:, 0:1]),
                r=["h_st", "FLG"], w=["h_st"])
        if last and "probe" in C.dbg:
            P.t_dma("gpsimd", E["d_ph0"], E["h_st"][:, :], r=["h_st"],
                    w=["prb"])
            P.t_dma("gpsimd", E["d_pdecp"], E["DECP_T"][:, :], r=["DECP_T"],
                    w=["prb"])

            def cast2(e):
                e.tensor_copy(E["vscl"][:, 0:1024].bitcast(F32),
                              E["k_tok"][:, :].bitcast(mybir.dt.uint16
                                                       ).bitcast(BF16)) \
                    if False else None
                return None
            P.t_dma("gpsimd", E["d_pkq"].bitcast(BF16)[:, 0:16 * 64],
                    E["k_tok"][:, :], r=["k_tok"], w=["prb"])
            P.t_dma("gpsimd", E["d_pkq"].bitcast(BF16)[:, 16 * 64:2 * 16 * 64],
                    E["q_tok"][:, :], r=["q_tok"], w=["prb"])
            P.t_dma("gpsimd", E["d_pvt"].bitcast(BF16)[:, 0:4 * DI],
                    E["vt"][:, :], r=[("v", 8)], w=["prb"])
            P.add("sync", lambda e: None, r=["prb"])
        P.run()


def _p6_ssd(C, E, pairs):
    nc = C.nc
    import contextlib
    st = contextlib.ExitStack()
    sfx = f"_s{pairs[0]}"
    for nm, shape, dt in [
        ("h_bf", [64, DI], BF16), ("h_bf2", [64, DI], BF16),
        ("kdiag", [128, 128], BF16), ("qstack", [128, 64], BF16),
        ("qflat_o", [64, 64], BF16), ("att", [128, NH * 128], BF16),
        ("Dmask", [128, NH * 64], BF16), ("CSB", [128, NH * 64], F32),
        ("G_sb", [128, 64], BF16), ("vscl", [128, DI], BF16),
        ("y_sl", [128, DI], BF16), ("zs_sl", [128, DI], BF16),
        ("yg_sl", [128, DI], BF16), ("YEV", [128, 512], F32),
        ("YEV2", [128, 512], F32), ("DVS", [128, 512], BF16),
        ("sqc", [128, 8], F32), ("rstd_y", [128, 1], F32),
        ("nwt", [128, 1], F32), ("sqs", [128, 512], F32),
    ]:
        E[nm] = st.enter_context(nc.sbuf_tensor(nm + sfx, shape, dt))
    with nc.psum_tensor(f"ps6sa{sfx}", [64, 512], F32) as PSa, \
         nc.psum_tensor(f"ps6sb{sfx}", [64, 512], F32) as PSb, \
         nc.psum_tensor(f"ps6Aa{sfx}", [128, 512], F32) as PAa, \
         nc.psum_tensor(f"ps6Ab{sfx}", [128, 512], F32) as PAb, \
         nc.psum_tensor(f"ps6Ba{sfx}", [128, 512], F32) as PBa, \
         nc.psum_tensor(f"ps6Bb{sfx}", [128, 512], F32) as PBb, \
         nc.psum_tensor(f"ps6T{sfx}", [128, 1024], BF16) as PT, \
         nc.psum_tensor(f"ps6G{sfx}", [128, 64], F32) as PG:
        P = Phase(C, f"ssd{pairs[0]}")
        SSW = {8: [("d", 0), ("d", 1), ("d", 2)],
               11: [("d", 7), ("o", 0), ("o", 1), ("o", 2)],
               15: [("o", 3)]}
        for j, (fam, ci) in enumerate(SSW.get(pairs[0], [])):
            st_, dt_ = {"d": (E["d_w_dnT"], E["d_wdb"]),
                        "o": (E["d_w_out"], E["d_wob"])}[fam]
            sl = slice(ci * 1024, (ci + 1) * 1024)
            P.t_dma("gpsimd", dt_[sl, :], st_[sl, :],
                    r=[("ygd", pairs[min(j, len(pairs) - 1)] - 8)],
                    w=[("wcv", fam, ci)])
        P.add("vector", lambda e: e.memset(E["att"][:, :], 0.0), w=["att"])
        P.add("vector", lambda e: e.memset(E["kdiag"][:, :], 0.0), w=["kdiag"])
        for p in pairs:
            ti = p + 1
            slot = ti % 4
            vsl = E["vt"][:, slot * DI:(slot + 1) * DI]

            P.t_dma("gpsimd", E["CSB"][0:64, :],
                    E["d_csd"][2 * p:2 * p + 1, :].partition_broadcast(64),
                    w=["CSBa"])
            P.t_dma("gpsimd", E["CSB"][64:128, :],
                    E["d_csd"][2 * p + 1:2 * p + 2, :].partition_broadcast(64),
                    w=["CSBb"])
            P.t_dma("sync", E["zs_sl"][:, 0:DI], E["d_zs"][p - 8],
                    r=[("zsd", zb2, p - 8) for zb2 in range(8)], w=["zs_sl"])

            def xm(e, p=p):
                e.scalar_tensor_tensor(
                    r3(E["CSB"][:, :], 64, 64), r3(E["CSB"][:, :], 64, 64), -1.0,
                    bfree(E["CS_T"][:, p * 64:(p + 1) * 64], 64),
                    OP.mult, OP.add)
                return e.tensor_scalar_max(E["CSB"][:, :], E["CSB"][:, :], 0.0)
            P.add("vector", xm, r=["CSBa", "CSBb", "CS_T"],
                  w=["CSBa", "CSBb", "Xm"])
            P.add("scalar", lambda e: e.activation(
                E["Dmask"][:, :], E["CSB"][:, :], AF.Exp, scale=-1.0),
                r=["Xm"], w=["Dmask"])

            def qktr(e, p=p):
                e.transpose(PT[0:64, 0:128], E["q_tok"][:, p * 64:(p + 1) * 64],
                            E["id_bf"][:, :])
                e.transpose(PT[64:128, 128:192],
                            E["q_tok"][64:128, p * 64:(p + 1) * 64],
                            E["id_bf"][64:128, 64:128], tile_position=(64, 64))
                e.transpose(PT[0:64, 192:256],
                            E["k_tok"][0:64, p * 64:(p + 1) * 64],
                            E["id_bf"][0:64, 0:64])
                return e.transpose(PT[64:128, 256:320],
                                   E["k_tok"][64:128, p * 64:(p + 1) * 64],
                                   E["id_bf"][64:128, 64:128],
                                   tile_position=(64, 64))
            P.add("tensor", qktr, r=["q_tok", "k_tok", "id_bf"], w=["psT"])

            def qkev(e):
                e.tensor_copy(E["qstack"][0:64, :], PT[0:64, 0:64])
                e.tensor_copy(E["qflat_o"][:, :], PT[0:64, 64:128])
                e.tensor_copy(E["qstack"][64:128, :], PT[64:128, 128:192])
                e.tensor_copy(E["kdiag"][0:64, 0:64], PT[0:64, 192:256])
                return e.tensor_copy(E["kdiag"][64:128, 64:128],
                                     PT[64:128, 256:320])
            P.add("vector", qkev, r=["psT"], w=["qstack", "qflat_o", "kdiag"])

            P.add("tensor", lambda e: e.matmul(PG[:, :], E["kdiag"][:, :],
                                               E["qstack"][:, :], start=True,
                                               stop=True),
                  r=["kdiag", "qstack"], w=["psG"])
            P.add("vector", lambda e: e.tensor_tensor(
                E["G_sb"][:, :], PG[:, :], E["causal"][:, :], OP.mult),
                r=["psG", "causal"], w=["G_sb"])

            def attb(e):
                A3 = E["att"].rearrange("p (h s) -> p h s", h=64)
                e.tensor_tensor(
                    A3[0:64, :, 0:64],
                    E["G_sb"][0:64, :].unsqueeze(1).broadcast_to([64, 64, 64]),
                    r3(E["Dmask"][0:64, :], 64, 64), OP.mult)
                return e.tensor_tensor(
                    A3[64:128, :, 64:128],
                    E["G_sb"][64:128, :].unsqueeze(1).broadcast_to([64, 64, 64]),
                    r3(E["Dmask"][64:128, :], 64, 64), OP.mult)
            P.add("vector", attb, r=["G_sb", "Dmask"], w=["att"])

            P.add("vector", lambda e, vsl=vsl, p=p: e.tensor_tensor(
                r3(E["vscl"][:, :], 64, 64), r3(vsl, 64, 64),
                bfree(E["DEC_T"][:, p * 64:(p + 1) * 64], 64), OP.mult),
                r=[("v", ti), "DEC_T"], w=["vscl"])

            P.add("vector", lambda e: e.tensor_copy(E["h_bf"][:, :],
                                                    E["h_st"][:, :]),
                  r=["h_st"], w=["h_bf"])
            P.add("vector", lambda e, p=p: e.tensor_tensor(
                r3(E["h_st"][:, :], 64, 64), r3(E["h_st"][:, :], 64, 64),
                bfree(E["DBC64"][0:64, (2 * p) * 64:(2 * p + 1) * 64], 64),
                OP.mult), r=["h_st", "DBC64"], w=["h_st"])
            for b in range(8):
                P.add("tensor", lambda e, p=p, b=b: e.matmul(
                    (PSa if b % 2 == 0 else PSb)[:, :],
                    E["k_tok"][0:64, p * 64:(p + 1) * 64],
                    E["vscl"][0:64, b * 512:(b + 1) * 512],
                    start=True, stop=True),
                    r=["vscl", "k_tok", ("pssr", b % 2)], w=[("pss", b % 2)])
                P.add("vector", lambda e, b=b: e.tensor_add(
                    E["h_st"][:, b * 512:(b + 1) * 512],
                    E["h_st"][:, b * 512:(b + 1) * 512],
                    (PSa if b % 2 == 0 else PSb)[:, :]),
                    r=[("pss", b % 2), "h_st"], w=["h_st", ("pssr", b % 2)])
            P.add("vector", lambda e: e.tensor_copy(E["h_bf2"][:, :],
                                                    E["h_st"][:, :]),
                  r=["h_st"], w=["h_bf2"])
            P.add("vector", lambda e, p=p: e.tensor_tensor(
                r3(E["h_st"][:, :], 64, 64), r3(E["h_st"][:, :], 64, 64),
                bfree(E["DBC64"][0:64, (2 * p + 1) * 64:(2 * p + 2) * 64], 64),
                OP.mult), r=["h_st", "DBC64"], w=["h_st"])
            for b in range(8):
                P.add("tensor", lambda e, p=p, b=b: e.matmul(
                    (PSa if b % 2 == 0 else PSb)[:, :],
                    E["k_tok"][64:128, p * 64:(p + 1) * 64],
                    E["vscl"][64:128, b * 512:(b + 1) * 512],
                    start=True, stop=True, tile_position=(64, 0)),
                    r=["vscl", "k_tok", ("pssr", b % 2)], w=[("pss", b % 2)])
                P.add("vector", lambda e, b=b: e.tensor_add(
                    E["h_st"][:, b * 512:(b + 1) * 512],
                    E["h_st"][:, b * 512:(b + 1) * 512],
                    (PSa if b % 2 == 0 else PSb)[:, :]),
                    r=[("pss", b % 2), "h_st"], w=["h_st", ("pssr", b % 2)])
            for b in range(8):
                PA = PAa if b % 2 == 0 else PAb
                PB = PBa if b % 2 == 0 else PBb
                P.add("tensor", lambda e, b=b, PA=PA: e.matmul(
                    PA[0:64, :], E["qstack"][0:64, :],
                    E["h_bf"][:, b * 512:(b + 1) * 512],
                    start=True, stop=False, skip_group_check=True),
                    r=["h_bf", "qstack", ("psar", b % 2)], w=[("psa", b % 2)])
                P.add("tensor", lambda e, b=b, PA=PA: e.matmul(
                    PA[64:128, :], E["qflat_o"][:, :],
                    E["h_bf2"][:, b * 512:(b + 1) * 512],
                    start=True, stop=True, tile_position=(0, 64),
                    skip_group_check=True),
                    r=["h_bf2", "qflat_o"], w=[("psa", b % 2)])

                def intra(e, b=b, vsl=vsl, PB=PB):
                    ins = None
                    for hh in range(8):
                        h = b * 8 + hh
                        ins = e.matmul(
                            PB[:, hh * 64:(hh + 1) * 64],
                            E["att"][:, h * 128:(h + 1) * 128],
                            vsl[:, h * 64:(h + 1) * 64],
                            start=(hh == 0), stop=(hh == 7),
                            skip_group_check=True)
                    return ins
                P.add("tensor", intra, r=["att", ("v", ti), ("psbr", b % 2)],
                      w=[("psb", b % 2)])

                def yev(e, p=p, b=b, vsl=vsl, PA=PA, PB=PB):
                    e.tensor_tensor(
                        r3(E["YEV"][:, :], 8, 64), r3(PA[:, :], 8, 64),
                        bfree(E["ECS_T"][:, p * 64 + b * 8:p * 64 + b * 8 + 8], 64),
                        OP.mult)
                    e.tensor_add(E["YEV2"][:, :], E["YEV"][:, :], PB[:, :])
                    e.tensor_tensor(r3(E["DVS"][:, :], 8, 64),
                                    r3(vsl[:, b * 512:(b + 1) * 512], 8, 64),
                                    bfree(E["DP"][:, b * 8:b * 8 + 8], 64),
                                    OP.mult)
                    return e.tensor_add(E["y_sl"][:, b * 512:(b + 1) * 512],
                                        E["YEV2"][:, :], E["DVS"][:, :])
                P.add("vector", yev,
                      r=[("psa", b % 2), ("psb", b % 2), ("v", ti), "ECS_T",
                         "DP"],
                      w=[("ysl", b), ("psar", b % 2), ("psbr", b % 2)])
                P.add("scalar", lambda e, b=b: e.activation(
                    E["sqs"][:, :], E["y_sl"][:, b * 512:(b + 1) * 512],
                    AF.Square, accum_out=E["sqc"][:, b:b + 1]),
                    r=[("ysl", b)], w=[("sqc", b)])

            def rsy1(e):
                e.reduce_sum(E["nwt"][:, :], E["sqc"][:, :], axis=AX.X)
                return e.tensor_scalar(E["nwt"][:, :], E["nwt"][:, :],
                                       1.0 / DI, 1e-6, OP.mult, OP.add)
            P.add("vector", rsy1, r=[("sqc", b) for b in range(8)], w=["nwt"])
            P.add("scalar", lambda e: e.activation(E["rstd_y"][:, :],
                                                   E["nwt"][:, :], AF.Sqrt),
                  r=["nwt"], w=["rsq"])

            def rsy2(e):
                RS, T = E["rstd_y"], E["nwt"]
                R0 = E["sqc"][:, 1:2]
                e.reciprocal(R0, RS[:, :])
                e.tensor_mul(E["sqc"][:, 0:1], R0, R0)
                e.tensor_mul(E["sqc"][:, 0:1], E["sqc"][:, 0:1], T[:, :])
                e.tensor_scalar(E["sqc"][:, 0:1], E["sqc"][:, 0:1], -0.5, 1.5,
                                OP.mult, OP.add)
                return e.tensor_mul(RS[:, :], R0, E["sqc"][:, 0:1])
            P.add("vector", rsy2, r=["rsq"], w=["rstd_y", ("sqc", 0),
                                               ("sqc", 1)])

            if p == 8 and "probe2" in C.dbg:
                for nm, src_ap, rt in [
                        ("py", E["y_sl"][:, :].bitcast(mybir.dt.uint16), None),
                        ("patt", E["att"][:, :].bitcast(mybir.dt.uint16), "att"),
                        ("pG", E["G_sb"][:, :].bitcast(mybir.dt.uint16), "G_sb"),
                        ("pD", E["Dmask"][:, :].bitcast(mybir.dt.uint16), "Dmask"),
                        ("phb", E["h_bf"][:, :].bitcast(mybir.dt.uint16), "h_bf"),
                        ("pqs", E["qstack"][:, :].bitcast(mybir.dt.uint16), "qstack"),
                        ("pkd", E["kdiag"][:, :].bitcast(mybir.dt.uint16), "kdiag"),
                        ("pvs", E["vscl"][:, :].bitcast(mybir.dt.uint16), "vscl")]:
                    rr = [rt] if rt else [("ysl", b) for b in range(8)]
                    P.t_dma("gpsimd",
                            E["d_" + nm].bitcast(mybir.dt.uint16
                                                 )[:, 0:src_ap.shape[1]],
                            src_ap, r=rr, w=["prb2"])
                P.add("sync", lambda e: None, r=["prb2"])
            P.add("vector", lambda e: e.scalar_tensor_tensor(
                E["yg_sl"][:, 0:DI], E["y_sl"][:, :], E["rstd_y"][:, 0:1],
                E["zs_sl"][:, 0:DI], OP.mult, OP.mult),
                r=["rstd_y", "zs_sl"] + [("ysl", b) for b in range(8)],
                w=["yg_sl"])
            P.t_dma("gpsimd", E["d_yg"][p - 8], E["yg_sl"][:, 0:DI],
                    r=["yg_sl"], w=[("ygd", p - 8)])
        P.add("sync", lambda e: None, r=[("ygd", p2 - 8) for p2 in pairs])
        P.run()
    st.close()


def _p6_main(C, E):
    _p6_conv_blk(C, E, 0)
    _p6_prefix(C, E, (0, 1, 2), True, False)
    _p6_conv_blk(C, E, 1)
    _p6_prefix(C, E, (3, 4, 5, 6), False, False)
    _p6_conv_blk(C, E, 2)
    _p6_prefix(C, E, (7,), False, True)
    _p6_ssd(C, E, (8, 9, 10))
    _p6_conv_blk(C, E, 3)
    _p6_ssd(C, E, (11, 12, 13, 14))
    _p6_conv_blk(C, E, 4)
    _p6_ssd(C, E, (15,))


def _p7_ffn_out(C, E):
    nc = C.nc
    with nc.sbuf_tensor("FFN_O", [128, 16 * 512], F32) as FFN_O:
        for tg in range(2):
            with nc.sbuf_tensor(f"HID_{tg}", [128, 64 * 512], BF16) as HID:
                E["HID"] = HID
                _p7_up(C, E, tg)
                _p7_dn(C, E, tg, FFN_O)
            _p7_out(C, E, tg, FFN_O)


def _p7_up(C, E, tg):
    nc = C.nc
    with nc.sbuf_tensor(f"XMV7_{tg}", [128, 16 * 512], BF16) as XMV, \
         nc.sbuf_tensor(f"SWUP_{tg}", [128, 2 * 16 * 512], BF16) as SWUP, \
         nc.psum_tensor(f"ps7ua_{tg}", [128, 512], F32) as PUA, \
         nc.psum_tensor(f"ps7ub_{tg}", [128, 512], F32) as PUB:
        P = Phase(C, f"up{tg}")
        SW4 = SWUP.rearrange("p (r k c) -> p r k c", r=2, k=16)
        HID3 = r3(E["HID"][:, :], 64, 512)
        P.t_dma("sync", r3(XMV[:, :], 16, 512),
                E["d_xnf"].rearrange("k d t -> d k t"
                                     )[:, :, 1152 + tg * 512:1152 + (tg + 1) * 512],
                r=["xnf_ready"], w=["XMV"])
        for mg in range(16):
            P.t_dma("sync", SW4[:, mg % 2],
                    E["d_wub"].rearrange("(k p) c -> p k c", p=128
                                         )[:, :, mg * 512:(mg + 1) * 512],
                    r=[("swu", mg % 2)] + [("wcv", "u", i4) for i4 in range(8)],
                    w=[("swraw", mg % 2)])
            P.add("scalar", lambda e, mg=mg: e.activation(
                SWUP.rearrange("p (r c) -> p r c", r=2)[:, mg % 2],
                SWUP.rearrange("p (r c) -> p r c", r=2)[:, mg % 2], AF.Sign),
                r=[("swraw", mg % 2)], w=[("sw", mg % 2)])
            for mm in range(4):
                m = mg * 4 + mm
                par = m % 2
                ps = PUA if par == 0 else PUB

                def mmu(e, mg=mg, mm=mm, ps=ps):
                    ins = None
                    for k in range(16):
                        ins = e.matmul(ps[:, :],
                                       SW4[:, mg % 2, k, mm * 128:(mm + 1) * 128],
                                       r3(XMV[:, :], 16, 512)[:, k, :],
                                       start=(k == 0), stop=(k == 15))
                    return ins
                P.add("tensor", mmu,
                      r=[("sw", mg % 2), "XMV", ("pur", par)],
                      w=[("pu", par)] + ([("swu", mg % 2)] if mm == 3 else []))
                P.add("vector", lambda e, m=m, ps=ps: e.tensor_scalar(
                    HID3[:, m, :], ps[:, :], E["usc_sb"][:, m:m + 1], 0.0,
                    OP.mult, OP.max),
                    r=[("pu", par), "usc_sb"], w=[("hid", m), ("pur", par)])
        P.run()


def _p7_dn(C, E, tg, FFN_O):
    nc = C.nc
    with nc.sbuf_tensor(f"SWDN_{tg}", [128, 2 * 32 * 512], BF16) as SWDN, \
         nc.psum_tensor(f"ps7d0_{tg}", [128, 512], F32) as PD0, \
         nc.psum_tensor(f"ps7d1_{tg}", [128, 512], F32) as PD1, \
         nc.psum_tensor(f"ps7d2_{tg}", [128, 512], F32) as PD2, \
         nc.psum_tensor(f"ps7d3_{tg}", [128, 512], F32) as PD3:
        PDS = [PD0, PD1, PD2, PD3]
        P = Phase(C, f"dn{tg}")
        SW4 = SWDN.rearrange("p (r k c) -> p r k c", r=2, k=32)
        HID3 = r3(E["HID"][:, :], 64, 512)
        for dmg in range(4):
            for kh in range(2):
                half = dmg * 2 + kh
                P.t_dma("sync", SW4[:, half % 2],
                        E["d_wdb"].rearrange("(k p) c -> p k c", p=128
                                             )[:, kh * 32:(kh + 1) * 32,
                                               dmg * 512:(dmg + 1) * 512],
                        r=[("swdu", half % 2)] +
                          [("wcv", "d", kh * 4 + j4) for j4 in range(4)],
                        w=[("swdraw", half % 2)])
                P.add("scalar", lambda e, half=half: e.activation(
                    SWDN.rearrange("p (r c) -> p r c", r=2)[:, half % 2],
                    SWDN.rearrange("p (r c) -> p r c", r=2)[:, half % 2],
                    AF.Sign), r=[("swdraw", half % 2)], w=[("swd", half % 2)])
                for dmm in range(4):
                    def mmd(e, kh=kh, dmm=dmm, dmg=dmg, half=half):
                        ins = None
                        for kk in range(32):
                            k = kh * 32 + kk
                            ins = e.matmul(
                                PDS[dmm][:, :],
                                SW4[:, half % 2, kk, dmm * 128:(dmm + 1) * 128],
                                HID3[:, k, :],
                                start=(kh == 0 and kk == 0),
                                stop=(kh == 1 and kk == 31))
                        return ins
                    P.add("tensor", mmd,
                          r=[("swd", half % 2), ("pdr", dmm)] +
                            [("hid", k2) for k2 in range(kh * 32, kh * 32 + 32)],
                          w=[("pd", dmm)] + ([("swdu", half % 2)]
                                             if dmm == 3 else []))
            for dmm in range(4):
                dmc = dmg * 4 + dmm
                P.add("vector", lambda e, dmm=dmm, dmc=dmc: e.tensor_scalar(
                    FFN_O[:, dmc * 512:(dmc + 1) * 512], PDS[dmm][:, :],
                    E["dsc_sb"][:, dmc:dmc + 1], None, OP.mult),
                    r=[("pd", dmm), "dsc_sb"], w=[("ffo", dmc), ("pdr", dmm)])
        P.run()


def _p7_out(C, E, tg, FFN_O):
    nc = C.nc
    with nc.sbuf_tensor(f"YGL_{tg}", [128, 2 * DI], BF16) as YGL, \
         nc.sbuf_tensor(f"YGT_{tg}", [128, 32 * 512], BF16) as YGT, \
         nc.sbuf_tensor(f"WOS_{tg}", [128, 32 * 512], BF16) as WOS, \
         nc.sbuf_tensor(f"UTS_{tg}", [128, 4 * 2048], F32) as UTS, \
         nc.sbuf_tensor(f"OSB_{tg}", [128, 4 * 2048], F32) as OSB, \
         nc.sbuf_tensor(f"SSL_{tg}", [128, 512], F32) as SSL, \
         nc.psum_tensor(f"ps7oa_{tg}", [128, 512], F32) as POA, \
         nc.psum_tensor(f"ps7ob_{tg}", [128, 512], F32) as POB, \
         nc.psum_tensor(f"ps7ty_{tg}", [128, 1024], BF16) as PTY, \
         nc.psum_tensor(f"ps7ka_{tg}", [128, 512], F32) as PKA, \
         nc.psum_tensor(f"ps7kb_{tg}", [128, 512], F32) as PKB:
        P = Phase(C, f"out{tg}")
        YGT3 = r3(YGT[:, :], 32, 512)
        WOS3 = r3(WOS[:, :], 32, 512)
        for ts in range(4):
            row = 1152 + tg * 512 + ts * 128
            P.t_dma("sync", UTS[:, ts * 2048:(ts + 1) * 2048],
                    E["d_u_tok"][row:row + 128, :], w=[("uts", ts)])
        for i in range(4):
            P.t_dma("sync", YGL[:, (i % 2) * DI:(i % 2 + 1) * DI],
                    E["d_yg"][tg * 4 + i], r=[("ygl", i % 2), ("ygd", tg * 4 + i)],
                    w=[("ygld", i % 2)])
            for half in range(4):
                def ytr(e, i=i, half=half):
                    ins = None
                    for kk in range(8):
                        k = half * 8 + kk
                        ins = e.transpose(
                            PTY[:, kk * 128:(kk + 1) * 128],
                            YGL[:, (i % 2) * DI + k * 128:(i % 2) * DI + (k + 1) * 128],
                            E["id_bf"][:, :])
                    return ins
                P.add("tensor", ytr, r=[("ygld", i % 2), "id_bf", ("ptyr",)],
                      w=[("pty",)])

                def yev(e, i=i, half=half):
                    return e.tensor_copy(
                        YGT3[:, half * 8:(half + 1) * 8, i * 128:(i + 1) * 128],
                        r3(PTY[:, :], 8, 128))
                P.add("vector", yev, r=[("pty",)],
                      w=[("ygt", i, half), ("ptyr",)] +
                        ([("ygl", i % 2)] if half == 3 else []))
        for mg in range(4):
            P.t_dma("sync", WOS3,
                    E["d_wob"].rearrange("(k p) c -> p k c", p=128
                                         )[:, :, mg * 512:(mg + 1) * 512],
                    r=[("wosu",)] + [("wcv", "o", i4) for i4 in range(4)],
                    w=[("wos_raw",)])

            P.add("vector", lambda e: e.tensor_tensor(
                WOS3, WOS3, bfree(E["nsc_sb"][:, 0:32], 512), OP.mult),
                r=[("wos_raw",), "nsc_sb"], w=[("wos",)])
            for m in range(4):
                dmc = mg * 4 + m
                par = m % 2
                po = POA if par == 0 else POB

                def mmo(e, m=m, po=po):
                    ins = None
                    for k in range(32):
                        ins = e.matmul(po[:, :], WOS3[:, k, m * 128:(m + 1) * 128],
                                       YGT3[:, k, :], start=(k == 0),
                                       stop=(k == 31))
                    return ins
                P.add("tensor", mmo,
                      r=[("wos",), ("por", par)] +
                        [("ygt", i, half) for i in range(4) for half in range(4)],
                      w=[("po", par)] + ([("wosu",)] if m == 3 else []))
                P.add("vector", lambda e, po=po, dmc=dmc: e.tensor_add(
                    SSL[:, :], po[:, :], FFN_O[:, dmc * 512:(dmc + 1) * 512]),
                    r=[("po", par), ("ffo", dmc), ("sslr",)],
                    w=[("ssl",), ("por", par)])

                def ktr(e, dmc=dmc, par=par):
                    ins = None
                    pk = PKA if par == 0 else PKB
                    for ts in range(4):
                        ins = e.transpose(pk[:, ts * 128:(ts + 1) * 128],
                                          SSL[:, ts * 128:(ts + 1) * 128],
                                          E["id_f32"][:, :])
                    return ins
                P.add("tensor", ktr, r=[("ssl",), "id_f32", ("pkr", par)],
                      w=[("pk", par), ("sslr",)])

                def radd(e, dmc=dmc, par=par):
                    ins = None
                    pk = PKA if par == 0 else PKB
                    for ts in range(4):
                        ins = e.tensor_add(
                            OSB[:, ts * 2048 + dmc * 128:ts * 2048 + (dmc + 1) * 128],
                            pk[:, ts * 128:(ts + 1) * 128],
                            UTS[:, ts * 2048 + dmc * 128:ts * 2048 + (dmc + 1) * 128])
                    return ins
                P.add("vector", radd,
                      r=[("pk", par)] + [("uts", ts) for ts in range(4)],
                      w=[("osb", dmc), ("pkr", par)])
        for ts in range(4):
            P.t_dma("gpsimd", E["d_out"][tg * 512 + ts * 128:tg * 512 + (ts + 1) * 128, :],
                    OSB[:, ts * 2048:(ts + 1) * 2048],
                    r=[("osb", dmc2) for dmc2 in range(16)], w=["outw"])
        P.add("sync", lambda e: None, r=["outw"])
        P.run()


_CACHED = {}


def _stage_inputs(inputs):
    u = np.ascontiguousarray(np.asarray(inputs["L0_attn_u"], dtype=np.float32))
    w_in = np.ascontiguousarray(np.asarray(inputs["L0_attn_in_proj_w"], np.float32))
    convw = np.asarray(inputs["L0_attn_conv_weight"], np.float32)
    convb = np.asarray(inputs["L0_attn_conv_bias"], np.float32)
    A_log = np.asarray(inputs["L0_attn_A_log"], np.float32)
    dt_bias = np.asarray(inputs["L0_attn_dt_bias"], np.float32)
    D_param = np.asarray(inputs["L0_attn_D_param"], np.float32)
    norm_scale = np.asarray(inputs["L0_attn_norm_scale"], np.float32)
    w_out = np.ascontiguousarray(np.asarray(inputs["L0_attn_out_proj_w"], np.float32))
    w_upT = np.ascontiguousarray(
        np.asarray(inputs["L0_ffn_up_weight"], np.float32).T)
    w_dnT = np.ascontiguousarray(
        np.asarray(inputs["L0_ffn_down_weight"], np.float32).T)
    up_scale = np.asarray(inputs["L0_ffn_up_scale"], np.float32)
    down_scale = np.asarray(inputs["L0_ffn_down_scale"], np.float32)
    out_scale = np.asarray(inputs["L0_ffn_output_scale"], np.float32)

    cw = np.ascontiguousarray(
        convw.T.reshape(32, 128, 4).transpose(1, 0, 2).reshape(128, 128))
    cb = np.ascontiguousarray(convb.reshape(32, 128).T)
    nsc = np.ascontiguousarray(norm_scale.reshape(32, 128).T)
    usc = np.ascontiguousarray(up_scale.reshape(64, 128).T)
    dsc = np.ascontiguousarray(down_scale.reshape(16, 128).T)
    vecs = np.ascontiguousarray(np.stack([A_log, dt_bias, D_param]))
    osc = out_scale.reshape(1, 1)

    shared = dict(w_in=w_in, w_upT=w_upT, w_dnT=w_dnT, w_out=w_out,
                  cw=cw, cb=cb, nsc=nsc, usc=usc, dsc=dsc, vecs=vecs, osc=osc)
    zeros = np.zeros((1024, DM), np.float32)
    halo = np.zeros((128, DM), np.float32)
    in_maps = []
    for core in range(8):
        b, half = core // 2, core % 2
        prefix = u[b, 0:1024] if half == 1 else zeros
        own = u[b, half * 1024:(half + 1) * 1024]
        u_tok = np.ascontiguousarray(np.concatenate([halo, prefix, own], 0))
        m = dict(shared)
        m["u_tok"] = u_tok
        m["flag"] = np.array([[1.0 if half == 1 else 0.0]], np.float32)
        in_maps.append(m)
    return in_maps


def kernel(**inputs):
    if "nc" not in _CACHED:
        _CACHED["nc"] = build_program()
    nc = _CACHED["nc"]
    in_maps = _stage_inputs(inputs)
    res = run_bass_kernel_spmd(nc, in_maps, core_ids=list(range(8)))
    out = np.empty((4, 2048, DM), np.float32)
    for core in range(8):
        b, half = core // 2, core % 2
        out[b, half * 1024:(half + 1) * 1024] = res.results[core]["out"]
    return out


# version 16
# speedup vs baseline: 1.0258x; 1.0258x over previous
import numpy as np
import ml_dtypes
import concourse.bass as bass
import concourse.mybir as mybir
from concourse.bass_utils import run_bass_kernel_spmd

F32 = mybir.dt.float32
BF16 = mybir.dt.bfloat16
AF = mybir.ActivationFunctionType
OP = mybir.AluOpType
AX = mybir.AxisListType
BF16NP = ml_dtypes.bfloat16

NT = 17
TTOT = NT * 128
DM = 2048
DI = 4096
DH = 8192
NH = 64

ENGINES = ("sync", "scalar", "vector", "tensor", "gpsimd")


def bfree(ap2d, inner):
    return ap2d.unsqueeze(2).broadcast_to([ap2d.shape[0], ap2d.shape[1], inner])


def r3(ap2d, a, b):
    return ap2d.rearrange("p (a b) -> p a b", a=a, b=b)


class Task:
    __slots__ = ("eng", "emit", "reads", "writes", "dma", "seq", "done",
                 "waits", "prewait", "fast")

    def __init__(self, eng, emit, reads, writes, dma, fast=False):
        self.eng, self.emit = eng, emit
        self.reads, self.writes = list(reads), list(writes)
        self.dma = dma
        self.fast = fast


class Phase:

    def __init__(self, C, name):
        self.C = C
        self.name = name
        self.tasks = []

    def add(self, eng, emit, r=(), w=(), dma=False, fast=False):
        self.tasks.append(Task(eng, emit, r, w, dma, fast))

    def t_dma(self, eng, out_ap, in_ap, r=(), w=(), **kw):
        def emit(e):
            return e.dma_start(out_ap, in_ap, **kw)
        self.add(eng, emit, r=r, w=w, dma=True)

    NDMA = 6

    def run(self):
        C = self.C
        nc = C.nc
        cnt = {}
        last_writer = C.last_writer
        readers = C.readers
        for t in self.tasks:
            if t.dma:
                k = C.dma_idx.get(t.eng, 0)
                C.dma_idx[t.eng] = k + 1
                slot = k % self.NDMA
                use = k // self.NDMA
                sname = f"{t.eng}D{slot}"
                t.prewait = (sname, 16 * use) if use > 0 else None
                t.done = (sname, 16 * (use + 1))
            else:
                key = (t.eng, t.dma)
                base = C.semval(self._sname(t.eng, t.dma))
                cnt[key] = cnt.get(key, 0) + 1
                t.seq = cnt[key]
                t.done = (self._sname(t.eng, t.dma), base + t.seq)
                t.prewait = None
        for t in self.tasks:
            waits = {}

            def need(dep, raw=False):
                if dep is None or dep is t:
                    return
                if (raw and dep.eng == t.eng and not dep.dma and not t.dma):
                    if dep.eng == "tensor":
                        return
                    if dep.eng in ("vector", "scalar") and not dep.fast:
                        return
                sname, val = dep.done
                waits[sname] = max(waits.get(sname, 0), val)

            for tok in t.reads:
                need(last_writer.get(tok), raw=True)
            for tok in t.writes:
                need(last_writer.get(tok))
                for rd in readers.get(tok, ()):
                    need(rd)
            t.waits = waits
            for tok in t.reads:
                readers.setdefault(tok, []).append(t)
            for tok in t.writes:
                last_writer[tok] = t
                readers[tok] = []
        with nc.Block(self.name) as blk:
            for eng in ENGINES:
                mytasks = [t for t in self.tasks if t.eng == eng]
                if not mytasks:
                    continue

                def body(e, mytasks=mytasks, eng=eng):
                    hwm = {}
                    for t in mytasks:
                        allw = dict(t.waits)
                        if t.prewait is not None:
                            sn, v = t.prewait
                            allw[sn] = max(allw.get(sn, 0), v)
                        for sname, val in sorted(allw.items()):
                            if hwm.get(sname, -1) >= val:
                                continue
                            e.wait_ge(C.sem(sname), val)
                            hwm[sname] = val
                        safe = (not t.dma) and (
                            C.race_detect or
                            (eng in ("vector", "scalar") and not t.fast))
                        ee = SafeEng(e, C, eng) if safe else e
                        ins = t.emit(ee)
                        sname, val = t.done
                        inc = 16 if t.dma else 1
                        if ins is None or safe:
                            e.sem_inc(C.sem(sname), inc)
                        else:
                            ins.then_inc(C.sem(sname), inc)
                        C.setsem(sname, val)

                getattr(blk, eng)(body)

    @staticmethod
    def _sname(eng, dma):
        return f"{eng}{'D' if dma else ''}"


class SafeEng:

    def __init__(self, eng, C, name):
        self._e = eng
        self._C = C
        self._n = name

    def __getattr__(self, name):
        attr = getattr(self._e, name)
        if not callable(attr) or name in ("wait_ge", "sem_inc", "nop",
                                          "dma_start", "load_library"):
            return attr

        def wrap(*a, **k):
            r = attr(*a, **k)
            C = self._C
            sn = "safe_" + self._n
            r.then_inc(C.sem(sn), 1)
            v = C.inc_safe(sn)
            self._e.wait_ge(C.sem(sn), v)
            return r
        return wrap


class Ctx:
    def __init__(self, dbg=(), race_detect=False):
        self.race_detect = race_detect
        self.dbg = set(dbg)
        self.nc = bass.Bass("TRN2", target_bir_lowering=False, debug=False,
                            detect_race_conditions=race_detect)
        self._sems = {}
        self._semn = {}
        self.dma_idx = {}
        self._safe = {}
        self.last_writer = {}
        self.readers = {}

    def inc_safe(self, name):
        self._safe[name] = self._safe.get(name, 0) + 1
        return self._safe[name]

    def sem(self, name):
        if name not in self._sems:
            self._sems[name] = self.nc.alloc_semaphore(name)
            self._semn[name] = 0
        return self._sems[name]

    def semval(self, name):
        self.sem(name)
        return self._semn[name]

    def setsem(self, name, v):
        self._semn[name] = max(self._semn[name], v)


def build_program(dbg=(), race_detect=False):
    C = Ctx(dbg, race_detect)
    nc = C.nc
    E = {}

    def dram(name, shape, dtype, kind="Internal"):
        k = "ExternalOutput" if name in C.dbg else kind
        E["d_" + name] = nc.dram_tensor(name, shape, dtype, kind=k).ap()

    for name, shape in [
        ("u_tok", [TTOT, DM]), ("w_in", [DM, 8384]),
        ("w_upT", [DM, DH]), ("w_dnT", [DH, DM]), ("w_out", [DI, DM]),
        ("cw", [128, 128]), ("cb", [128, 32]), ("nsc", [128, 32]),
        ("usc", [128, 64]), ("dsc", [128, 16]), ("vecs", [3, 64]),
        ("osc", [1, 1]), ("flag", [1, 1]),
    ]:
        E["d_" + name] = nc.dram_tensor(name, shape, F32, kind="ExternalInput").ap()
    E["d_out"] = nc.dram_tensor("out", [1024, DM], F32, kind="ExternalOutput").ap()

    dram("xnf", [16, 128, TTOT], BF16)
    dram("wb", [DM, 8384], BF16)
    dram("wub", [DM, DH], BF16)
    dram("wdb", [DH, DM], BF16)
    dram("wob", [DI, DM], BF16)
    dram("yg", [8, 128, DI], BF16)
    dram("zs", [8, 128, DI], BF16)
    dram("csd", [32, NH * 64], F32)
    dram("trow", [1, 2048], F32)
    dram("nrow", [1, 64], F32)
    if "stats" in C.dbg:
        E["d_pstat"] = nc.dram_tensor("pstat", [128, 6 * NT], F32,
                                      kind="ExternalOutput").ap()
    if "probe2" in C.dbg:
        for nm, shape in [("py", [128, DI]), ("patt", [128, NH * 128]),
                          ("pG", [128, 64]), ("pD", [128, NH * 64]),
                          ("phb", [64, DI]), ("pqs", [128, 64]),
                          ("pkd", [128, 128]), ("pvs", [128, DI])]:
            E["d_" + nm] = nc.dram_tensor(nm, shape, F32,
                                          kind="ExternalOutput").ap()
    if "probe" in C.dbg:
        E["d_ph0"] = nc.dram_tensor("ph0", [64, DI], F32,
                                    kind="ExternalOutput").ap()
        E["d_pdecp"] = nc.dram_tensor("pdecp", [128, 8 * 64], F32,
                                      kind="ExternalOutput").ap()
        E["d_pkq"] = nc.dram_tensor("pkq", [128, 2 * 16 * 64], F32,
                                    kind="ExternalOutput").ap()
        E["d_pvt"] = nc.dram_tensor("pvt", [128, 4 * DI], F32,
                                    kind="ExternalOutput").ap()

    A = nc.alloc_sbuf_tensor
    for nm, shape, dt in [
        ("causal", [128, 64], BF16), ("id_bf", [128, 128], BF16),
        ("id_f32", [128, 128], F32), ("cw_sb", [128, 128], F32),
        ("cb_sb", [128, 32], F32), ("nsc_sb", [128, 32], F32),
        ("usc_sb", [128, 64], F32), ("dsc_sb", [128, 16], F32),
        ("arow", [1, 64], F32), ("brow", [1, 64], F32), ("drow", [1, 64], F32),
        ("oscv", [1, 1], F32), ("flagv", [1, 1], F32), ("OSCB", [128, 1], F32),
        ("NEGA", [128, 64], F32), ("DTB", [128, 64], F32), ("DP", [128, 64], F32),
        ("FLG", [64, 1], F32),
        ("mu", [128, NT], F32), ("rstd", [128, NT], F32), ("nmr", [128, NT], F32),
        ("musum", [128, NT], F32), ("sum2", [128, NT], F32),
        ("stat_t", [128, NT], F32),
    ]:
        E[nm] = A(nm, shape, dt)

    import contextlib
    early_stack = contextlib.ExitStack()
    for nm, shape, dt in [
        ("ones", [128, 2048], F32), ("reset", [64, 2048], F32),
        ("dtv", [128, 16 * 64], F32), ("CS", [64, 2048], F32),
        ("GCS", [64, 1024], F32),
    ]:
        E[nm] = early_stack.enter_context(nc.sbuf_tensor(nm, shape, dt,
                                                         side="right"))

    _p0_consts(C, E)
    _p1_stats(C, E)
    _p2_xnf(C, E)

    ssd_stack = contextlib.ExitStack()
    for nm, shape, dt in [
        ("k_tok", [128, 16 * 64], BF16), ("q_tok", [128, 16 * 64], BF16),
        ("CS_T", [128, 16 * 64], F32), ("ECS_T", [128, 16 * 64], F32),
        ("DEC_T", [128, 16 * 64], F32), ("DECP_T", [128, 8 * 64], F32),
        ("DBC64", [64, 2048], F32), ("h_st", [64, DI], F32),
        ("vt", [128, 4 * DI], BF16),
        ("xtail", [128, 32 * 12], F32), ("sqs_cv", [128, 512], F32),
    ]:
        E[nm] = ssd_stack.enter_context(nc.sbuf_tensor(nm, shape, dt))

    _p3_bcdt(C, E)
    _p4_csfam(C, E)
    early_stack.close()
    _p5_z(C, E)
    _p6_main(C, E)
    ssd_stack.close()
    _p7_ffn_out(C, E)
    return nc


def _p0_consts(C, E):
    nc = C.nc
    P = Phase(C, "p0")
    for dst, src in [
        (E["cw_sb"][:, :], E["d_cw"]), (E["cb_sb"][:, :], E["d_cb"]),
        (E["nsc_sb"][:, :], E["d_nsc"]), (E["usc_sb"][:, :], E["d_usc"]),
        (E["dsc_sb"][:, :], E["d_dsc"]), (E["oscv"][:, :], E["d_osc"]),
        (E["flagv"][:, :], E["d_flag"]),
        (E["arow"][:, :], E["d_vecs"][0:1, :]),
        (E["brow"][:, :], E["d_vecs"][1:2, :]),
        (E["drow"][:, :], E["d_vecs"][2:3, :]),
    ]:
        nm = src.name if hasattr(src, "name") else "x"
        P.t_dma("sync", dst, src, w=[id(dst.tensor) if hasattr(dst, "tensor") else nm])
    P.tasks = []
    loads = [
        ("cw_sb", E["d_cw"]), ("cb_sb", E["d_cb"]), ("nsc_sb", E["d_nsc"]),
        ("usc_sb", E["d_usc"]), ("dsc_sb", E["d_dsc"]), ("oscv", E["d_osc"]),
        ("flagv", E["d_flag"]),
    ]
    for nm, src in loads:
        P.t_dma("sync", E[nm][:, :], src, w=[nm])
    P.t_dma("sync", E["arow"][:, :], E["d_vecs"][0:1, :], w=["arow"])
    P.t_dma("sync", E["brow"][:, :], E["d_vecs"][1:2, :], w=["brow"])
    P.t_dma("sync", E["drow"][:, :], E["d_vecs"][2:3, :], w=["drow"])

    P.add("scalar", lambda e: e.activation(E["arow"][:, :], E["arow"][:, :], AF.Exp),
          r=["arow"], w=["arow"])
    P.add("vector", lambda e: e.tensor_scalar_mul(E["arow"][:, :], E["arow"][:, :], -1.0),
          r=["arow"], w=["arow"])
    P.t_dma("gpsimd", E["d_nrow"], E["arow"][:, :], r=["arow"],
            w=["nrow_dram"])

    P.add("gpsimd", lambda e: e.memset(E["ones"][:, :], 1.0), w=["ones"])
    P.add("gpsimd", lambda e: e.affine_select(
        E["reset"][:, :], E["ones"][0:64, :], [[0, 32], [1, 64]], OP.is_gt, 0.0,
        base=0, channel_multiplier=0), r=["ones"], w=["reset"])
    P.add("gpsimd", lambda e: e.affine_select(
        E["causal"][0:64, :], E["ones"][0:64, 0:64], [[1, 64]], OP.is_ge, 0.0,
        base=0, channel_multiplier=-1), r=["ones"], w=["causal"])
    P.add("gpsimd", lambda e: e.affine_select(
        E["causal"][64:128, :], E["ones"][0:64, 0:64], [[1, 64]], OP.is_ge, 0.0,
        base=0, channel_multiplier=-1), r=["ones"], w=["causal"])
    P.add("gpsimd", lambda e: e.affine_select(
        E["id_bf"][:, :], E["ones"][:, 0:128], [[-1, 128]], OP.is_equal, 0.0,
        base=0, channel_multiplier=1), r=["ones"], w=["id_bf"])
    P.add("gpsimd", lambda e: e.affine_select(
        E["id_f32"][:, :], E["ones"][:, 0:128], [[-1, 128]], OP.is_equal, 0.0,
        base=0, channel_multiplier=1), r=["ones"], w=["id_f32"])
    P.t_dma("gpsimd", E["NEGA"][:, :], E["d_nrow"].partition_broadcast(128),
            r=["nrow_dram"], w=["NEGA"])
    P.t_dma("gpsimd", E["DTB"][:, :],
            E["d_vecs"][1:2, :].partition_broadcast(128), w=["DTB"])
    P.t_dma("gpsimd", E["DP"][:, :],
            E["d_vecs"][2:3, :].partition_broadcast(128), w=["DP"])
    P.t_dma("gpsimd", E["FLG"][:, :], E["d_flag"].partition_broadcast(64),
            w=["FLG"])
    P.t_dma("gpsimd", E["OSCB"][:, :], E["d_osc"].partition_broadcast(128),
            w=["OSCB"])
    P.add("vector", lambda e: e.tensor_scalar_mul(
        E["dsc_sb"][:, :], E["dsc_sb"][:, :], E["OSCB"][:, 0:1]),
        r=["OSCB", "dsc_sb"], w=["dsc_sb"])
    P.t_dma("gpsimd", E["d_wb"][:, 8192:8384], E["d_w_in"][:, 8192:8384],
            w=[("wcv", "b")])
    P.run()


def _p1_stats(C, E):
    nc = C.nc
    with nc.sbuf_tensor("UT1", [128, 2 * DM], F32) as UT, \
         nc.sbuf_tensor("SQS1", [128, DM], F32) as SQS:
        P = Phase(C, "p1")
        for i in range(NT):
            buf = UT[:, (i % 2) * DM:(i % 2 + 1) * DM]
            P.t_dma("sync", buf, E["d_u_tok"][i * 128:(i + 1) * 128, :],
                    w=[("ut", i % 2)])
            P.add("vector", lambda e, b=buf, i=i: e.reduce_sum(
                E["musum"][:, i:i + 1], b, axis=AX.X), r=[("ut", i % 2)],
                w=["musum"])
            P.add("scalar", lambda e, b=buf, i=i: e.activation(
                SQS[:, :], b, AF.Square, accum_out=E["sum2"][:, i:i + 1]),
                r=[("ut", i % 2)], w=["sum2"])
        P.run()

    P = Phase(C, "p1b")

    def v1(e):
        MU, RS, NM, T = E["mu"], E["rstd"], E["nmr"], E["stat_t"]
        e.tensor_scalar_mul(MU[:, :], E["musum"][:, :], 1.0 / DM)
        e.tensor_scalar_mul(T[:, :], E["sum2"][:, :], 1.0 / DM)
        e.tensor_mul(RS[:, :], MU[:, :], MU[:, :])
        e.tensor_sub(T[:, :], T[:, :], RS[:, :])
        return e.tensor_scalar_add(T[:, :], T[:, :], 1e-5)
    P.add("vector", v1, r=["musum", "sum2"], w=["mu", "vareps"])
    P.add("scalar", lambda e: e.activation(E["rstd"][:, :], E["stat_t"][:, :],
                                           AF.Sqrt), r=["vareps"], w=["sq"])

    def v2(e):
        RS, T, MU, NM = E["rstd"], E["stat_t"], E["mu"], E["nmr"]
        R0, S2 = E["musum"], E["sum2"]
        e.reciprocal(R0[:, :], RS[:, :])
        e.tensor_mul(S2[:, :], R0[:, :], R0[:, :])
        e.tensor_mul(S2[:, :], S2[:, :], T[:, :])
        e.tensor_scalar(S2[:, :], S2[:, :], -0.5, 1.5, OP.mult, OP.add)
        e.tensor_mul(RS[:, :], R0[:, :], S2[:, :])
        e.tensor_mul(NM[:, :], MU[:, :], RS[:, :])
        return e.tensor_scalar_mul(NM[:, :], NM[:, :], -1.0)
    P.add("vector", v2, r=["sq", "mu", "vareps"], w=["rstd", "musum", "sum2"])
    if "stats" in C.dbg:
        for i, nm in enumerate(["mu", "rstd", "nmr", "musum", "sum2",
                                "stat_t"]):
            P.t_dma("gpsimd", E["d_pstat"][:, i * NT:(i + 1) * NT],
                    E[nm][:, :], r=["rstd", "mu"], w=["pst"])
        P.add("sync", lambda e: None, r=["pst"])
    P.run()


def _p2_xnf(C, E):
    nc = C.nc
    groups = [(0, 4), (4, 4), (8, 4), (12, 4), (16, 1)]
    with nc.sbuf_tensor("UT2", [128, 2 * DM], F32) as UT, \
         nc.sbuf_tensor("XT2", [128, 2 * DM], BF16) as XT, \
         nc.sbuf_tensor("XSL2", [128, 2 * 16 * 512], BF16) as XSL, \
         nc.psum_tensor("ps2a", [128, 1024], BF16) as PSA, \
         nc.psum_tensor("ps2b", [128, 1024], BF16) as PSB:
        P = Phase(C, "p2")
        XSL4 = XSL.rearrange("p (r k t) -> p r k t", r=2, k=16)
        for gi, (t0, gn) in enumerate(groups):
            for j in range(gn):
                i = t0 + j
                n = gi * 4 + j
                pr = n % 2
                ub = UT[:, pr * DM:(pr + 1) * DM]
                xb = XT[:, pr * DM:(pr + 1) * DM]
                ps = PSA if pr == 0 else PSB
                P.t_dma("sync", ub, E["d_u_tok"][i * 128:(i + 1) * 128, :],
                        w=[("u2", pr)])
                P.add("scalar", lambda e, ub=ub, xb=xb, i=i: e.activation(
                    xb, ub, AF.Identity, bias=E["nmr"][:, i:i + 1],
                    scale=E["rstd"][:, i:i + 1]),
                    r=[("u2", pr), "rstd"], w=[("xn", pr)])
                for half in range(2):
                    def trh(e, xb=xb, ps=ps, half=half):
                        ins = None
                        for k in range(half * 8, half * 8 + 8):
                            ins = e.transpose(
                                ps[:, (k % 8) * 128:(k % 8) * 128 + 128],
                                xb[:, k * 128:(k + 1) * 128], E["id_bf"][:, :])
                        return ins
                    P.add("tensor", trh, r=[("xn", pr), "id_bf"],
                          w=[("tph", pr)])
                    dst = XSL4[:, gi % 2, half * 8:(half + 1) * 8,
                               j * 128:(j + 1) * 128]
                    P.add("vector", lambda e, dst=dst, ps=ps: e.tensor_copy(
                        dst, r3(ps[:, :], 8, 128)),
                        r=[("tph", pr)],
                        w=[("slab", gi % 2)])
            dst = E["d_xnf"].rearrange("k d t -> d k t"
                                       )[:, :, t0 * 128:t0 * 128 + gn * 128]
            P.t_dma("scalar", dst, XSL4[:, gi % 2, :, 0:gn * 128],
                    r=[("slab", gi % 2)], w=[("xnfd", gi), ("slab", gi % 2)])
            P2W = {0: [("z", 0), ("z", 1)], 1: [("z", 2), ("z", 3)],
                   2: [("z", 4), ("z", 5)], 3: [("z", 6), ("z", 7)],
                   4: [("x", 0), ("x", 1)]}
            for fam, ci in P2W.get(gi, []):
                off = 4096 if fam == "x" else 0
                sl = slice(off + ci * 512, off + (ci + 1) * 512)
                P.t_dma("gpsimd", E["d_wb"][:, sl], E["d_w_in"][:, sl],
                        r=[("xnfd", gi)], w=[("wcv", fam, ci)])
        P.add("sync", lambda e: None,
              r=[("xnfd", g2) for g2 in range(len(groups))], w=["xnf_ready"])
        P.run()


def _p3_bcdt(C, E):
    nc = C.nc
    with nc.sbuf_tensor("WBCDT", [128, 16 * 192], BF16) as W, \
         nc.sbuf_tensor("XST3", [128, 2 * 16 * 512], BF16) as XST, \
         nc.sbuf_tensor("p3T1", [128, 2 * 64], F32) as T1, \
         nc.sbuf_tensor("p3T2", [128, 2 * 64], F32) as T2, \
         nc.sbuf_tensor("p3T3", [128, 2 * 64], F32) as T3, \
         nc.sbuf_tensor("p3MSK", [128, 2 * 64], mybir.dt.int32) as MSK, \
         nc.psum_tensor("ps3a", [128, 192], F32) as PA, \
         nc.psum_tensor("ps3b", [128, 192], F32) as PB:
        P = Phase(C, "p3")
        W3 = r3(W[:, :], 16, 192)
        XST4 = XST.rearrange("p (r k t) -> p r k t", r=2, k=16)
        P.t_dma("sync", W3,
                E["d_wb"].rearrange("(k p) c -> p k c", p=128)[:, :, 8192:8384],
                r=[("wcv", "b")], w=["W3"])
        for g in range(4):
            src = E["d_xnf"].rearrange("k d t -> d k t"
                                       )[:, :, 128 + g * 512:128 + (g + 1) * 512]
            P.t_dma("sync", XST4[:, g % 2], src,
                    r=[("xst_used", g - 2), "xnf_ready"], w=[("xst", g)])
            if g >= 1:
                ci = g + 1
                sl = slice(4096 + ci * 512, 4096 + (ci + 1) * 512)
                P.t_dma("gpsimd", E["d_wb"][:, sl], E["d_w_in"][:, sl],
                        r=[("xst", g)], w=[("wcv", "x", ci)])
            for j in range(4):
                idx = g * 4 + j
                par = idx % 2
                ps = PA if par == 0 else PB

                def mm(e, g=g, j=j, ps=ps):
                    ins = None
                    for k in range(16):
                        ins = e.matmul(ps[:, :], XST4[:, g % 2, k,
                                                      j * 128:(j + 1) * 128],
                                       W3[:, k, :], start=(k == 0),
                                       stop=(k == 15))
                    return ins
                P.add("tensor", mm, r=[("xst", g), "W3", ("ps3r", par)],
                      w=[("ps3", par), ("xst_used", g) if j == 3 else ("xju", g, j)])

                for col, dst in ((0, "k_tok"), (64, "q_tok")):
                    P.add("vector", lambda e, ps=ps, col=col, par=par:
                          e.tensor_scalar_min(T1[:, par * 64:(par + 1) * 64],
                                              ps[:, col:col + 64], 0.0),
                          r=[("ps3", par)], w=[("T1", par)])
                    P.add("scalar", lambda e, par=par: e.activation(
                        T2[:, par * 64:(par + 1) * 64],
                        T1[:, par * 64:(par + 1) * 64], AF.Exp),
                        r=[("T1", par)], w=[("T2", par)])
                    P.add("vector", lambda e, ps=ps, col=col, par=par, dst=dst,
                          idx=idx: e.scalar_tensor_tensor(
                              E[dst][:, idx * 64:(idx + 1) * 64],
                              ps[:, col:col + 64], 0.0,
                              T2[:, par * 64:(par + 1) * 64], OP.max, OP.add),
                          r=[("ps3", par), ("T2", par)], w=[dst, ("T1", par)])

                def vdt(e, ps=ps, par=par):
                    e.tensor_add(T1[:, par * 64:(par + 1) * 64],
                                 ps[:, 128:192], E["DTB"][:, 0:64])
                    e.tensor_scalar_min(T3[:, par * 64:(par + 1) * 64],
                                        T1[:, par * 64:(par + 1) * 64], 20.0)
                    return e.tensor_single_scalar(
                        MSK[:, par * 64:(par + 1) * 64],
                        T1[:, par * 64:(par + 1) * 64], 20.0, OP.is_gt)
                P.add("vector", vdt, r=[("ps3", par), "DTB"],
                      w=[("T1", par), ("T3", par), ("MSK", par)])

                def sdt(e, par=par):
                    e.activation(T2[:, par * 64:(par + 1) * 64],
                                 T3[:, par * 64:(par + 1) * 64], AF.Exp)
                    return e.activation(T2[:, par * 64:(par + 1) * 64],
                                        T2[:, par * 64:(par + 1) * 64],
                                        AF.Ln, bias=1.0)
                P.add("scalar", sdt, r=[("T3", par)], w=[("T2", par)])
                P.add("vector", lambda e, par=par, idx=idx: e.select(
                    E["dtv"][:, idx * 64:(idx + 1) * 64],
                    MSK[:, par * 64:(par + 1) * 64],
                    T1[:, par * 64:(par + 1) * 64],
                    T2[:, par * 64:(par + 1) * 64], add_drain=True),
                    r=[("T2", par), ("T1", par), ("MSK", par)],
                    w=["dtv", ("ps3r", par)])
        P.run()


def _p4_csfam(C, E):
    nc = C.nc
    with nc.sbuf_tensor("p4XX", [64, 2048], F32) as XX, \
         nc.sbuf_tensor("p4YY", [64, 1024], F32) as YY, \
         nc.sbuf_tensor("p4TOT", [64, 32], F32) as TOT, \
         nc.sbuf_tensor("p4TOTT", [32, 64], F32) as TOTT, \
         nc.psum_tensor("ps4a", [128, 512], F32) as PSA, \
         nc.psum_tensor("ps4b", [128, 512], F32) as PSB:
        P = Phase(C, "p4")
        P.add("vector", lambda e: e.tensor_tensor(
            r3(E["dtv"][:, :], 16, 64), r3(E["dtv"][:, :], 16, 64),
            E["NEGA"][:, :].unsqueeze(1).broadcast_to([128, 16, 64]), OP.mult),
            r=["dtv", "NEGA"], w=["dtv"])
        for g in range(4):
            ps = PSA if g % 2 == 0 else PSB

            def trs(e, g=g, ps=ps):
                ins = None
                for i in range(4):
                    ins = e.transpose(ps[0:64, i * 128:(i + 1) * 128],
                                      E["dtv"][:, (g * 4 + i) * 64:(g * 4 + i + 1) * 64],
                                      E["id_f32"][:, :])
                return ins
            P.add("tensor", trs, r=["dtv", "id_f32", ("ps4r", g % 2)],
                  w=[("ps4", g % 2)])
            P.add("vector", lambda e, g=g, ps=ps: e.tensor_copy(
                XX[:, g * 512:(g + 1) * 512], ps[0:64, 0:512]),
                r=[("ps4", g % 2)], w=["LAH", ("ps4r", g % 2)])

        def scans(e):
            e.tensor_tensor_scan(E["CS"][:, :], E["reset"][:, :], XX[:, :],
                                 0.0, OP.mult, OP.add)
            e.tensor_tensor_scan(E["GCS"][:, :], E["ones"][0:64, 0:1024],
                                 XX[:, 0:1024], 0.0, OP.mult, OP.add)
            e.tensor_copy(TOT[:, :],
                          E["CS"].rearrange("h (c s) -> h c s", c=32)[:, :, 63])
            e.tensor_tensor(r3(XX[:, :], 32, 64), bfree(TOT[:, :], 64),
                            r3(E["CS"][:, :], 32, 64), OP.subtract)
            return e.tensor_tensor(
                YY[:, :].unsqueeze(1),
                bfree(E["GCS"][:, 1023:1024], 1024).transpose([0, 2, 1]),
                E["GCS"][:, :].unsqueeze(1), OP.subtract)
        P.add("vector", scans, r=["LAH", "reset", "ones"],
              w=["CS", "GCS", "TOT", "XXd", "YYd", "LAH"])

        def exps(e):
            e.activation(XX[:, :], XX[:, :], AF.Exp)
            e.activation(YY[:, :], YY[:, :], AF.Exp)
            return e.activation(TOT[:, :], TOT[:, :], AF.Exp)
        P.add("scalar", exps, r=["XXd", "YYd", "TOT"],
              w=["XXd", "YYd", "TOT"])

        jobs = [(E["CS"], "CS_T", 16, "CS"), (XX, "DEC_T", 16, "XXd"),
                (YY, "DECP_T", 8, "YYd")]
        gidx = 0
        for src, dstn, cnt, tok in jobs:
            for g in range((cnt + 3) // 4):
                take = min(4, cnt - g * 4)
                ps = PSA if gidx % 2 == 0 else PSB

                def trs2(e, src=src, g=g, take=take, ps=ps):
                    ins = None
                    for i in range(take):
                        ins = e.transpose(ps[:, i * 64:(i + 1) * 64],
                                          src[:, (g * 4 + i) * 128:(g * 4 + i + 1) * 128],
                                          E["id_f32"][0:64, 0:64])
                    return ins
                P.add("tensor", trs2, r=[tok, "id_f32", ("ps4r", gidx % 2)],
                      w=[("ps4", gidx % 2)])
                dst = E[dstn][:, (g * 4) * 64:(g * 4) * 64 + take * 64]
                P.add("vector", lambda e, dst=dst, ps=ps, take=take:
                      e.tensor_copy(dst, ps[:, 0:take * 64]),
                      r=[("ps4", gidx % 2)], w=[dstn, ("ps4r", gidx % 2)])
                gidx += 1
        P.add("tensor", lambda e: e.transpose(PSA[0:32, 0:64], TOT[:, :],
                                              E["id_f32"][0:64, 0:64]),
              r=["TOT", "id_f32", ("ps4r", 0)], w=[("ps4", 0)])
        P.add("vector", lambda e: e.tensor_copy(TOTT[:, :], PSA[0:32, 0:64]),
              r=[("ps4", 0)], w=["TOTT", ("ps4r", 0)])
        P.add("scalar", lambda e: e.activation(E["ECS_T"][:, :], E["CS_T"][:, :],
                                               AF.Exp), r=["CS_T"], w=["ECS_T"])
        for ci in (5, 6, 7):
            sl = slice(4096 + ci * 512, 4096 + (ci + 1) * 512)
            P.t_dma("gpsimd", E["d_wb"][:, sl], E["d_w_in"][:, sl],
                    r=["CS"], w=[("wcv", "x", ci)])
        P.t_dma("scalar", E["d_csd"].rearrange("c (h s) -> h c s", h=64),
                r3(E["CS"][:, :], 32, 64), r=["CS"], w=["csd_dram"])
        P.t_dma("scalar", E["d_trow"].rearrange("o (c h) -> (o c) h", c=32),
                TOTT[:, :], r=["TOTT"], w=["trow_dram"])
        P.t_dma("gpsimd", E["DBC64"][:, :],
                E["d_trow"].partition_broadcast(64), r=["trow_dram"],
                w=["DBC64"])
        P.add("sync", lambda e: None, r=["csd_dram"])
        P.run()


def _p5_z(C, E):
    nc = C.nc
    with nc.sbuf_tensor("WZ5", [128, 2 * 16 * 512], BF16) as WZ, \
         nc.sbuf_tensor("XST5", [128, 2 * 16 * 512], BF16) as XST, \
         nc.sbuf_tensor("ZR5", [128, 4 * 512], BF16) as ZR, \
         nc.psum_tensor("ps5a", [128, 512], F32) as PA, \
         nc.psum_tensor("ps5b", [128, 512], F32) as PB:
        P = Phase(C, "p5")
        WZ4 = WZ.rearrange("p (r k c) -> p r k c", r=2, k=16)
        XST4 = XST.rearrange("p (r k t) -> p r k t", r=2, k=16)
        for tg in range(2):
            src = E["d_xnf"].rearrange("k d t -> d k t"
                                       )[:, :, 1152 + tg * 512:1152 + (tg + 1) * 512]
            P.t_dma("sync", XST4[:, tg], src,
                    r=["xnf_ready"], w=[("xst5", tg)])
        for zb in range(8):
            P.t_dma("sync", WZ4[:, zb % 2],
                    E["d_wb"].rearrange("(k p) c -> p k c", p=128
                                        )[:, :, zb * 512:(zb + 1) * 512],
                    r=[("wzu", zb - 2), ("wcv", "z", zb)], w=[("wz", zb)])
            for tg in range(2):
                for j in range(4):
                    idx = zb * 8 + tg * 4 + j
                    par = idx % 2
                    ps = PA if par == 0 else PB

                    def mm(e, zb=zb, tg=tg, j=j, ps=ps):
                        ins = None
                        for k in range(16):
                            ins = e.matmul(ps[:, :],
                                           XST4[:, tg, k, j * 128:(j + 1) * 128],
                                           WZ4[:, zb % 2, k, :],
                                           start=(k == 0), stop=(k == 15))
                        return ins
                    P.add("tensor", mm,
                          r=[("xst5", tg), ("wz", zb), ("ps5r", par)],
                          w=[("ps5", par)] + ([("wzu", zb)] if (tg == 1 and j == 3) else []))
                    P.add("scalar", lambda e, ps=ps, idx=idx: e.activation(
                        ZR[:, (idx % 4) * 512:(idx % 4 + 1) * 512], ps[:, :],
                        AF.Silu),
                        r=[("ps5", par), ("zru", idx - 3)],
                        w=[("zr", idx), ("ps5r", par)])
                    ti = tg * 4 + j
                    P.t_dma("scalar", E["d_zs"][ti, :, zb * 512:(zb + 1) * 512],
                            ZR[:, (idx % 4) * 512:(idx % 4 + 1) * 512],
                            r=[("zr", idx)], w=[("zru", idx), ("zsd", zb, ti)])
        P.run()


CONV_LEVEL = 99
CONV_SKIP = ""
CONV_NMG = 8

XBLOCKS = [(0, 512, (1, 2, 3)), (512, 512, (4, 5, 6, 7)),
           (1024, 512, (8, 9, 10, 11)), (1536, 512, (12, 13, 14, 15)),
           (2048, 128, (16,))]


def _p6_conv_blk(C, E, bi):
    nc = C.nc
    a, w, tiles = XBLOCKS[bi]
    with nc.sbuf_tensor(f"XMV6_{bi}", [128, 16 * 512], BF16) as XMV, \
         nc.sbuf_tensor(f"WXG6_{bi}", [128, 2 * 16 * 512], BF16) as WXG, \
         nc.sbuf_tensor(f"XJ6_{bi}", [128, 2 * 4 * 516], F32) as XJ, \
         nc.psum_tensor(f"ps6xa_{bi}", [128, 512], F32) as PXA, \
         nc.psum_tensor(f"ps6xb_{bi}", [128, 512], F32) as PXB, \
         nc.psum_tensor(f"ps6ca_{bi}", [128, 512], F32) as PCA, \
         nc.psum_tensor(f"ps6cb_{bi}", [128, 512], F32) as PCB:
        P = Phase(C, f"cv{bi}")
        WXG4 = WXG.rearrange("p (r k c) -> p r k c", r=2, k=16)
        XJ4 = XJ.rearrange("p (r j c) -> p r j c", r=2, j=4)
        P.t_dma("sync", r3(XMV[:, 0:16 * w], 16, w),
                E["d_xnf"].rearrange("k d t -> d k t")[:, :, a:a + w],
                r=["xnf_ready"], w=["XMV"])
        CVW = {1: [("u", 0), ("u", 1), ("u", 2), ("u", 3)],
               2: [("u", 4), ("u", 5), ("u", 6), ("u", 7)],
               3: [("d", 3), ("d", 4), ("d", 5), ("d", 6)]}
        for j, (fam, ci) in enumerate(CVW.get(bi, [])):
            st_, dt_, rows = {"u": (E["d_w_upT"], E["d_wub"], 256),
                              "d": (E["d_w_dnT"], E["d_wdb"], 1024)}[fam]
            sl = slice(ci * rows, (ci + 1) * rows)
            P.t_dma("gpsimd", dt_[sl, :], st_[sl, :],
                    r=[("v", tiles[j])], w=[("wcv", fam, ci)])
        for mg in range(CONV_NMG):
            P.t_dma("sync", WXG4[:, mg % 2],
                    E["d_wb"].rearrange("(k p) c -> p k c", p=128
                                        )[:, :, 4096 + mg * 512:4096 + (mg + 1) * 512],
                    r=[("wcv", "x", mg)], w=[("wx", mg % 2)])
            for mm in range(4):
                m = mg * 4 + mm
                par = m % 2
                psx = PXA if par == 0 else PXB
                psc = PCA if par == 0 else PCB

                def mmx(e, mg=mg, mm=mm, psx=psx):
                    ins = None
                    for k in range(16):
                        ins = e.matmul(psx[:, 0:w], WXG4[:, mg % 2, k,
                                                        mm * 128:(mm + 1) * 128],
                                       r3(XMV[:, 0:16 * w], 16, w)[:, k, :],
                                       start=(k == 0), stop=(k == 15))
                    return ins
                P.add("tensor", mmx, r=[("wx", mg % 2), "XMV"],
                      w=[("psx", par)])

                if bi > 0:
                    P.add("vector", lambda e, m=m, par=par: e.tensor_copy(
                        XJ4[:, par, :, 0:3],
                        E["xtail"][:, m * 12:(m + 1) * 12].rearrange(
                            "p (j c) -> p j c", j=4)),
                        r=[("xtail", m)], w=[("xj0", par)])

                if CONV_LEVEL < 2:
                    continue

                def cpj01(e, m=m, par=par, psx=psx):
                    if CONV_SKIP == "act1":
                        return e.activation(XJ4[:, par, 0, 3:3 + w], psx[:, 0:w],
                                            AF.Identity)
                    e.activation(XJ4[:, par, 0, 3:3 + w], psx[:, 0:w], AF.Identity,
                                 bias=E["cb_sb"][:, m:m + 1],
                                 scale=E["cw_sb"][:, 4 * m:4 * m + 1])
                    return e.activation(XJ4[:, par, 1, 3:3 + w], psx[:, 0:w],
                                        AF.Identity,
                                        scale=E["cw_sb"][:, 4 * m + 1:4 * m + 2])
                P.add("scalar", cpj01, r=[("psx", par), "cw_sb", "cb_sb"],
                      w=[("xj1", par)], fast=True)

                def cpj23(e, m=m, par=par, psx=psx):
                    if CONV_SKIP == "dve_scr":
                        e.tensor_tensor(
                            E["sqs_cv"][:, 0:w], psx[:, 0:w],
                            E["cw_sb"][:, 4 * m + 2:4 * m + 3
                                       ].broadcast_to([128, w]), OP.mult)
                        return e.tensor_tensor(
                            E["sqs_cv"][:, 0:w], psx[:, 0:w],
                            E["cw_sb"][:, 4 * m + 3:4 * m + 4
                                       ].broadcast_to([128, w]), OP.mult)
                    if CONV_SKIP == "dve_sbsrc":
                        e.tensor_tensor(
                            XJ4[:, par, 2, 3:3 + w], E["sqs_cv"][:, 0:w],
                            E["cw_sb"][:, 4 * m + 2:4 * m + 3
                                       ].broadcast_to([128, w]), OP.mult)
                        return e.tensor_tensor(
                            XJ4[:, par, 3, 3:3 + w], E["sqs_cv"][:, 0:w],
                            E["cw_sb"][:, 4 * m + 3:4 * m + 4
                                       ].broadcast_to([128, w]), OP.mult)
                    e.tensor_tensor(
                        XJ4[:, par, 2, 3:3 + w], psx[:, 0:w],
                        E["cw_sb"][:, 4 * m + 2:4 * m + 3
                                   ].broadcast_to([128, w]), OP.mult)
                    return e.tensor_tensor(
                        XJ4[:, par, 3, 3:3 + w], psx[:, 0:w],
                        E["cw_sb"][:, 4 * m + 3:4 * m + 4
                                   ].broadcast_to([128, w]), OP.mult)
                if CONV_SKIP != "dve":
                    P.add("vector", cpj23, r=[("psx", par), ("xj1", par),
                                              "cw_sb"],
                          w=[("xj2", par)], fast=True)
                else:
                    P.add("vector", lambda e, par=par: e.engine_nop(),
                          w=[("xj2", par)])

                if CONV_LEVEL >= 3:
                    P.add("vector", lambda e, m=m, par=par: e.tensor_copy(
                        E["xtail"][:, m * 12:(m + 1) * 12].rearrange(
                            "p (j c) -> p j c", j=4), XJ4[:, par, :, w:w + 3]),
                        r=[("xj1", par), ("xj2", par)], w=[("xtail", m)])

                def trc(e, m=m, par=par, psc=psc, tiles=tiles, a=a):
                    ins = None
                    first = True
                    for wl, ti in enumerate(tiles):
                        for j in range(4):
                            col = 128 * ti + j - a
                            ins = e.matmul(psc[:, wl * 128:(wl + 1) * 128],
                                           XJ4[:, par, j, col:col + 128],
                                           E["id_f32"][:, :],
                                           start=first, stop=False,
                                           is_transpose=True,
                                           skip_group_check=True)
                            first = False
                    return ins
                rd = [("xj1", par), ("xj2", par), "id_f32"]
                if bi > 0:
                    rd.append(("xj0", par))
                if CONV_LEVEL >= 4:
                    P.add("tensor", trc, r=rd, w=[("psc", par)])

                def sil(e, m=m, par=par, psc=psc, tiles=tiles):
                    ins = None
                    for wl, ti in enumerate(tiles):
                        slot = ti % 4
                        ins = e.activation(
                            E["vt"][:, slot * DI + m * 128:slot * DI + (m + 1) * 128],
                            psc[:, wl * 128:(wl + 1) * 128], AF.Silu)
                    return ins
                if CONV_LEVEL >= 5:
                    P.add("scalar", sil, r=[("psc", par)],
                          w=[("v", ti) for ti in tiles])
        P.run()


def _p6_prefix(C, E, pairs, first, last):
    nc = C.nc
    with nc.psum_tensor(f"ps6h_{pairs[0]}", [64, DI], F32) as PH, \
         nc.sbuf_tensor(f"vscl_p{pairs[0]}", [128, DI], BF16) as VSCL:
        E["vscl"] = VSCL
        P = Phase(C, f"pre{pairs[0]}")
        if first:
            P.add("vector", lambda e: e.memset(E["h_st"][:, :], 0.0),
                  w=["h_st"])
        for p in pairs:
            ti = p + 1
            P.add("vector", lambda e, p=p, ti=ti: e.tensor_tensor(
                r3(E["vscl"][:, :], 64, 64),
                r3(E["vt"][:, (ti % 4) * DI:(ti % 4 + 1) * DI], 64, 64),
                bfree(E["DECP_T"][:, p * 64:(p + 1) * 64], 64), OP.mult),
                r=[("v", ti), "DECP_T"], w=["vscl"])
            for b in range(8):
                P.add("tensor", lambda e, p=p, b=b: e.matmul(
                    PH[:, b * 512:(b + 1) * 512],
                    E["k_tok"][:, p * 64:(p + 1) * 64],
                    E["vscl"][:, b * 512:(b + 1) * 512],
                    start=(p == pairs[0]), stop=(p == pairs[-1]),
                    skip_group_check=True),
                    r=["vscl", "k_tok"], w=[("ph", b)])
        for b in range(8):
            P.add("vector", lambda e, b=b: e.tensor_add(
                E["h_st"][:, b * 512:(b + 1) * 512],
                E["h_st"][:, b * 512:(b + 1) * 512], PH[:, b * 512:(b + 1) * 512]),
                r=[("ph", b), "h_st"], w=["h_st"])
        if last:
            P.add("vector", lambda e: e.tensor_scalar_mul(
                E["h_st"][:, :], E["h_st"][:, :], E["FLG"][:, 0:1]),
                r=["h_st", "FLG"], w=["h_st"])
        if last and "probe" in C.dbg:
            P.t_dma("gpsimd", E["d_ph0"], E["h_st"][:, :], r=["h_st"],
                    w=["prb"])
            P.t_dma("gpsimd", E["d_pdecp"], E["DECP_T"][:, :], r=["DECP_T"],
                    w=["prb"])

            def cast2(e):
                e.tensor_copy(E["vscl"][:, 0:1024].bitcast(F32),
                              E["k_tok"][:, :].bitcast(mybir.dt.uint16
                                                       ).bitcast(BF16)) \
                    if False else None
                return None
            P.t_dma("gpsimd", E["d_pkq"].bitcast(BF16)[:, 0:16 * 64],
                    E["k_tok"][:, :], r=["k_tok"], w=["prb"])
            P.t_dma("gpsimd", E["d_pkq"].bitcast(BF16)[:, 16 * 64:2 * 16 * 64],
                    E["q_tok"][:, :], r=["q_tok"], w=["prb"])
            P.t_dma("gpsimd", E["d_pvt"].bitcast(BF16)[:, 0:4 * DI],
                    E["vt"][:, :], r=[("v", 8)], w=["prb"])
            P.add("sync", lambda e: None, r=["prb"])
        P.run()


def _p6_ssd(C, E, pairs):
    nc = C.nc
    import contextlib
    st = contextlib.ExitStack()
    sfx = f"_s{pairs[0]}"
    for nm, shape, dt in [
        ("h_bf", [64, DI], BF16), ("h_bf2", [64, DI], BF16),
        ("kdiag", [128, 128], BF16), ("qstack", [128, 64], BF16),
        ("qflat_o", [64, 64], BF16), ("att", [128, NH * 128], BF16),
        ("Dmask", [128, NH * 64], BF16), ("CSB", [128, NH * 64], F32),
        ("G_sb", [128, 64], BF16), ("vscl", [128, DI], BF16),
        ("y_sl", [128, DI], BF16), ("zs_sl", [128, DI], BF16),
        ("yg_sl", [128, DI], BF16), ("YEV", [128, 512], F32),
        ("YEV2", [128, 512], F32), ("DVS", [128, 512], BF16),
        ("sqc", [128, 8], F32), ("rstd_y", [128, 1], F32),
        ("nwt", [128, 1], F32), ("sqs", [128, 512], F32),
    ]:
        E[nm] = st.enter_context(nc.sbuf_tensor(nm + sfx, shape, dt))
    with nc.psum_tensor(f"ps6sa{sfx}", [64, 512], F32) as PSa, \
         nc.psum_tensor(f"ps6sb{sfx}", [64, 512], F32) as PSb, \
         nc.psum_tensor(f"ps6Aa{sfx}", [128, 512], F32) as PAa, \
         nc.psum_tensor(f"ps6Ab{sfx}", [128, 512], F32) as PAb, \
         nc.psum_tensor(f"ps6Ba{sfx}", [128, 512], F32) as PBa, \
         nc.psum_tensor(f"ps6Bb{sfx}", [128, 512], F32) as PBb, \
         nc.psum_tensor(f"ps6T{sfx}", [128, 1024], BF16) as PT, \
         nc.psum_tensor(f"ps6G{sfx}", [128, 64], F32) as PG:
        P = Phase(C, f"ssd{pairs[0]}")
        SSW = {8: [("d", 0), ("d", 1), ("d", 2)],
               11: [("d", 7), ("o", 0), ("o", 1), ("o", 2)],
               15: [("o", 3)]}
        for j, (fam, ci) in enumerate(SSW.get(pairs[0], [])):
            st_, dt_ = {"d": (E["d_w_dnT"], E["d_wdb"]),
                        "o": (E["d_w_out"], E["d_wob"])}[fam]
            sl = slice(ci * 1024, (ci + 1) * 1024)
            P.t_dma("gpsimd", dt_[sl, :], st_[sl, :],
                    r=[("ygd", pairs[min(j, len(pairs) - 1)] - 8)],
                    w=[("wcv", fam, ci)])
        P.add("vector", lambda e: e.memset(E["att"][:, :], 0.0), w=["att"])
        P.add("vector", lambda e: e.memset(E["kdiag"][:, :], 0.0), w=["kdiag"])
        for p in pairs:
            ti = p + 1
            slot = ti % 4
            vsl = E["vt"][:, slot * DI:(slot + 1) * DI]

            P.t_dma("gpsimd", E["CSB"][0:64, :],
                    E["d_csd"][2 * p:2 * p + 1, :].partition_broadcast(64),
                    w=["CSBa"])
            P.t_dma("gpsimd", E["CSB"][64:128, :],
                    E["d_csd"][2 * p + 1:2 * p + 2, :].partition_broadcast(64),
                    w=["CSBb"])
            P.t_dma("sync", E["zs_sl"][:, 0:DI], E["d_zs"][p - 8],
                    r=[("zsd", zb2, p - 8) for zb2 in range(8)], w=["zs_sl"])

            def xm(e, p=p):
                e.scalar_tensor_tensor(
                    r3(E["CSB"][:, :], 64, 64), r3(E["CSB"][:, :], 64, 64), -1.0,
                    bfree(E["CS_T"][:, p * 64:(p + 1) * 64], 64),
                    OP.mult, OP.add)
                return e.tensor_scalar_max(E["CSB"][:, :], E["CSB"][:, :], 0.0)
            P.add("vector", xm, r=["CSBa", "CSBb", "CS_T"],
                  w=["CSBa", "CSBb", "Xm"])
            P.add("scalar", lambda e: e.activation(
                E["Dmask"][:, :], E["CSB"][:, :], AF.Exp, scale=-1.0),
                r=["Xm"], w=["Dmask"])

            def qktr(e, p=p):
                e.transpose(PT[0:64, 0:128], E["q_tok"][:, p * 64:(p + 1) * 64],
                            E["id_bf"][:, :])
                e.transpose(PT[64:128, 128:192],
                            E["q_tok"][64:128, p * 64:(p + 1) * 64],
                            E["id_bf"][64:128, 64:128], tile_position=(64, 64))
                e.transpose(PT[0:64, 192:256],
                            E["k_tok"][0:64, p * 64:(p + 1) * 64],
                            E["id_bf"][0:64, 0:64])
                return e.transpose(PT[64:128, 256:320],
                                   E["k_tok"][64:128, p * 64:(p + 1) * 64],
                                   E["id_bf"][64:128, 64:128],
                                   tile_position=(64, 64))
            P.add("tensor", qktr, r=["q_tok", "k_tok", "id_bf"], w=["psT"])

            def qkev(e):
                e.tensor_copy(E["qstack"][0:64, :], PT[0:64, 0:64])
                e.tensor_copy(E["qflat_o"][:, :], PT[0:64, 64:128])
                e.tensor_copy(E["qstack"][64:128, :], PT[64:128, 128:192])
                e.tensor_copy(E["kdiag"][0:64, 0:64], PT[0:64, 192:256])
                return e.tensor_copy(E["kdiag"][64:128, 64:128],
                                     PT[64:128, 256:320])
            P.add("vector", qkev, r=["psT"], w=["qstack", "qflat_o", "kdiag"],
                  fast=True)

            P.add("tensor", lambda e: e.matmul(PG[:, :], E["kdiag"][:, :],
                                               E["qstack"][:, :], start=True,
                                               stop=True),
                  r=["kdiag", "qstack"], w=["psG"])
            P.add("vector", lambda e: e.tensor_tensor(
                E["G_sb"][:, :], PG[:, :], E["causal"][:, :], OP.mult),
                r=["psG", "causal"], w=["G_sb"])

            def attb(e):
                A3 = E["att"].rearrange("p (h s) -> p h s", h=64)
                e.tensor_tensor(
                    A3[0:64, :, 0:64],
                    E["G_sb"][0:64, :].unsqueeze(1).broadcast_to([64, 64, 64]),
                    r3(E["Dmask"][0:64, :], 64, 64), OP.mult)
                return e.tensor_tensor(
                    A3[64:128, :, 64:128],
                    E["G_sb"][64:128, :].unsqueeze(1).broadcast_to([64, 64, 64]),
                    r3(E["Dmask"][64:128, :], 64, 64), OP.mult)
            P.add("vector", attb, r=["G_sb", "Dmask"], w=["att"],
                  fast=True)

            P.add("vector", lambda e, vsl=vsl, p=p: e.tensor_tensor(
                r3(E["vscl"][:, :], 64, 64), r3(vsl, 64, 64),
                bfree(E["DEC_T"][:, p * 64:(p + 1) * 64], 64), OP.mult),
                r=[("v", ti), "DEC_T"], w=["vscl"])

            P.add("vector", lambda e: e.tensor_copy(E["h_bf"][:, :],
                                                    E["h_st"][:, :]),
                  r=["h_st"], w=["h_bf"])
            P.add("vector", lambda e, p=p: e.tensor_tensor(
                r3(E["h_st"][:, :], 64, 64), r3(E["h_st"][:, :], 64, 64),
                bfree(E["DBC64"][0:64, (2 * p) * 64:(2 * p + 1) * 64], 64),
                OP.mult), r=["h_st", "DBC64"], w=["h_st"])
            for b in range(8):
                P.add("tensor", lambda e, p=p, b=b: e.matmul(
                    (PSa if b % 2 == 0 else PSb)[:, :],
                    E["k_tok"][0:64, p * 64:(p + 1) * 64],
                    E["vscl"][0:64, b * 512:(b + 1) * 512],
                    start=True, stop=True),
                    r=["vscl", "k_tok", ("pssr", b % 2)], w=[("pss", b % 2)])
                P.add("vector", lambda e, b=b: e.tensor_add(
                    E["h_st"][:, b * 512:(b + 1) * 512],
                    E["h_st"][:, b * 512:(b + 1) * 512],
                    (PSa if b % 2 == 0 else PSb)[:, :]),
                    r=[("pss", b % 2), "h_st"], w=["h_st", ("pssr", b % 2)])
            P.add("vector", lambda e: e.tensor_copy(E["h_bf2"][:, :],
                                                    E["h_st"][:, :]),
                  r=["h_st"], w=["h_bf2"])
            P.add("vector", lambda e, p=p: e.tensor_tensor(
                r3(E["h_st"][:, :], 64, 64), r3(E["h_st"][:, :], 64, 64),
                bfree(E["DBC64"][0:64, (2 * p + 1) * 64:(2 * p + 2) * 64], 64),
                OP.mult), r=["h_st", "DBC64"], w=["h_st"])
            for b in range(8):
                P.add("tensor", lambda e, p=p, b=b: e.matmul(
                    (PSa if b % 2 == 0 else PSb)[:, :],
                    E["k_tok"][64:128, p * 64:(p + 1) * 64],
                    E["vscl"][64:128, b * 512:(b + 1) * 512],
                    start=True, stop=True, tile_position=(64, 0)),
                    r=["vscl", "k_tok", ("pssr", b % 2)], w=[("pss", b % 2)])
                P.add("vector", lambda e, b=b: e.tensor_add(
                    E["h_st"][:, b * 512:(b + 1) * 512],
                    E["h_st"][:, b * 512:(b + 1) * 512],
                    (PSa if b % 2 == 0 else PSb)[:, :]),
                    r=[("pss", b % 2), "h_st"], w=["h_st", ("pssr", b % 2)])
            for b in range(8):
                PA = PAa if b % 2 == 0 else PAb
                PB = PBa if b % 2 == 0 else PBb
                P.add("tensor", lambda e, b=b, PA=PA: e.matmul(
                    PA[0:64, :], E["qstack"][0:64, :],
                    E["h_bf"][:, b * 512:(b + 1) * 512],
                    start=True, stop=False, skip_group_check=True),
                    r=["h_bf", "qstack", ("psar", b % 2)], w=[("psa", b % 2)])
                P.add("tensor", lambda e, b=b, PA=PA: e.matmul(
                    PA[64:128, :], E["qflat_o"][:, :],
                    E["h_bf2"][:, b * 512:(b + 1) * 512],
                    start=True, stop=True, tile_position=(0, 64),
                    skip_group_check=True),
                    r=["h_bf2", "qflat_o"], w=[("psa", b % 2)])

                def intra(e, b=b, vsl=vsl, PB=PB):
                    ins = None
                    for hh in range(8):
                        h = b * 8 + hh
                        ins = e.matmul(
                            PB[:, hh * 64:(hh + 1) * 64],
                            E["att"][:, h * 128:(h + 1) * 128],
                            vsl[:, h * 64:(h + 1) * 64],
                            start=(hh == 0), stop=(hh == 7),
                            skip_group_check=True)
                    return ins
                P.add("tensor", intra, r=["att", ("v", ti), ("psbr", b % 2)],
                      w=[("psb", b % 2)])

                def yev(e, p=p, b=b, vsl=vsl, PA=PA, PB=PB):
                    e.tensor_tensor(
                        r3(E["YEV"][:, :], 8, 64), r3(PA[:, :], 8, 64),
                        bfree(E["ECS_T"][:, p * 64 + b * 8:p * 64 + b * 8 + 8], 64),
                        OP.mult)
                    e.tensor_add(E["YEV2"][:, :], E["YEV"][:, :], PB[:, :])
                    e.tensor_tensor(r3(E["DVS"][:, :], 8, 64),
                                    r3(vsl[:, b * 512:(b + 1) * 512], 8, 64),
                                    bfree(E["DP"][:, b * 8:b * 8 + 8], 64),
                                    OP.mult)
                    return e.tensor_add(E["y_sl"][:, b * 512:(b + 1) * 512],
                                        E["YEV2"][:, :], E["DVS"][:, :])
                P.add("vector", yev,
                      r=[("psa", b % 2), ("psb", b % 2), ("v", ti), "ECS_T",
                         "DP"],
                      w=[("ysl", b), ("psar", b % 2), ("psbr", b % 2)])
                P.add("scalar", lambda e, b=b: e.activation(
                    E["sqs"][:, :], E["y_sl"][:, b * 512:(b + 1) * 512],
                    AF.Square, accum_out=E["sqc"][:, b:b + 1]),
                    r=[("ysl", b)], w=[("sqc", b)])

            def rsy1(e):
                e.reduce_sum(E["nwt"][:, :], E["sqc"][:, :], axis=AX.X)
                return e.tensor_scalar(E["nwt"][:, :], E["nwt"][:, :],
                                       1.0 / DI, 1e-6, OP.mult, OP.add)
            P.add("vector", rsy1, r=[("sqc", b) for b in range(8)], w=["nwt"])
            P.add("scalar", lambda e: e.activation(E["rstd_y"][:, :],
                                                   E["nwt"][:, :], AF.Sqrt),
                  r=["nwt"], w=["rsq"])

            def rsy2(e):
                RS, T = E["rstd_y"], E["nwt"]
                R0 = E["sqc"][:, 1:2]
                e.reciprocal(R0, RS[:, :])
                e.tensor_mul(E["sqc"][:, 0:1], R0, R0)
                e.tensor_mul(E["sqc"][:, 0:1], E["sqc"][:, 0:1], T[:, :])
                e.tensor_scalar(E["sqc"][:, 0:1], E["sqc"][:, 0:1], -0.5, 1.5,
                                OP.mult, OP.add)
                return e.tensor_mul(RS[:, :], R0, E["sqc"][:, 0:1])
            P.add("vector", rsy2, r=["rsq"], w=["rstd_y", ("sqc", 0),
                                               ("sqc", 1)])

            if p == 8 and "probe2" in C.dbg:
                for nm, src_ap, rt in [
                        ("py", E["y_sl"][:, :].bitcast(mybir.dt.uint16), None),
                        ("patt", E["att"][:, :].bitcast(mybir.dt.uint16), "att"),
                        ("pG", E["G_sb"][:, :].bitcast(mybir.dt.uint16), "G_sb"),
                        ("pD", E["Dmask"][:, :].bitcast(mybir.dt.uint16), "Dmask"),
                        ("phb", E["h_bf"][:, :].bitcast(mybir.dt.uint16), "h_bf"),
                        ("pqs", E["qstack"][:, :].bitcast(mybir.dt.uint16), "qstack"),
                        ("pkd", E["kdiag"][:, :].bitcast(mybir.dt.uint16), "kdiag"),
                        ("pvs", E["vscl"][:, :].bitcast(mybir.dt.uint16), "vscl")]:
                    rr = [rt] if rt else [("ysl", b) for b in range(8)]
                    P.t_dma("gpsimd",
                            E["d_" + nm].bitcast(mybir.dt.uint16
                                                 )[:, 0:src_ap.shape[1]],
                            src_ap, r=rr, w=["prb2"])
                P.add("sync", lambda e: None, r=["prb2"])
            P.add("vector", lambda e: e.scalar_tensor_tensor(
                E["yg_sl"][:, 0:DI], E["y_sl"][:, :], E["rstd_y"][:, 0:1],
                E["zs_sl"][:, 0:DI], OP.mult, OP.mult),
                r=["rstd_y", "zs_sl"] + [("ysl", b) for b in range(8)],
                w=["yg_sl"])
            P.t_dma("gpsimd", E["d_yg"][p - 8], E["yg_sl"][:, 0:DI],
                    r=["yg_sl"], w=[("ygd", p - 8)])
        P.add("sync", lambda e: None, r=[("ygd", p2 - 8) for p2 in pairs])
        P.run()
    st.close()


def _p6_main(C, E):
    _p6_conv_blk(C, E, 0)
    _p6_prefix(C, E, (0, 1, 2), True, False)
    _p6_conv_blk(C, E, 1)
    _p6_prefix(C, E, (3, 4, 5, 6), False, False)
    _p6_conv_blk(C, E, 2)
    _p6_prefix(C, E, (7,), False, True)
    _p6_ssd(C, E, (8, 9, 10))
    _p6_conv_blk(C, E, 3)
    _p6_ssd(C, E, (11, 12, 13, 14))
    _p6_conv_blk(C, E, 4)
    _p6_ssd(C, E, (15,))


def _p7_ffn_out(C, E):
    nc = C.nc
    with nc.sbuf_tensor("FFN_O", [128, 16 * 512], F32) as FFN_O:
        for tg in range(2):
            with nc.sbuf_tensor(f"HID_{tg}", [128, 64 * 512], BF16) as HID:
                E["HID"] = HID
                _p7_up(C, E, tg)
                _p7_dn(C, E, tg, FFN_O)
            _p7_out(C, E, tg, FFN_O)


def _p7_up(C, E, tg):
    nc = C.nc
    with nc.sbuf_tensor(f"XMV7_{tg}", [128, 16 * 512], BF16) as XMV, \
         nc.sbuf_tensor(f"SWUP_{tg}", [128, 2 * 16 * 512], BF16) as SWUP, \
         nc.psum_tensor(f"ps7ua_{tg}", [128, 512], F32) as PUA, \
         nc.psum_tensor(f"ps7ub_{tg}", [128, 512], F32) as PUB:
        P = Phase(C, f"up{tg}")
        SW4 = SWUP.rearrange("p (r k c) -> p r k c", r=2, k=16)
        HID3 = r3(E["HID"][:, :], 64, 512)
        P.t_dma("sync", r3(XMV[:, :], 16, 512),
                E["d_xnf"].rearrange("k d t -> d k t"
                                     )[:, :, 1152 + tg * 512:1152 + (tg + 1) * 512],
                r=["xnf_ready"], w=["XMV"])
        for mg in range(16):
            P.t_dma("sync", SW4[:, mg % 2],
                    E["d_wub"].rearrange("(k p) c -> p k c", p=128
                                         )[:, :, mg * 512:(mg + 1) * 512],
                    r=[("swu", mg % 2)] + [("wcv", "u", i4) for i4 in range(8)],
                    w=[("swraw", mg % 2)])
            P.add("scalar", lambda e, mg=mg: e.activation(
                SWUP.rearrange("p (r c) -> p r c", r=2)[:, mg % 2],
                SWUP.rearrange("p (r c) -> p r c", r=2)[:, mg % 2], AF.Sign),
                r=[("swraw", mg % 2)], w=[("sw", mg % 2)])
            for mm in range(4):
                m = mg * 4 + mm
                par = m % 2
                ps = PUA if par == 0 else PUB

                def mmu(e, mg=mg, mm=mm, ps=ps):
                    ins = None
                    for k in range(16):
                        ins = e.matmul(ps[:, :],
                                       SW4[:, mg % 2, k, mm * 128:(mm + 1) * 128],
                                       r3(XMV[:, :], 16, 512)[:, k, :],
                                       start=(k == 0), stop=(k == 15))
                    return ins
                P.add("tensor", mmu,
                      r=[("sw", mg % 2), "XMV", ("pur", par)],
                      w=[("pu", par)] + ([("swu", mg % 2)] if mm == 3 else []))
                P.add("vector", lambda e, m=m, ps=ps: e.tensor_scalar(
                    HID3[:, m, :], ps[:, :], E["usc_sb"][:, m:m + 1], 0.0,
                    OP.mult, OP.max),
                    r=[("pu", par), "usc_sb"], w=[("hid", m), ("pur", par)])
        P.run()


def _p7_dn(C, E, tg, FFN_O):
    nc = C.nc
    with nc.sbuf_tensor(f"SWDN_{tg}", [128, 2 * 32 * 512], BF16) as SWDN, \
         nc.psum_tensor(f"ps7d0_{tg}", [128, 512], F32) as PD0, \
         nc.psum_tensor(f"ps7d1_{tg}", [128, 512], F32) as PD1, \
         nc.psum_tensor(f"ps7d2_{tg}", [128, 512], F32) as PD2, \
         nc.psum_tensor(f"ps7d3_{tg}", [128, 512], F32) as PD3:
        PDS = [PD0, PD1, PD2, PD3]
        P = Phase(C, f"dn{tg}")
        SW4 = SWDN.rearrange("p (r k c) -> p r k c", r=2, k=32)
        HID3 = r3(E["HID"][:, :], 64, 512)
        for dmg in range(4):
            for kh in range(2):
                half = dmg * 2 + kh
                P.t_dma("sync", SW4[:, half % 2],
                        E["d_wdb"].rearrange("(k p) c -> p k c", p=128
                                             )[:, kh * 32:(kh + 1) * 32,
                                               dmg * 512:(dmg + 1) * 512],
                        r=[("swdu", half % 2)] +
                          [("wcv", "d", kh * 4 + j4) for j4 in range(4)],
                        w=[("swdraw", half % 2)])
                P.add("scalar", lambda e, half=half: e.activation(
                    SWDN.rearrange("p (r c) -> p r c", r=2)[:, half % 2],
                    SWDN.rearrange("p (r c) -> p r c", r=2)[:, half % 2],
                    AF.Sign), r=[("swdraw", half % 2)], w=[("swd", half % 2)])
                for dmm in range(4):
                    def mmd(e, kh=kh, dmm=dmm, dmg=dmg, half=half):
                        ins = None
                        for kk in range(32):
                            k = kh * 32 + kk
                            ins = e.matmul(
                                PDS[dmm][:, :],
                                SW4[:, half % 2, kk, dmm * 128:(dmm + 1) * 128],
                                HID3[:, k, :],
                                start=(kh == 0 and kk == 0),
                                stop=(kh == 1 and kk == 31))
                        return ins
                    P.add("tensor", mmd,
                          r=[("swd", half % 2), ("pdr", dmm)] +
                            [("hid", k2) for k2 in range(kh * 32, kh * 32 + 32)],
                          w=[("pd", dmm)] + ([("swdu", half % 2)]
                                             if dmm == 3 else []))
            for dmm in range(4):
                dmc = dmg * 4 + dmm
                P.add("vector", lambda e, dmm=dmm, dmc=dmc: e.tensor_scalar(
                    FFN_O[:, dmc * 512:(dmc + 1) * 512], PDS[dmm][:, :],
                    E["dsc_sb"][:, dmc:dmc + 1], None, OP.mult),
                    r=[("pd", dmm), "dsc_sb"], w=[("ffo", dmc), ("pdr", dmm)])
        P.run()


def _p7_out(C, E, tg, FFN_O):
    nc = C.nc
    with nc.sbuf_tensor(f"YGL_{tg}", [128, 2 * DI], BF16) as YGL, \
         nc.sbuf_tensor(f"YGT_{tg}", [128, 32 * 512], BF16) as YGT, \
         nc.sbuf_tensor(f"WOS_{tg}", [128, 32 * 512], BF16) as WOS, \
         nc.sbuf_tensor(f"UTS_{tg}", [128, 4 * 2048], F32) as UTS, \
         nc.sbuf_tensor(f"OSB_{tg}", [128, 4 * 2048], F32) as OSB, \
         nc.sbuf_tensor(f"SSL_{tg}", [128, 512], F32) as SSL, \
         nc.psum_tensor(f"ps7oa_{tg}", [128, 512], F32) as POA, \
         nc.psum_tensor(f"ps7ob_{tg}", [128, 512], F32) as POB, \
         nc.psum_tensor(f"ps7ty_{tg}", [128, 1024], BF16) as PTY, \
         nc.psum_tensor(f"ps7ka_{tg}", [128, 512], F32) as PKA, \
         nc.psum_tensor(f"ps7kb_{tg}", [128, 512], F32) as PKB:
        P = Phase(C, f"out{tg}")
        YGT3 = r3(YGT[:, :], 32, 512)
        WOS3 = r3(WOS[:, :], 32, 512)
        for ts in range(4):
            row = 1152 + tg * 512 + ts * 128
            P.t_dma("sync", UTS[:, ts * 2048:(ts + 1) * 2048],
                    E["d_u_tok"][row:row + 128, :], w=[("uts", ts)])
        for i in range(4):
            P.t_dma("sync", YGL[:, (i % 2) * DI:(i % 2 + 1) * DI],
                    E["d_yg"][tg * 4 + i], r=[("ygl", i % 2), ("ygd", tg * 4 + i)],
                    w=[("ygld", i % 2)])
            for half in range(4):
                def ytr(e, i=i, half=half):
                    ins = None
                    for kk in range(8):
                        k = half * 8 + kk
                        ins = e.transpose(
                            PTY[:, kk * 128:(kk + 1) * 128],
                            YGL[:, (i % 2) * DI + k * 128:(i % 2) * DI + (k + 1) * 128],
                            E["id_bf"][:, :])
                    return ins
                P.add("tensor", ytr, r=[("ygld", i % 2), "id_bf", ("ptyr",)],
                      w=[("pty",)])

                def yev(e, i=i, half=half):
                    return e.tensor_copy(
                        YGT3[:, half * 8:(half + 1) * 8, i * 128:(i + 1) * 128],
                        r3(PTY[:, :], 8, 128))
                P.add("vector", yev, r=[("pty",)],
                      w=[("ygt", i, half), ("ptyr",)] +
                        ([("ygl", i % 2)] if half == 3 else []))
        for mg in range(4):
            P.t_dma("sync", WOS3,
                    E["d_wob"].rearrange("(k p) c -> p k c", p=128
                                         )[:, :, mg * 512:(mg + 1) * 512],
                    r=[("wosu",)] + [("wcv", "o", i4) for i4 in range(4)],
                    w=[("wos_raw",)])

            P.add("vector", lambda e: e.tensor_tensor(
                WOS3, WOS3, bfree(E["nsc_sb"][:, 0:32], 512), OP.mult),
                r=[("wos_raw",), "nsc_sb"], w=[("wos",)])
            for m in range(4):
                dmc = mg * 4 + m
                par = m % 2
                po = POA if par == 0 else POB

                def mmo(e, m=m, po=po):
                    ins = None
                    for k in range(32):
                        ins = e.matmul(po[:, :], WOS3[:, k, m * 128:(m + 1) * 128],
                                       YGT3[:, k, :], start=(k == 0),
                                       stop=(k == 31))
                    return ins
                P.add("tensor", mmo,
                      r=[("wos",), ("por", par)] +
                        [("ygt", i, half) for i in range(4) for half in range(4)],
                      w=[("po", par)] + ([("wosu",)] if m == 3 else []))
                P.add("vector", lambda e, po=po, dmc=dmc: e.tensor_add(
                    SSL[:, :], po[:, :], FFN_O[:, dmc * 512:(dmc + 1) * 512]),
                    r=[("po", par), ("ffo", dmc), ("sslr",)],
                    w=[("ssl",), ("por", par)])

                def ktr(e, dmc=dmc, par=par):
                    ins = None
                    pk = PKA if par == 0 else PKB
                    for ts in range(4):
                        ins = e.transpose(pk[:, ts * 128:(ts + 1) * 128],
                                          SSL[:, ts * 128:(ts + 1) * 128],
                                          E["id_f32"][:, :])
                    return ins
                P.add("tensor", ktr, r=[("ssl",), "id_f32", ("pkr", par)],
                      w=[("pk", par), ("sslr",)])

                def radd(e, dmc=dmc, par=par):
                    ins = None
                    pk = PKA if par == 0 else PKB
                    for ts in range(4):
                        ins = e.tensor_add(
                            OSB[:, ts * 2048 + dmc * 128:ts * 2048 + (dmc + 1) * 128],
                            pk[:, ts * 128:(ts + 1) * 128],
                            UTS[:, ts * 2048 + dmc * 128:ts * 2048 + (dmc + 1) * 128])
                    return ins
                P.add("vector", radd,
                      r=[("pk", par)] + [("uts", ts) for ts in range(4)],
                      w=[("osb", dmc), ("pkr", par)], fast=True)
        for ts in range(4):
            P.t_dma("gpsimd", E["d_out"][tg * 512 + ts * 128:tg * 512 + (ts + 1) * 128, :],
                    OSB[:, ts * 2048:(ts + 1) * 2048],
                    r=[("osb", dmc2) for dmc2 in range(16)], w=["outw"])
        P.add("sync", lambda e: None, r=["outw"])
        P.run()


_CACHED = {}


def _stage_inputs(inputs):
    u = np.ascontiguousarray(np.asarray(inputs["L0_attn_u"], dtype=np.float32))
    w_in = np.ascontiguousarray(np.asarray(inputs["L0_attn_in_proj_w"], np.float32))
    convw = np.asarray(inputs["L0_attn_conv_weight"], np.float32)
    convb = np.asarray(inputs["L0_attn_conv_bias"], np.float32)
    A_log = np.asarray(inputs["L0_attn_A_log"], np.float32)
    dt_bias = np.asarray(inputs["L0_attn_dt_bias"], np.float32)
    D_param = np.asarray(inputs["L0_attn_D_param"], np.float32)
    norm_scale = np.asarray(inputs["L0_attn_norm_scale"], np.float32)
    w_out = np.ascontiguousarray(np.asarray(inputs["L0_attn_out_proj_w"], np.float32))
    w_upT = np.ascontiguousarray(
        np.asarray(inputs["L0_ffn_up_weight"], np.float32).T)
    w_dnT = np.ascontiguousarray(
        np.asarray(inputs["L0_ffn_down_weight"], np.float32).T)
    up_scale = np.asarray(inputs["L0_ffn_up_scale"], np.float32)
    down_scale = np.asarray(inputs["L0_ffn_down_scale"], np.float32)
    out_scale = np.asarray(inputs["L0_ffn_output_scale"], np.float32)

    cw = np.ascontiguousarray(
        convw.T.reshape(32, 128, 4).transpose(1, 0, 2).reshape(128, 128))
    cb = np.ascontiguousarray(convb.reshape(32, 128).T)
    nsc = np.ascontiguousarray(norm_scale.reshape(32, 128).T)
    usc = np.ascontiguousarray(up_scale.reshape(64, 128).T)
    dsc = np.ascontiguousarray(down_scale.reshape(16, 128).T)
    vecs = np.ascontiguousarray(np.stack([A_log, dt_bias, D_param]))
    osc = out_scale.reshape(1, 1)

    shared = dict(w_in=w_in, w_upT=w_upT, w_dnT=w_dnT, w_out=w_out,
                  cw=cw, cb=cb, nsc=nsc, usc=usc, dsc=dsc, vecs=vecs, osc=osc)
    zeros = np.zeros((1024, DM), np.float32)
    halo = np.zeros((128, DM), np.float32)
    in_maps = []
    for core in range(8):
        b, half = core // 2, core % 2
        prefix = u[b, 0:1024] if half == 1 else zeros
        own = u[b, half * 1024:(half + 1) * 1024]
        u_tok = np.ascontiguousarray(np.concatenate([halo, prefix, own], 0))
        m = dict(shared)
        m["u_tok"] = u_tok
        m["flag"] = np.array([[1.0 if half == 1 else 0.0]], np.float32)
        in_maps.append(m)
    return in_maps


def kernel(**inputs):
    if "nc" not in _CACHED:
        _CACHED["nc"] = build_program()
    nc = _CACHED["nc"]
    in_maps = _stage_inputs(inputs)
    res = run_bass_kernel_spmd(nc, in_maps, core_ids=list(range(8)))
    out = np.empty((4, 2048, DM), np.float32)
    for core in range(8):
        b, half = core // 2, core % 2
        out[b, half * 1024:(half + 1) * 1024] = res.results[core]["out"]
    return out


# version 17 (final)
# speedup vs baseline: 1.0481x; 1.0218x over previous
import numpy as np
import ml_dtypes
import concourse.bass as bass
import concourse.mybir as mybir
from concourse.bass_utils import run_bass_kernel_spmd

F32 = mybir.dt.float32
BF16 = mybir.dt.bfloat16
AF = mybir.ActivationFunctionType
OP = mybir.AluOpType
AX = mybir.AxisListType
BF16NP = ml_dtypes.bfloat16

NT = 17
TTOT = NT * 128
DM = 2048
DI = 4096
DH = 8192
NH = 64

ENGINES = ("sync", "scalar", "vector", "tensor", "gpsimd")


def bfree(ap2d, inner):
    return ap2d.unsqueeze(2).broadcast_to([ap2d.shape[0], ap2d.shape[1], inner])


def r3(ap2d, a, b):
    return ap2d.rearrange("p (a b) -> p a b", a=a, b=b)


class Task:
    __slots__ = ("eng", "emit", "reads", "writes", "dma", "seq", "done",
                 "waits", "prewait", "fast")

    def __init__(self, eng, emit, reads, writes, dma, fast=False):
        self.eng, self.emit = eng, emit
        self.reads, self.writes = list(reads), list(writes)
        self.dma = dma
        self.fast = fast


class Phase:

    def __init__(self, C, name):
        self.C = C
        self.name = name
        self.tasks = []

    def add(self, eng, emit, r=(), w=(), dma=False, fast=False):
        if emit.__name__ == "<lambda>" and eng in ("vector", "scalar"):
            fast = True
        self.tasks.append(Task(eng, emit, r, w, dma, fast))

    def t_dma(self, eng, out_ap, in_ap, r=(), w=(), **kw):
        def emit(e):
            return e.dma_start(out_ap, in_ap, **kw)
        self.add(eng, emit, r=r, w=w, dma=True)

    NDMA = 6

    def run(self):
        C = self.C
        nc = C.nc
        cnt = {}
        last_writer = C.last_writer
        readers = C.readers
        for t in self.tasks:
            if t.dma:
                k = C.dma_idx.get(t.eng, 0)
                C.dma_idx[t.eng] = k + 1
                slot = k % self.NDMA
                use = k // self.NDMA
                sname = f"{t.eng}D{slot}"
                t.prewait = (sname, 16 * use) if use > 0 else None
                t.done = (sname, 16 * (use + 1))
            else:
                key = (t.eng, t.dma)
                base = C.semval(self._sname(t.eng, t.dma))
                cnt[key] = cnt.get(key, 0) + 1
                t.seq = cnt[key]
                t.done = (self._sname(t.eng, t.dma), base + t.seq)
                t.prewait = None
        for t in self.tasks:
            waits = {}

            def need(dep, raw=False):
                if dep is None or dep is t:
                    return
                if (raw and dep.eng == t.eng and not dep.dma and not t.dma):
                    if dep.eng == "tensor":
                        return
                    if dep.eng in ("vector", "scalar") and not dep.fast:
                        return
                sname, val = dep.done
                waits[sname] = max(waits.get(sname, 0), val)

            for tok in t.reads:
                need(last_writer.get(tok), raw=True)
            for tok in t.writes:
                need(last_writer.get(tok))
                for rd in readers.get(tok, ()):
                    need(rd)
            t.waits = waits
            for tok in t.reads:
                readers.setdefault(tok, []).append(t)
            for tok in t.writes:
                last_writer[tok] = t
                readers[tok] = []
        with nc.Block(self.name) as blk:
            for eng in ENGINES:
                mytasks = [t for t in self.tasks if t.eng == eng]
                if not mytasks:
                    continue

                def body(e, mytasks=mytasks, eng=eng):
                    hwm = {}
                    for t in mytasks:
                        allw = dict(t.waits)
                        if t.prewait is not None:
                            sn, v = t.prewait
                            allw[sn] = max(allw.get(sn, 0), v)
                        for sname, val in sorted(allw.items()):
                            if hwm.get(sname, -1) >= val:
                                continue
                            e.wait_ge(C.sem(sname), val)
                            hwm[sname] = val
                        safe = (not t.dma) and (
                            C.race_detect or
                            (eng in ("vector", "scalar") and not t.fast))
                        ee = SafeEng(e, C, eng) if safe else e
                        ins = t.emit(ee)
                        sname, val = t.done
                        inc = 16 if t.dma else 1
                        if ins is None or safe:
                            e.sem_inc(C.sem(sname), inc)
                        else:
                            ins.then_inc(C.sem(sname), inc)
                        C.setsem(sname, val)

                getattr(blk, eng)(body)

    @staticmethod
    def _sname(eng, dma):
        return f"{eng}{'D' if dma else ''}"


class SafeEng:

    def __init__(self, eng, C, name):
        self._e = eng
        self._C = C
        self._n = name

    def __getattr__(self, name):
        attr = getattr(self._e, name)
        if not callable(attr) or name in ("wait_ge", "sem_inc", "nop",
                                          "dma_start", "load_library"):
            return attr

        def wrap(*a, **k):
            r = attr(*a, **k)
            C = self._C
            sn = "safe_" + self._n
            r.then_inc(C.sem(sn), 1)
            v = C.inc_safe(sn)
            self._e.wait_ge(C.sem(sn), v)
            return r
        return wrap


class Ctx:
    def __init__(self, dbg=(), race_detect=False):
        self.race_detect = race_detect
        self.dbg = set(dbg)
        self.nc = bass.Bass("TRN2", target_bir_lowering=False, debug=False,
                            detect_race_conditions=race_detect)
        self._sems = {}
        self._semn = {}
        self.dma_idx = {}
        self._safe = {}
        self.last_writer = {}
        self.readers = {}

    def inc_safe(self, name):
        self._safe[name] = self._safe.get(name, 0) + 1
        return self._safe[name]

    def sem(self, name):
        if name not in self._sems:
            self._sems[name] = self.nc.alloc_semaphore(name)
            self._semn[name] = 0
        return self._sems[name]

    def semval(self, name):
        self.sem(name)
        return self._semn[name]

    def setsem(self, name, v):
        self._semn[name] = max(self._semn[name], v)


def build_program(dbg=(), race_detect=False):
    C = Ctx(dbg, race_detect)
    nc = C.nc
    E = {}

    def dram(name, shape, dtype, kind="Internal"):
        k = "ExternalOutput" if name in C.dbg else kind
        E["d_" + name] = nc.dram_tensor(name, shape, dtype, kind=k).ap()

    for name, shape in [
        ("u_tok", [TTOT, DM]), ("w_in", [DM, 8384]),
        ("w_upT", [DM, DH]), ("w_dnT", [DH, DM]), ("w_out", [DI, DM]),
        ("cw", [128, 128]), ("cb", [128, 32]), ("nsc", [128, 32]),
        ("usc", [128, 64]), ("dsc", [128, 16]), ("vecs", [3, 64]),
        ("osc", [1, 1]), ("flag", [1, 1]),
    ]:
        E["d_" + name] = nc.dram_tensor(name, shape, F32, kind="ExternalInput").ap()
    E["d_out"] = nc.dram_tensor("out", [1024, DM], F32, kind="ExternalOutput").ap()

    dram("xnf", [16, 128, TTOT], BF16)
    dram("wb", [DM, 8384], BF16)
    dram("wub", [DM, DH], BF16)
    dram("wdb", [DH, DM], BF16)
    dram("wob", [DI, DM], BF16)
    dram("yg", [8, 128, DI], BF16)
    dram("zs", [8, 128, DI], BF16)
    dram("csd", [32, NH * 64], F32)
    dram("trow", [1, 2048], F32)
    dram("nrow", [1, 64], F32)
    if "stats" in C.dbg:
        E["d_pstat"] = nc.dram_tensor("pstat", [128, 6 * NT], F32,
                                      kind="ExternalOutput").ap()
    if "probe2" in C.dbg:
        for nm, shape in [("py", [128, DI]), ("patt", [128, NH * 128]),
                          ("pG", [128, 64]), ("pD", [128, NH * 64]),
                          ("phb", [64, DI]), ("pqs", [128, 64]),
                          ("pkd", [128, 128]), ("pvs", [128, DI])]:
            E["d_" + nm] = nc.dram_tensor(nm, shape, F32,
                                          kind="ExternalOutput").ap()
    if "probe" in C.dbg:
        E["d_ph0"] = nc.dram_tensor("ph0", [64, DI], F32,
                                    kind="ExternalOutput").ap()
        E["d_pdecp"] = nc.dram_tensor("pdecp", [128, 8 * 64], F32,
                                      kind="ExternalOutput").ap()
        E["d_pkq"] = nc.dram_tensor("pkq", [128, 2 * 16 * 64], F32,
                                    kind="ExternalOutput").ap()
        E["d_pvt"] = nc.dram_tensor("pvt", [128, 4 * DI], F32,
                                    kind="ExternalOutput").ap()

    A = nc.alloc_sbuf_tensor
    for nm, shape, dt in [
        ("causal", [128, 64], BF16), ("id_bf", [128, 128], BF16),
        ("id_f32", [128, 128], F32), ("cw_sb", [128, 128], F32),
        ("cb_sb", [128, 32], F32), ("nsc_sb", [128, 32], F32),
        ("usc_sb", [128, 64], F32), ("dsc_sb", [128, 16], F32),
        ("arow", [1, 64], F32), ("brow", [1, 64], F32), ("drow", [1, 64], F32),
        ("oscv", [1, 1], F32), ("flagv", [1, 1], F32), ("OSCB", [128, 1], F32),
        ("NEGA", [128, 64], F32), ("DTB", [128, 64], F32), ("DP", [128, 64], F32),
        ("FLG", [64, 1], F32),
        ("mu", [128, NT], F32), ("rstd", [128, NT], F32), ("nmr", [128, NT], F32),
        ("musum", [128, NT], F32), ("sum2", [128, NT], F32),
        ("stat_t", [128, NT], F32),
    ]:
        E[nm] = A(nm, shape, dt)

    import contextlib
    early_stack = contextlib.ExitStack()
    for nm, shape, dt in [
        ("ones", [128, 2048], F32), ("reset", [64, 2048], F32),
        ("dtv", [128, 16 * 64], F32), ("CS", [64, 2048], F32),
        ("GCS", [64, 1024], F32),
    ]:
        E[nm] = early_stack.enter_context(nc.sbuf_tensor(nm, shape, dt,
                                                         side="right"))

    _p0_consts(C, E)
    _p1_stats(C, E)
    _p2_xnf(C, E)

    ssd_stack = contextlib.ExitStack()
    for nm, shape, dt in [
        ("k_tok", [128, 16 * 64], BF16), ("q_tok", [128, 16 * 64], BF16),
        ("CS_T", [128, 16 * 64], F32), ("ECS_T", [128, 16 * 64], F32),
        ("DEC_T", [128, 16 * 64], F32), ("DECP_T", [128, 8 * 64], F32),
        ("DBC64", [64, 2048], F32), ("h_st", [64, DI], F32),
        ("vt", [128, 4 * DI], BF16),
        ("xtail", [128, 32 * 12], F32), ("sqs_cv", [128, 512], F32),
    ]:
        E[nm] = ssd_stack.enter_context(nc.sbuf_tensor(nm, shape, dt))

    _p3_bcdt(C, E)
    _p4_csfam(C, E)
    early_stack.close()
    _p5_z(C, E)
    _p6_main(C, E)
    ssd_stack.close()
    _p7_ffn_out(C, E)
    return nc


def _p0_consts(C, E):
    nc = C.nc
    P = Phase(C, "p0")
    for dst, src in [
        (E["cw_sb"][:, :], E["d_cw"]), (E["cb_sb"][:, :], E["d_cb"]),
        (E["nsc_sb"][:, :], E["d_nsc"]), (E["usc_sb"][:, :], E["d_usc"]),
        (E["dsc_sb"][:, :], E["d_dsc"]), (E["oscv"][:, :], E["d_osc"]),
        (E["flagv"][:, :], E["d_flag"]),
        (E["arow"][:, :], E["d_vecs"][0:1, :]),
        (E["brow"][:, :], E["d_vecs"][1:2, :]),
        (E["drow"][:, :], E["d_vecs"][2:3, :]),
    ]:
        nm = src.name if hasattr(src, "name") else "x"
        P.t_dma("sync", dst, src, w=[id(dst.tensor) if hasattr(dst, "tensor") else nm])
    P.tasks = []
    loads = [
        ("cw_sb", E["d_cw"]), ("cb_sb", E["d_cb"]), ("nsc_sb", E["d_nsc"]),
        ("usc_sb", E["d_usc"]), ("dsc_sb", E["d_dsc"]), ("oscv", E["d_osc"]),
        ("flagv", E["d_flag"]),
    ]
    for nm, src in loads:
        P.t_dma("sync", E[nm][:, :], src, w=[nm])
    P.t_dma("sync", E["arow"][:, :], E["d_vecs"][0:1, :], w=["arow"])
    P.t_dma("sync", E["brow"][:, :], E["d_vecs"][1:2, :], w=["brow"])
    P.t_dma("sync", E["drow"][:, :], E["d_vecs"][2:3, :], w=["drow"])

    P.add("scalar", lambda e: e.activation(E["arow"][:, :], E["arow"][:, :], AF.Exp),
          r=["arow"], w=["arow"])
    P.add("vector", lambda e: e.tensor_scalar_mul(E["arow"][:, :], E["arow"][:, :], -1.0),
          r=["arow"], w=["arow"])
    P.t_dma("gpsimd", E["d_nrow"], E["arow"][:, :], r=["arow"],
            w=["nrow_dram"])

    P.add("gpsimd", lambda e: e.memset(E["ones"][:, :], 1.0), w=["ones"])
    P.add("gpsimd", lambda e: e.affine_select(
        E["reset"][:, :], E["ones"][0:64, :], [[0, 32], [1, 64]], OP.is_gt, 0.0,
        base=0, channel_multiplier=0), r=["ones"], w=["reset"])
    P.add("gpsimd", lambda e: e.affine_select(
        E["causal"][0:64, :], E["ones"][0:64, 0:64], [[1, 64]], OP.is_ge, 0.0,
        base=0, channel_multiplier=-1), r=["ones"], w=["causal"])
    P.add("gpsimd", lambda e: e.affine_select(
        E["causal"][64:128, :], E["ones"][0:64, 0:64], [[1, 64]], OP.is_ge, 0.0,
        base=0, channel_multiplier=-1), r=["ones"], w=["causal"])
    P.add("gpsimd", lambda e: e.affine_select(
        E["id_bf"][:, :], E["ones"][:, 0:128], [[-1, 128]], OP.is_equal, 0.0,
        base=0, channel_multiplier=1), r=["ones"], w=["id_bf"])
    P.add("gpsimd", lambda e: e.affine_select(
        E["id_f32"][:, :], E["ones"][:, 0:128], [[-1, 128]], OP.is_equal, 0.0,
        base=0, channel_multiplier=1), r=["ones"], w=["id_f32"])
    P.t_dma("gpsimd", E["NEGA"][:, :], E["d_nrow"].partition_broadcast(128),
            r=["nrow_dram"], w=["NEGA"])
    P.t_dma("gpsimd", E["DTB"][:, :],
            E["d_vecs"][1:2, :].partition_broadcast(128), w=["DTB"])
    P.t_dma("gpsimd", E["DP"][:, :],
            E["d_vecs"][2:3, :].partition_broadcast(128), w=["DP"])
    P.t_dma("gpsimd", E["FLG"][:, :], E["d_flag"].partition_broadcast(64),
            w=["FLG"])
    P.t_dma("gpsimd", E["OSCB"][:, :], E["d_osc"].partition_broadcast(128),
            w=["OSCB"])
    P.add("vector", lambda e: e.tensor_scalar_mul(
        E["dsc_sb"][:, :], E["dsc_sb"][:, :], E["OSCB"][:, 0:1]),
        r=["OSCB", "dsc_sb"], w=["dsc_sb"])
    P.t_dma("gpsimd", E["d_wb"][:, 8192:8384], E["d_w_in"][:, 8192:8384],
            w=[("wcv", "b")])
    P.run()


def _p1_stats(C, E):
    nc = C.nc
    with nc.sbuf_tensor("UT1", [128, 2 * DM], F32) as UT, \
         nc.sbuf_tensor("SQS1", [128, DM], F32) as SQS:
        P = Phase(C, "p1")
        for i in range(NT):
            buf = UT[:, (i % 2) * DM:(i % 2 + 1) * DM]
            P.t_dma("sync", buf, E["d_u_tok"][i * 128:(i + 1) * 128, :],
                    w=[("ut", i % 2)])
            P.add("vector", lambda e, b=buf, i=i: e.reduce_sum(
                E["musum"][:, i:i + 1], b, axis=AX.X), r=[("ut", i % 2)],
                w=["musum"])
            P.add("scalar", lambda e, b=buf, i=i: e.activation(
                SQS[:, :], b, AF.Square, accum_out=E["sum2"][:, i:i + 1]),
                r=[("ut", i % 2)], w=["sum2"])
        P.run()

    P = Phase(C, "p1b")

    def v1(e):
        MU, RS, NM, T = E["mu"], E["rstd"], E["nmr"], E["stat_t"]
        e.tensor_scalar_mul(MU[:, :], E["musum"][:, :], 1.0 / DM)
        e.tensor_scalar_mul(T[:, :], E["sum2"][:, :], 1.0 / DM)
        e.tensor_mul(RS[:, :], MU[:, :], MU[:, :])
        e.tensor_sub(T[:, :], T[:, :], RS[:, :])
        return e.tensor_scalar_add(T[:, :], T[:, :], 1e-5)
    P.add("vector", v1, r=["musum", "sum2"], w=["mu", "vareps"])
    P.add("scalar", lambda e: e.activation(E["rstd"][:, :], E["stat_t"][:, :],
                                           AF.Sqrt), r=["vareps"], w=["sq"])

    def v2(e):
        RS, T, MU, NM = E["rstd"], E["stat_t"], E["mu"], E["nmr"]
        R0, S2 = E["musum"], E["sum2"]
        e.reciprocal(R0[:, :], RS[:, :])
        e.tensor_mul(S2[:, :], R0[:, :], R0[:, :])
        e.tensor_mul(S2[:, :], S2[:, :], T[:, :])
        e.tensor_scalar(S2[:, :], S2[:, :], -0.5, 1.5, OP.mult, OP.add)
        e.tensor_mul(RS[:, :], R0[:, :], S2[:, :])
        e.tensor_mul(NM[:, :], MU[:, :], RS[:, :])
        return e.tensor_scalar_mul(NM[:, :], NM[:, :], -1.0)
    P.add("vector", v2, r=["sq", "mu", "vareps"], w=["rstd", "musum", "sum2"])
    if "stats" in C.dbg:
        for i, nm in enumerate(["mu", "rstd", "nmr", "musum", "sum2",
                                "stat_t"]):
            P.t_dma("gpsimd", E["d_pstat"][:, i * NT:(i + 1) * NT],
                    E[nm][:, :], r=["rstd", "mu"], w=["pst"])
        P.add("sync", lambda e: None, r=["pst"])
    P.run()


def _p2_xnf(C, E):
    nc = C.nc
    groups = [(0, 4), (4, 4), (8, 4), (12, 4), (16, 1)]
    with nc.sbuf_tensor("UT2", [128, 2 * DM], F32) as UT, \
         nc.sbuf_tensor("XT2", [128, 2 * DM], BF16) as XT, \
         nc.sbuf_tensor("XSL2", [128, 2 * 16 * 512], BF16) as XSL, \
         nc.psum_tensor("ps2a", [128, 1024], BF16) as PSA, \
         nc.psum_tensor("ps2b", [128, 1024], BF16) as PSB:
        P = Phase(C, "p2")
        XSL4 = XSL.rearrange("p (r k t) -> p r k t", r=2, k=16)
        for gi, (t0, gn) in enumerate(groups):
            for j in range(gn):
                i = t0 + j
                n = gi * 4 + j
                pr = n % 2
                ub = UT[:, pr * DM:(pr + 1) * DM]
                xb = XT[:, pr * DM:(pr + 1) * DM]
                ps = PSA if pr == 0 else PSB
                P.t_dma("sync", ub, E["d_u_tok"][i * 128:(i + 1) * 128, :],
                        w=[("u2", pr)])
                P.add("scalar", lambda e, ub=ub, xb=xb, i=i: e.activation(
                    xb, ub, AF.Identity, bias=E["nmr"][:, i:i + 1],
                    scale=E["rstd"][:, i:i + 1]),
                    r=[("u2", pr), "rstd"], w=[("xn", pr)])
                for half in range(2):
                    def trh(e, xb=xb, ps=ps, half=half):
                        ins = None
                        for k in range(half * 8, half * 8 + 8):
                            ins = e.transpose(
                                ps[:, (k % 8) * 128:(k % 8) * 128 + 128],
                                xb[:, k * 128:(k + 1) * 128], E["id_bf"][:, :])
                        return ins
                    P.add("tensor", trh, r=[("xn", pr), "id_bf"],
                          w=[("tph", pr)])
                    dst = XSL4[:, gi % 2, half * 8:(half + 1) * 8,
                               j * 128:(j + 1) * 128]
                    P.add("vector", lambda e, dst=dst, ps=ps: e.tensor_copy(
                        dst, r3(ps[:, :], 8, 128)),
                        r=[("tph", pr)],
                        w=[("slab", gi % 2)])
            dst = E["d_xnf"].rearrange("k d t -> d k t"
                                       )[:, :, t0 * 128:t0 * 128 + gn * 128]
            P.t_dma("scalar", dst, XSL4[:, gi % 2, :, 0:gn * 128],
                    r=[("slab", gi % 2)], w=[("xnfd", gi), ("slab", gi % 2)])
            P2W = {0: [("z", 0), ("z", 1)], 1: [("z", 2), ("z", 3)],
                   2: [("z", 4), ("z", 5)], 3: [("z", 6), ("z", 7)],
                   4: [("x", 0), ("x", 1)]}
            for fam, ci in P2W.get(gi, []):
                off = 4096 if fam == "x" else 0
                sl = slice(off + ci * 512, off + (ci + 1) * 512)
                P.t_dma("gpsimd", E["d_wb"][:, sl], E["d_w_in"][:, sl],
                        r=[("xnfd", gi)], w=[("wcv", fam, ci)])
        P.add("sync", lambda e: None,
              r=[("xnfd", g2) for g2 in range(len(groups))], w=["xnf_ready"])
        P.run()


def _p3_bcdt(C, E):
    nc = C.nc
    with nc.sbuf_tensor("WBCDT", [128, 16 * 192], BF16) as W, \
         nc.sbuf_tensor("XST3", [128, 2 * 16 * 512], BF16) as XST, \
         nc.sbuf_tensor("p3T1", [128, 2 * 64], F32) as T1, \
         nc.sbuf_tensor("p3T2", [128, 2 * 64], F32) as T2, \
         nc.sbuf_tensor("p3T3", [128, 2 * 64], F32) as T3, \
         nc.sbuf_tensor("p3MSK", [128, 2 * 64], mybir.dt.int32) as MSK, \
         nc.psum_tensor("ps3a", [128, 192], F32) as PA, \
         nc.psum_tensor("ps3b", [128, 192], F32) as PB:
        P = Phase(C, "p3")
        W3 = r3(W[:, :], 16, 192)
        XST4 = XST.rearrange("p (r k t) -> p r k t", r=2, k=16)
        P.t_dma("sync", W3,
                E["d_wb"].rearrange("(k p) c -> p k c", p=128)[:, :, 8192:8384],
                r=[("wcv", "b")], w=["W3"])
        for g in range(4):
            src = E["d_xnf"].rearrange("k d t -> d k t"
                                       )[:, :, 128 + g * 512:128 + (g + 1) * 512]
            P.t_dma("sync", XST4[:, g % 2], src,
                    r=[("xst_used", g - 2), "xnf_ready"], w=[("xst", g)])
            if g >= 1:
                ci = g + 1
                sl = slice(4096 + ci * 512, 4096 + (ci + 1) * 512)
                P.t_dma("gpsimd", E["d_wb"][:, sl], E["d_w_in"][:, sl],
                        r=[("xst", g)], w=[("wcv", "x", ci)])
            for j in range(4):
                idx = g * 4 + j
                par = idx % 2
                ps = PA if par == 0 else PB

                def mm(e, g=g, j=j, ps=ps):
                    ins = None
                    for k in range(16):
                        ins = e.matmul(ps[:, :], XST4[:, g % 2, k,
                                                      j * 128:(j + 1) * 128],
                                       W3[:, k, :], start=(k == 0),
                                       stop=(k == 15))
                    return ins
                P.add("tensor", mm, r=[("xst", g), "W3", ("ps3r", par)],
                      w=[("ps3", par), ("xst_used", g) if j == 3 else ("xju", g, j)])

                for col, dst in ((0, "k_tok"), (64, "q_tok")):
                    P.add("vector", lambda e, ps=ps, col=col, par=par:
                          e.tensor_scalar_min(T1[:, par * 64:(par + 1) * 64],
                                              ps[:, col:col + 64], 0.0),
                          r=[("ps3", par)], w=[("T1", par)])
                    P.add("scalar", lambda e, par=par: e.activation(
                        T2[:, par * 64:(par + 1) * 64],
                        T1[:, par * 64:(par + 1) * 64], AF.Exp),
                        r=[("T1", par)], w=[("T2", par)])
                    P.add("vector", lambda e, ps=ps, col=col, par=par, dst=dst,
                          idx=idx: e.scalar_tensor_tensor(
                              E[dst][:, idx * 64:(idx + 1) * 64],
                              ps[:, col:col + 64], 0.0,
                              T2[:, par * 64:(par + 1) * 64], OP.max, OP.add),
                          r=[("ps3", par), ("T2", par)], w=[dst, ("T1", par)])

                def vdt(e, ps=ps, par=par):
                    e.tensor_add(T1[:, par * 64:(par + 1) * 64],
                                 ps[:, 128:192], E["DTB"][:, 0:64])
                    e.tensor_scalar_min(T3[:, par * 64:(par + 1) * 64],
                                        T1[:, par * 64:(par + 1) * 64], 20.0)
                    return e.tensor_single_scalar(
                        MSK[:, par * 64:(par + 1) * 64],
                        T1[:, par * 64:(par + 1) * 64], 20.0, OP.is_gt)
                P.add("vector", vdt, r=[("ps3", par), "DTB"],
                      w=[("T1", par), ("T3", par), ("MSK", par)])

                def sdt(e, par=par):
                    e.activation(T2[:, par * 64:(par + 1) * 64],
                                 T3[:, par * 64:(par + 1) * 64], AF.Exp)
                    return e.activation(T2[:, par * 64:(par + 1) * 64],
                                        T2[:, par * 64:(par + 1) * 64],
                                        AF.Ln, bias=1.0)
                P.add("scalar", sdt, r=[("T3", par)], w=[("T2", par)])
                P.add("vector", lambda e, par=par, idx=idx: e.select(
                    E["dtv"][:, idx * 64:(idx + 1) * 64],
                    MSK[:, par * 64:(par + 1) * 64],
                    T1[:, par * 64:(par + 1) * 64],
                    T2[:, par * 64:(par + 1) * 64], add_drain=True),
                    r=[("T2", par), ("T1", par), ("MSK", par)],
                    w=["dtv", ("ps3r", par)])
        P.run()


def _p4_csfam(C, E):
    nc = C.nc
    with nc.sbuf_tensor("p4XX", [64, 2048], F32) as XX, \
         nc.sbuf_tensor("p4YY", [64, 1024], F32) as YY, \
         nc.sbuf_tensor("p4TOT", [64, 32], F32) as TOT, \
         nc.sbuf_tensor("p4TOTT", [32, 64], F32) as TOTT, \
         nc.psum_tensor("ps4a", [128, 512], F32) as PSA, \
         nc.psum_tensor("ps4b", [128, 512], F32) as PSB:
        P = Phase(C, "p4")
        P.add("vector", lambda e: e.tensor_tensor(
            r3(E["dtv"][:, :], 16, 64), r3(E["dtv"][:, :], 16, 64),
            E["NEGA"][:, :].unsqueeze(1).broadcast_to([128, 16, 64]), OP.mult),
            r=["dtv", "NEGA"], w=["dtv"])
        for g in range(4):
            ps = PSA if g % 2 == 0 else PSB

            def trs(e, g=g, ps=ps):
                ins = None
                for i in range(4):
                    ins = e.transpose(ps[0:64, i * 128:(i + 1) * 128],
                                      E["dtv"][:, (g * 4 + i) * 64:(g * 4 + i + 1) * 64],
                                      E["id_f32"][:, :])
                return ins
            P.add("tensor", trs, r=["dtv", "id_f32", ("ps4r", g % 2)],
                  w=[("ps4", g % 2)])
            P.add("vector", lambda e, g=g, ps=ps: e.tensor_copy(
                XX[:, g * 512:(g + 1) * 512], ps[0:64, 0:512]),
                r=[("ps4", g % 2)], w=["LAH", ("ps4r", g % 2)])

        def scans(e):
            e.tensor_tensor_scan(E["CS"][:, :], E["reset"][:, :], XX[:, :],
                                 0.0, OP.mult, OP.add)
            e.tensor_tensor_scan(E["GCS"][:, :], E["ones"][0:64, 0:1024],
                                 XX[:, 0:1024], 0.0, OP.mult, OP.add)
            e.tensor_copy(TOT[:, :],
                          E["CS"].rearrange("h (c s) -> h c s", c=32)[:, :, 63])
            e.tensor_tensor(r3(XX[:, :], 32, 64), bfree(TOT[:, :], 64),
                            r3(E["CS"][:, :], 32, 64), OP.subtract)
            return e.tensor_tensor(
                YY[:, :].unsqueeze(1),
                bfree(E["GCS"][:, 1023:1024], 1024).transpose([0, 2, 1]),
                E["GCS"][:, :].unsqueeze(1), OP.subtract)
        P.add("vector", scans, r=["LAH", "reset", "ones"],
              w=["CS", "GCS", "TOT", "XXd", "YYd", "LAH"])

        def exps(e):
            e.activation(XX[:, :], XX[:, :], AF.Exp)
            e.activation(YY[:, :], YY[:, :], AF.Exp)
            return e.activation(TOT[:, :], TOT[:, :], AF.Exp)
        P.add("scalar", exps, r=["XXd", "YYd", "TOT"],
              w=["XXd", "YYd", "TOT"])

        jobs = [(E["CS"], "CS_T", 16, "CS"), (XX, "DEC_T", 16, "XXd"),
                (YY, "DECP_T", 8, "YYd")]
        gidx = 0
        for src, dstn, cnt, tok in jobs:
            for g in range((cnt + 3) // 4):
                take = min(4, cnt - g * 4)
                ps = PSA if gidx % 2 == 0 else PSB

                def trs2(e, src=src, g=g, take=take, ps=ps):
                    ins = None
                    for i in range(take):
                        ins = e.transpose(ps[:, i * 64:(i + 1) * 64],
                                          src[:, (g * 4 + i) * 128:(g * 4 + i + 1) * 128],
                                          E["id_f32"][0:64, 0:64])
                    return ins
                P.add("tensor", trs2, r=[tok, "id_f32", ("ps4r", gidx % 2)],
                      w=[("ps4", gidx % 2)])
                dst = E[dstn][:, (g * 4) * 64:(g * 4) * 64 + take * 64]
                P.add("vector", lambda e, dst=dst, ps=ps, take=take:
                      e.tensor_copy(dst, ps[:, 0:take * 64]),
                      r=[("ps4", gidx % 2)], w=[dstn, ("ps4r", gidx % 2)])
                gidx += 1
        P.add("tensor", lambda e: e.transpose(PSA[0:32, 0:64], TOT[:, :],
                                              E["id_f32"][0:64, 0:64]),
              r=["TOT", "id_f32", ("ps4r", 0)], w=[("ps4", 0)])
        P.add("vector", lambda e: e.tensor_copy(TOTT[:, :], PSA[0:32, 0:64]),
              r=[("ps4", 0)], w=["TOTT", ("ps4r", 0)])
        P.add("scalar", lambda e: e.activation(E["ECS_T"][:, :], E["CS_T"][:, :],
                                               AF.Exp), r=["CS_T"], w=["ECS_T"])
        for ci in (5, 6, 7):
            sl = slice(4096 + ci * 512, 4096 + (ci + 1) * 512)
            P.t_dma("gpsimd", E["d_wb"][:, sl], E["d_w_in"][:, sl],
                    r=["CS"], w=[("wcv", "x", ci)])
        P.t_dma("scalar", E["d_csd"].rearrange("c (h s) -> h c s", h=64),
                r3(E["CS"][:, :], 32, 64), r=["CS"], w=["csd_dram"])
        P.t_dma("scalar", E["d_trow"].rearrange("o (c h) -> (o c) h", c=32),
                TOTT[:, :], r=["TOTT"], w=["trow_dram"])
        P.t_dma("gpsimd", E["DBC64"][:, :],
                E["d_trow"].partition_broadcast(64), r=["trow_dram"],
                w=["DBC64"])
        P.add("sync", lambda e: None, r=["csd_dram"])
        P.run()


def _p5_z(C, E):
    nc = C.nc
    with nc.sbuf_tensor("WZ5", [128, 2 * 16 * 512], BF16) as WZ, \
         nc.sbuf_tensor("XST5", [128, 2 * 16 * 512], BF16) as XST, \
         nc.sbuf_tensor("ZR5", [128, 4 * 512], BF16) as ZR, \
         nc.psum_tensor("ps5a", [128, 512], F32) as PA, \
         nc.psum_tensor("ps5b", [128, 512], F32) as PB:
        P = Phase(C, "p5")
        WZ4 = WZ.rearrange("p (r k c) -> p r k c", r=2, k=16)
        XST4 = XST.rearrange("p (r k t) -> p r k t", r=2, k=16)
        for tg in range(2):
            src = E["d_xnf"].rearrange("k d t -> d k t"
                                       )[:, :, 1152 + tg * 512:1152 + (tg + 1) * 512]
            P.t_dma("sync", XST4[:, tg], src,
                    r=["xnf_ready"], w=[("xst5", tg)])
        for zb in range(8):
            P.t_dma("sync", WZ4[:, zb % 2],
                    E["d_wb"].rearrange("(k p) c -> p k c", p=128
                                        )[:, :, zb * 512:(zb + 1) * 512],
                    r=[("wzu", zb - 2), ("wcv", "z", zb)], w=[("wz", zb)])
            for tg in range(2):
                for j in range(4):
                    idx = zb * 8 + tg * 4 + j
                    par = idx % 2
                    ps = PA if par == 0 else PB

                    def mm(e, zb=zb, tg=tg, j=j, ps=ps):
                        ins = None
                        for k in range(16):
                            ins = e.matmul(ps[:, :],
                                           XST4[:, tg, k, j * 128:(j + 1) * 128],
                                           WZ4[:, zb % 2, k, :],
                                           start=(k == 0), stop=(k == 15))
                        return ins
                    P.add("tensor", mm,
                          r=[("xst5", tg), ("wz", zb), ("ps5r", par)],
                          w=[("ps5", par)] + ([("wzu", zb)] if (tg == 1 and j == 3) else []))
                    P.add("scalar", lambda e, ps=ps, idx=idx: e.activation(
                        ZR[:, (idx % 4) * 512:(idx % 4 + 1) * 512], ps[:, :],
                        AF.Silu),
                        r=[("ps5", par), ("zru", idx - 3)],
                        w=[("zr", idx), ("ps5r", par)])
                    ti = tg * 4 + j
                    P.t_dma("scalar", E["d_zs"][ti, :, zb * 512:(zb + 1) * 512],
                            ZR[:, (idx % 4) * 512:(idx % 4 + 1) * 512],
                            r=[("zr", idx)], w=[("zru", idx), ("zsd", zb, ti)])
        P.run()


CONV_LEVEL = 99
CONV_SKIP = ""
CONV_NMG = 8

XBLOCKS = [(0, 512, (1, 2, 3)), (512, 512, (4, 5, 6, 7)),
           (1024, 512, (8, 9, 10, 11)), (1536, 512, (12, 13, 14, 15)),
           (2048, 128, (16,))]


def _p6_conv_blk(C, E, bi):
    nc = C.nc
    a, w, tiles = XBLOCKS[bi]
    with nc.sbuf_tensor(f"XMV6_{bi}", [128, 16 * 512], BF16) as XMV, \
         nc.sbuf_tensor(f"WXG6_{bi}", [128, 2 * 16 * 512], BF16) as WXG, \
         nc.sbuf_tensor(f"XJ6_{bi}", [128, 2 * 4 * 516], F32) as XJ, \
         nc.psum_tensor(f"ps6xa_{bi}", [128, 512], F32) as PXA, \
         nc.psum_tensor(f"ps6xb_{bi}", [128, 512], F32) as PXB, \
         nc.psum_tensor(f"ps6ca_{bi}", [128, 512], F32) as PCA, \
         nc.psum_tensor(f"ps6cb_{bi}", [128, 512], F32) as PCB:
        P = Phase(C, f"cv{bi}")
        WXG4 = WXG.rearrange("p (r k c) -> p r k c", r=2, k=16)
        XJ4 = XJ.rearrange("p (r j c) -> p r j c", r=2, j=4)
        P.t_dma("sync", r3(XMV[:, 0:16 * w], 16, w),
                E["d_xnf"].rearrange("k d t -> d k t")[:, :, a:a + w],
                r=["xnf_ready"], w=["XMV"])
        CVW = {1: [("u", 0), ("u", 1), ("u", 2), ("u", 3)],
               2: [("u", 4), ("u", 5), ("u", 6), ("u", 7)],
               3: [("d", 3), ("d", 4), ("d", 5), ("d", 6)]}
        for j, (fam, ci) in enumerate(CVW.get(bi, [])):
            st_, dt_, rows = {"u": (E["d_w_upT"], E["d_wub"], 256),
                              "d": (E["d_w_dnT"], E["d_wdb"], 1024)}[fam]
            sl = slice(ci * rows, (ci + 1) * rows)
            P.t_dma("gpsimd", dt_[sl, :], st_[sl, :],
                    r=[("v", tiles[j])], w=[("wcv", fam, ci)])
        for mg in range(CONV_NMG):
            P.t_dma("sync", WXG4[:, mg % 2],
                    E["d_wb"].rearrange("(k p) c -> p k c", p=128
                                        )[:, :, 4096 + mg * 512:4096 + (mg + 1) * 512],
                    r=[("wcv", "x", mg)], w=[("wx", mg % 2)])
            for mm in range(4):
                m = mg * 4 + mm
                par = m % 2
                psx = PXA if par == 0 else PXB
                psc = PCA if par == 0 else PCB

                def mmx(e, mg=mg, mm=mm, psx=psx):
                    ins = None
                    for k in range(16):
                        ins = e.matmul(psx[:, 0:w], WXG4[:, mg % 2, k,
                                                        mm * 128:(mm + 1) * 128],
                                       r3(XMV[:, 0:16 * w], 16, w)[:, k, :],
                                       start=(k == 0), stop=(k == 15))
                    return ins
                P.add("tensor", mmx, r=[("wx", mg % 2), "XMV"],
                      w=[("psx", par)])

                if bi > 0:
                    P.add("vector", lambda e, m=m, par=par: e.tensor_copy(
                        XJ4[:, par, :, 0:3],
                        E["xtail"][:, m * 12:(m + 1) * 12].rearrange(
                            "p (j c) -> p j c", j=4)),
                        r=[("xtail", m)], w=[("xj0", par)])

                if CONV_LEVEL < 2:
                    continue

                def cpj01(e, m=m, par=par, psx=psx):
                    if CONV_SKIP == "act1":
                        return e.activation(XJ4[:, par, 0, 3:3 + w], psx[:, 0:w],
                                            AF.Identity)
                    e.activation(XJ4[:, par, 0, 3:3 + w], psx[:, 0:w], AF.Identity,
                                 bias=E["cb_sb"][:, m:m + 1],
                                 scale=E["cw_sb"][:, 4 * m:4 * m + 1])
                    return e.activation(XJ4[:, par, 1, 3:3 + w], psx[:, 0:w],
                                        AF.Identity,
                                        scale=E["cw_sb"][:, 4 * m + 1:4 * m + 2])
                P.add("scalar", cpj01, r=[("psx", par), "cw_sb", "cb_sb"],
                      w=[("xj1", par)], fast=True)

                def cpj23(e, m=m, par=par, psx=psx):
                    if CONV_SKIP == "dve_scr":
                        e.tensor_tensor(
                            E["sqs_cv"][:, 0:w], psx[:, 0:w],
                            E["cw_sb"][:, 4 * m + 2:4 * m + 3
                                       ].broadcast_to([128, w]), OP.mult)
                        return e.tensor_tensor(
                            E["sqs_cv"][:, 0:w], psx[:, 0:w],
                            E["cw_sb"][:, 4 * m + 3:4 * m + 4
                                       ].broadcast_to([128, w]), OP.mult)
                    if CONV_SKIP == "dve_sbsrc":
                        e.tensor_tensor(
                            XJ4[:, par, 2, 3:3 + w], E["sqs_cv"][:, 0:w],
                            E["cw_sb"][:, 4 * m + 2:4 * m + 3
                                       ].broadcast_to([128, w]), OP.mult)
                        return e.tensor_tensor(
                            XJ4[:, par, 3, 3:3 + w], E["sqs_cv"][:, 0:w],
                            E["cw_sb"][:, 4 * m + 3:4 * m + 4
                                       ].broadcast_to([128, w]), OP.mult)
                    e.tensor_tensor(
                        XJ4[:, par, 2, 3:3 + w], psx[:, 0:w],
                        E["cw_sb"][:, 4 * m + 2:4 * m + 3
                                   ].broadcast_to([128, w]), OP.mult)
                    return e.tensor_tensor(
                        XJ4[:, par, 3, 3:3 + w], psx[:, 0:w],
                        E["cw_sb"][:, 4 * m + 3:4 * m + 4
                                   ].broadcast_to([128, w]), OP.mult)
                if CONV_SKIP != "dve":
                    P.add("vector", cpj23, r=[("psx", par), ("xj1", par),
                                              "cw_sb"],
                          w=[("xj2", par)], fast=True)
                else:
                    P.add("vector", lambda e, par=par: e.engine_nop(),
                          w=[("xj2", par)])

                if CONV_LEVEL >= 3:
                    P.add("vector", lambda e, m=m, par=par: e.tensor_copy(
                        E["xtail"][:, m * 12:(m + 1) * 12].rearrange(
                            "p (j c) -> p j c", j=4), XJ4[:, par, :, w:w + 3]),
                        r=[("xj1", par), ("xj2", par)], w=[("xtail", m)])

                def trc(e, m=m, par=par, psc=psc, tiles=tiles, a=a):
                    ins = None
                    first = True
                    for wl, ti in enumerate(tiles):
                        for j in range(4):
                            col = 128 * ti + j - a
                            ins = e.matmul(psc[:, wl * 128:(wl + 1) * 128],
                                           XJ4[:, par, j, col:col + 128],
                                           E["id_f32"][:, :],
                                           start=first, stop=False,
                                           is_transpose=True,
                                           skip_group_check=True)
                            first = False
                    return ins
                rd = [("xj1", par), ("xj2", par), "id_f32"]
                if bi > 0:
                    rd.append(("xj0", par))
                if CONV_LEVEL >= 4:
                    P.add("tensor", trc, r=rd, w=[("psc", par)])

                def sil(e, m=m, par=par, psc=psc, tiles=tiles):
                    ins = None
                    for wl, ti in enumerate(tiles):
                        slot = ti % 4
                        ins = e.activation(
                            E["vt"][:, slot * DI + m * 128:slot * DI + (m + 1) * 128],
                            psc[:, wl * 128:(wl + 1) * 128], AF.Silu)
                    return ins
                if CONV_LEVEL >= 5:
                    P.add("scalar", sil, r=[("psc", par)],
                          w=[("v", ti) for ti in tiles])
        P.run()


def _p6_prefix(C, E, pairs, first, last):
    nc = C.nc
    with nc.psum_tensor(f"ps6h_{pairs[0]}", [64, DI], F32) as PH, \
         nc.sbuf_tensor(f"vscl_p{pairs[0]}", [128, DI], BF16) as VSCL:
        E["vscl"] = VSCL
        P = Phase(C, f"pre{pairs[0]}")
        if first:
            P.add("vector", lambda e: e.memset(E["h_st"][:, :], 0.0),
                  w=["h_st"])
        for p in pairs:
            ti = p + 1
            P.add("vector", lambda e, p=p, ti=ti: e.tensor_tensor(
                r3(E["vscl"][:, :], 64, 64),
                r3(E["vt"][:, (ti % 4) * DI:(ti % 4 + 1) * DI], 64, 64),
                bfree(E["DECP_T"][:, p * 64:(p + 1) * 64], 64), OP.mult),
                r=[("v", ti), "DECP_T"], w=["vscl"])
            for b in range(8):
                P.add("tensor", lambda e, p=p, b=b: e.matmul(
                    PH[:, b * 512:(b + 1) * 512],
                    E["k_tok"][:, p * 64:(p + 1) * 64],
                    E["vscl"][:, b * 512:(b + 1) * 512],
                    start=(p == pairs[0]), stop=(p == pairs[-1]),
                    skip_group_check=True),
                    r=["vscl", "k_tok"], w=[("ph", b)])
        for b in range(8):
            P.add("vector", lambda e, b=b: e.tensor_add(
                E["h_st"][:, b * 512:(b + 1) * 512],
                E["h_st"][:, b * 512:(b + 1) * 512], PH[:, b * 512:(b + 1) * 512]),
                r=[("ph", b), "h_st"], w=["h_st"])
        if last:
            P.add("vector", lambda e: e.tensor_scalar_mul(
                E["h_st"][:, :], E["h_st"][:, :], E["FLG"][:, 0:1]),
                r=["h_st", "FLG"], w=["h_st"])
        if last and "probe" in C.dbg:
            P.t_dma("gpsimd", E["d_ph0"], E["h_st"][:, :], r=["h_st"],
                    w=["prb"])
            P.t_dma("gpsimd", E["d_pdecp"], E["DECP_T"][:, :], r=["DECP_T"],
                    w=["prb"])

            def cast2(e):
                e.tensor_copy(E["vscl"][:, 0:1024].bitcast(F32),
                              E["k_tok"][:, :].bitcast(mybir.dt.uint16
                                                       ).bitcast(BF16)) \
                    if False else None
                return None
            P.t_dma("gpsimd", E["d_pkq"].bitcast(BF16)[:, 0:16 * 64],
                    E["k_tok"][:, :], r=["k_tok"], w=["prb"])
            P.t_dma("gpsimd", E["d_pkq"].bitcast(BF16)[:, 16 * 64:2 * 16 * 64],
                    E["q_tok"][:, :], r=["q_tok"], w=["prb"])
            P.t_dma("gpsimd", E["d_pvt"].bitcast(BF16)[:, 0:4 * DI],
                    E["vt"][:, :], r=[("v", 8)], w=["prb"])
            P.add("sync", lambda e: None, r=["prb"])
        P.run()


def _p6_ssd(C, E, pairs):
    nc = C.nc
    import contextlib
    st = contextlib.ExitStack()
    sfx = f"_s{pairs[0]}"
    for nm, shape, dt in [
        ("h_bf", [64, DI], BF16), ("h_bf2", [64, DI], BF16),
        ("kdiag", [128, 128], BF16), ("qstack", [128, 64], BF16),
        ("qflat_o", [64, 64], BF16), ("att", [128, NH * 128], BF16),
        ("Dmask", [128, NH * 64], BF16), ("CSB", [128, NH * 64], F32),
        ("G_sb", [128, 64], BF16), ("vscl", [128, DI], BF16),
        ("y_sl", [128, DI], BF16), ("zs_sl", [128, DI], BF16),
        ("yg_sl", [128, DI], BF16), ("YEV", [128, 512], F32),
        ("YEV2", [128, 512], F32), ("DVS", [128, 512], BF16),
        ("sqc", [128, 8], F32), ("rstd_y", [128, 1], F32),
        ("nwt", [128, 1], F32), ("sqs", [128, 512], F32),
    ]:
        E[nm] = st.enter_context(nc.sbuf_tensor(nm + sfx, shape, dt))
    with nc.psum_tensor(f"ps6sa{sfx}", [64, 512], F32) as PSa, \
         nc.psum_tensor(f"ps6sb{sfx}", [64, 512], F32) as PSb, \
         nc.psum_tensor(f"ps6Aa{sfx}", [128, 512], F32) as PAa, \
         nc.psum_tensor(f"ps6Ab{sfx}", [128, 512], F32) as PAb, \
         nc.psum_tensor(f"ps6Ba{sfx}", [128, 512], F32) as PBa, \
         nc.psum_tensor(f"ps6Bb{sfx}", [128, 512], F32) as PBb, \
         nc.psum_tensor(f"ps6T{sfx}", [128, 1024], BF16) as PT, \
         nc.psum_tensor(f"ps6G{sfx}", [128, 64], F32) as PG:
        P = Phase(C, f"ssd{pairs[0]}")
        SSW = {8: [("d", 0), ("d", 1), ("d", 2)],
               11: [("d", 7), ("o", 0), ("o", 1), ("o", 2)],
               15: [("o", 3)]}
        for j, (fam, ci) in enumerate(SSW.get(pairs[0], [])):
            st_, dt_ = {"d": (E["d_w_dnT"], E["d_wdb"]),
                        "o": (E["d_w_out"], E["d_wob"])}[fam]
            sl = slice(ci * 1024, (ci + 1) * 1024)
            P.t_dma("gpsimd", dt_[sl, :], st_[sl, :],
                    r=[("ygd", pairs[min(j, len(pairs) - 1)] - 8)],
                    w=[("wcv", fam, ci)])
        P.add("vector", lambda e: e.memset(E["att"][:, :], 0.0), w=["att"])
        P.add("vector", lambda e: e.memset(E["kdiag"][:, :], 0.0), w=["kdiag"])
        for p in pairs:
            ti = p + 1
            slot = ti % 4
            vsl = E["vt"][:, slot * DI:(slot + 1) * DI]

            P.t_dma("gpsimd", E["CSB"][0:64, :],
                    E["d_csd"][2 * p:2 * p + 1, :].partition_broadcast(64),
                    w=["CSBa"])
            P.t_dma("gpsimd", E["CSB"][64:128, :],
                    E["d_csd"][2 * p + 1:2 * p + 2, :].partition_broadcast(64),
                    w=["CSBb"])
            P.t_dma("sync", E["zs_sl"][:, 0:DI], E["d_zs"][p - 8],
                    r=[("zsd", zb2, p - 8) for zb2 in range(8)], w=["zs_sl"])

            def xm(e, p=p):
                e.scalar_tensor_tensor(
                    r3(E["CSB"][:, :], 64, 64), r3(E["CSB"][:, :], 64, 64), -1.0,
                    bfree(E["CS_T"][:, p * 64:(p + 1) * 64], 64),
                    OP.mult, OP.add)
                return e.tensor_scalar_max(E["CSB"][:, :], E["CSB"][:, :], 0.0)
            P.add("vector", xm, r=["CSBa", "CSBb", "CS_T"],
                  w=["CSBa", "CSBb", "Xm"])
            P.add("scalar", lambda e: e.activation(
                E["Dmask"][:, :], E["CSB"][:, :], AF.Exp, scale=-1.0),
                r=["Xm"], w=["Dmask"])

            def qktr(e, p=p):
                e.transpose(PT[0:64, 0:128], E["q_tok"][:, p * 64:(p + 1) * 64],
                            E["id_bf"][:, :])
                e.transpose(PT[64:128, 128:192],
                            E["q_tok"][64:128, p * 64:(p + 1) * 64],
                            E["id_bf"][64:128, 64:128], tile_position=(64, 64))
                e.transpose(PT[0:64, 192:256],
                            E["k_tok"][0:64, p * 64:(p + 1) * 64],
                            E["id_bf"][0:64, 0:64])
                return e.transpose(PT[64:128, 256:320],
                                   E["k_tok"][64:128, p * 64:(p + 1) * 64],
                                   E["id_bf"][64:128, 64:128],
                                   tile_position=(64, 64))
            P.add("tensor", qktr, r=["q_tok", "k_tok", "id_bf"], w=["psT"])

            def qkev(e):
                e.tensor_copy(E["qstack"][0:64, :], PT[0:64, 0:64])
                e.tensor_copy(E["qflat_o"][:, :], PT[0:64, 64:128])
                e.tensor_copy(E["qstack"][64:128, :], PT[64:128, 128:192])
                e.tensor_copy(E["kdiag"][0:64, 0:64], PT[0:64, 192:256])
                return e.tensor_copy(E["kdiag"][64:128, 64:128],
                                     PT[64:128, 256:320])
            P.add("vector", qkev, r=["psT"], w=["qstack", "qflat_o", "kdiag"],
                  fast=True)

            P.add("tensor", lambda e: e.matmul(PG[:, :], E["kdiag"][:, :],
                                               E["qstack"][:, :], start=True,
                                               stop=True),
                  r=["kdiag", "qstack"], w=["psG"])
            P.add("vector", lambda e: e.tensor_tensor(
                E["G_sb"][:, :], PG[:, :], E["causal"][:, :], OP.mult),
                r=["psG", "causal"], w=["G_sb"])

            def attb(e):
                A3 = E["att"].rearrange("p (h s) -> p h s", h=64)
                e.tensor_tensor(
                    A3[0:64, :, 0:64],
                    E["G_sb"][0:64, :].unsqueeze(1).broadcast_to([64, 64, 64]),
                    r3(E["Dmask"][0:64, :], 64, 64), OP.mult)
                return e.tensor_tensor(
                    A3[64:128, :, 64:128],
                    E["G_sb"][64:128, :].unsqueeze(1).broadcast_to([64, 64, 64]),
                    r3(E["Dmask"][64:128, :], 64, 64), OP.mult)
            P.add("vector", attb, r=["G_sb", "Dmask"], w=["att"],
                  fast=True)

            P.add("vector", lambda e, vsl=vsl, p=p: e.tensor_tensor(
                r3(E["vscl"][:, :], 64, 64), r3(vsl, 64, 64),
                bfree(E["DEC_T"][:, p * 64:(p + 1) * 64], 64), OP.mult),
                r=[("v", ti), "DEC_T"], w=["vscl"])

            P.add("vector", lambda e: e.tensor_copy(E["h_bf"][:, :],
                                                    E["h_st"][:, :]),
                  r=["h_st"], w=["h_bf"])
            P.add("vector", lambda e, p=p: e.tensor_tensor(
                r3(E["h_st"][:, :], 64, 64), r3(E["h_st"][:, :], 64, 64),
                bfree(E["DBC64"][0:64, (2 * p) * 64:(2 * p + 1) * 64], 64),
                OP.mult), r=["h_st", "DBC64"], w=["h_st"])
            for b in range(8):
                P.add("tensor", lambda e, p=p, b=b: e.matmul(
                    (PSa if b % 2 == 0 else PSb)[:, :],
                    E["k_tok"][0:64, p * 64:(p + 1) * 64],
                    E["vscl"][0:64, b * 512:(b + 1) * 512],
                    start=True, stop=True),
                    r=["vscl", "k_tok", ("pssr", b % 2)], w=[("pss", b % 2)])
                P.add("vector", lambda e, b=b: e.tensor_add(
                    E["h_st"][:, b * 512:(b + 1) * 512],
                    E["h_st"][:, b * 512:(b + 1) * 512],
                    (PSa if b % 2 == 0 else PSb)[:, :]),
                    r=[("pss", b % 2), "h_st"], w=["h_st", ("pssr", b % 2)])
            P.add("vector", lambda e: e.tensor_copy(E["h_bf2"][:, :],
                                                    E["h_st"][:, :]),
                  r=["h_st"], w=["h_bf2"])
            P.add("vector", lambda e, p=p: e.tensor_tensor(
                r3(E["h_st"][:, :], 64, 64), r3(E["h_st"][:, :], 64, 64),
                bfree(E["DBC64"][0:64, (2 * p + 1) * 64:(2 * p + 2) * 64], 64),
                OP.mult), r=["h_st", "DBC64"], w=["h_st"])
            for b in range(8):
                P.add("tensor", lambda e, p=p, b=b: e.matmul(
                    (PSa if b % 2 == 0 else PSb)[:, :],
                    E["k_tok"][64:128, p * 64:(p + 1) * 64],
                    E["vscl"][64:128, b * 512:(b + 1) * 512],
                    start=True, stop=True, tile_position=(64, 0)),
                    r=["vscl", "k_tok", ("pssr", b % 2)], w=[("pss", b % 2)])
                P.add("vector", lambda e, b=b: e.tensor_add(
                    E["h_st"][:, b * 512:(b + 1) * 512],
                    E["h_st"][:, b * 512:(b + 1) * 512],
                    (PSa if b % 2 == 0 else PSb)[:, :]),
                    r=[("pss", b % 2), "h_st"], w=["h_st", ("pssr", b % 2)])
            for b in range(8):
                PA = PAa if b % 2 == 0 else PAb
                PB = PBa if b % 2 == 0 else PBb
                P.add("tensor", lambda e, b=b, PA=PA: e.matmul(
                    PA[0:64, :], E["qstack"][0:64, :],
                    E["h_bf"][:, b * 512:(b + 1) * 512],
                    start=True, stop=False, skip_group_check=True),
                    r=["h_bf", "qstack", ("psar", b % 2)], w=[("psa", b % 2)])
                P.add("tensor", lambda e, b=b, PA=PA: e.matmul(
                    PA[64:128, :], E["qflat_o"][:, :],
                    E["h_bf2"][:, b * 512:(b + 1) * 512],
                    start=True, stop=True, tile_position=(0, 64),
                    skip_group_check=True),
                    r=["h_bf2", "qflat_o"], w=[("psa", b % 2)])

                def intra(e, b=b, vsl=vsl, PB=PB):
                    ins = None
                    for hh in range(8):
                        h = b * 8 + hh
                        ins = e.matmul(
                            PB[:, hh * 64:(hh + 1) * 64],
                            E["att"][:, h * 128:(h + 1) * 128],
                            vsl[:, h * 64:(h + 1) * 64],
                            start=(hh == 0), stop=(hh == 7),
                            skip_group_check=True)
                    return ins
                P.add("tensor", intra, r=["att", ("v", ti), ("psbr", b % 2)],
                      w=[("psb", b % 2)])

                def yev(e, p=p, b=b, vsl=vsl, PA=PA, PB=PB):
                    e.tensor_tensor(
                        r3(E["YEV"][:, :], 8, 64), r3(PA[:, :], 8, 64),
                        bfree(E["ECS_T"][:, p * 64 + b * 8:p * 64 + b * 8 + 8], 64),
                        OP.mult)
                    e.tensor_add(E["YEV2"][:, :], E["YEV"][:, :], PB[:, :])
                    e.tensor_tensor(r3(E["DVS"][:, :], 8, 64),
                                    r3(vsl[:, b * 512:(b + 1) * 512], 8, 64),
                                    bfree(E["DP"][:, b * 8:b * 8 + 8], 64),
                                    OP.mult)
                    return e.tensor_add(E["y_sl"][:, b * 512:(b + 1) * 512],
                                        E["YEV2"][:, :], E["DVS"][:, :])
                P.add("vector", yev,
                      r=[("psa", b % 2), ("psb", b % 2), ("v", ti), "ECS_T",
                         "DP"],
                      w=[("ysl", b), ("psar", b % 2), ("psbr", b % 2)])
                P.add("scalar", lambda e, b=b: e.activation(
                    E["sqs"][:, :], E["y_sl"][:, b * 512:(b + 1) * 512],
                    AF.Square, accum_out=E["sqc"][:, b:b + 1]),
                    r=[("ysl", b)], w=[("sqc", b)])

            def rsy1(e):
                e.reduce_sum(E["nwt"][:, :], E["sqc"][:, :], axis=AX.X)
                return e.tensor_scalar(E["nwt"][:, :], E["nwt"][:, :],
                                       1.0 / DI, 1e-6, OP.mult, OP.add)
            P.add("vector", rsy1, r=[("sqc", b) for b in range(8)], w=["nwt"])
            P.add("scalar", lambda e: e.activation(E["rstd_y"][:, :],
                                                   E["nwt"][:, :], AF.Sqrt),
                  r=["nwt"], w=["rsq"])

            def rsy2(e):
                RS, T = E["rstd_y"], E["nwt"]
                R0 = E["sqc"][:, 1:2]
                e.reciprocal(R0, RS[:, :])
                e.tensor_mul(E["sqc"][:, 0:1], R0, R0)
                e.tensor_mul(E["sqc"][:, 0:1], E["sqc"][:, 0:1], T[:, :])
                e.tensor_scalar(E["sqc"][:, 0:1], E["sqc"][:, 0:1], -0.5, 1.5,
                                OP.mult, OP.add)
                return e.tensor_mul(RS[:, :], R0, E["sqc"][:, 0:1])
            P.add("vector", rsy2, r=["rsq"], w=["rstd_y", ("sqc", 0),
                                               ("sqc", 1)])

            if p == 8 and "probe2" in C.dbg:
                for nm, src_ap, rt in [
                        ("py", E["y_sl"][:, :].bitcast(mybir.dt.uint16), None),
                        ("patt", E["att"][:, :].bitcast(mybir.dt.uint16), "att"),
                        ("pG", E["G_sb"][:, :].bitcast(mybir.dt.uint16), "G_sb"),
                        ("pD", E["Dmask"][:, :].bitcast(mybir.dt.uint16), "Dmask"),
                        ("phb", E["h_bf"][:, :].bitcast(mybir.dt.uint16), "h_bf"),
                        ("pqs", E["qstack"][:, :].bitcast(mybir.dt.uint16), "qstack"),
                        ("pkd", E["kdiag"][:, :].bitcast(mybir.dt.uint16), "kdiag"),
                        ("pvs", E["vscl"][:, :].bitcast(mybir.dt.uint16), "vscl")]:
                    rr = [rt] if rt else [("ysl", b) for b in range(8)]
                    P.t_dma("gpsimd",
                            E["d_" + nm].bitcast(mybir.dt.uint16
                                                 )[:, 0:src_ap.shape[1]],
                            src_ap, r=rr, w=["prb2"])
                P.add("sync", lambda e: None, r=["prb2"])
            P.add("vector", lambda e: e.scalar_tensor_tensor(
                E["yg_sl"][:, 0:DI], E["y_sl"][:, :], E["rstd_y"][:, 0:1],
                E["zs_sl"][:, 0:DI], OP.mult, OP.mult),
                r=["rstd_y", "zs_sl"] + [("ysl", b) for b in range(8)],
                w=["yg_sl"])
            P.t_dma("gpsimd", E["d_yg"][p - 8], E["yg_sl"][:, 0:DI],
                    r=["yg_sl"], w=[("ygd", p - 8)])
        P.add("sync", lambda e: None, r=[("ygd", p2 - 8) for p2 in pairs])
        P.run()
    st.close()


def _p6_main(C, E):
    _p6_conv_blk(C, E, 0)
    _p6_prefix(C, E, (0, 1, 2), True, False)
    _p6_conv_blk(C, E, 1)
    _p6_prefix(C, E, (3, 4, 5, 6), False, False)
    _p6_conv_blk(C, E, 2)
    _p6_prefix(C, E, (7,), False, True)
    _p6_ssd(C, E, (8, 9, 10))
    _p6_conv_blk(C, E, 3)
    _p6_ssd(C, E, (11, 12, 13, 14))
    _p6_conv_blk(C, E, 4)
    _p6_ssd(C, E, (15,))


def _p7_ffn_out(C, E):
    nc = C.nc
    with nc.sbuf_tensor("FFN_O", [128, 16 * 512], F32) as FFN_O:
        for tg in range(2):
            with nc.sbuf_tensor(f"HID_{tg}", [128, 64 * 512], BF16) as HID:
                E["HID"] = HID
                _p7_up(C, E, tg)
                _p7_dn(C, E, tg, FFN_O)
            _p7_out(C, E, tg, FFN_O)


def _p7_up(C, E, tg):
    nc = C.nc
    with nc.sbuf_tensor(f"XMV7_{tg}", [128, 16 * 512], BF16) as XMV, \
         nc.sbuf_tensor(f"SWUP_{tg}", [128, 2 * 16 * 512], BF16) as SWUP, \
         nc.psum_tensor(f"ps7ua_{tg}", [128, 512], F32) as PUA, \
         nc.psum_tensor(f"ps7ub_{tg}", [128, 512], F32) as PUB:
        P = Phase(C, f"up{tg}")
        SW4 = SWUP.rearrange("p (r k c) -> p r k c", r=2, k=16)
        HID3 = r3(E["HID"][:, :], 64, 512)
        P.t_dma("sync", r3(XMV[:, :], 16, 512),
                E["d_xnf"].rearrange("k d t -> d k t"
                                     )[:, :, 1152 + tg * 512:1152 + (tg + 1) * 512],
                r=["xnf_ready"], w=["XMV"])
        for mg in range(16):
            P.t_dma("sync", SW4[:, mg % 2],
                    E["d_wub"].rearrange("(k p) c -> p k c", p=128
                                         )[:, :, mg * 512:(mg + 1) * 512],
                    r=[("swu", mg % 2)] + [("wcv", "u", i4) for i4 in range(8)],
                    w=[("swraw", mg % 2)])
            P.add("scalar", lambda e, mg=mg: e.activation(
                SWUP.rearrange("p (r c) -> p r c", r=2)[:, mg % 2],
                SWUP.rearrange("p (r c) -> p r c", r=2)[:, mg % 2], AF.Sign),
                r=[("swraw", mg % 2)], w=[("sw", mg % 2)])
            for mm in range(4):
                m = mg * 4 + mm
                par = m % 2
                ps = PUA if par == 0 else PUB

                def mmu(e, mg=mg, mm=mm, ps=ps):
                    ins = None
                    for k in range(16):
                        ins = e.matmul(ps[:, :],
                                       SW4[:, mg % 2, k, mm * 128:(mm + 1) * 128],
                                       r3(XMV[:, :], 16, 512)[:, k, :],
                                       start=(k == 0), stop=(k == 15))
                    return ins
                P.add("tensor", mmu,
                      r=[("sw", mg % 2), "XMV", ("pur", par)],
                      w=[("pu", par)] + ([("swu", mg % 2)] if mm == 3 else []))
                P.add("vector", lambda e, m=m, ps=ps: e.tensor_scalar(
                    HID3[:, m, :], ps[:, :], E["usc_sb"][:, m:m + 1], 0.0,
                    OP.mult, OP.max),
                    r=[("pu", par), "usc_sb"], w=[("hid", m), ("pur", par)])
        P.run()


def _p7_dn(C, E, tg, FFN_O):
    nc = C.nc
    with nc.sbuf_tensor(f"SWDN_{tg}", [128, 2 * 32 * 512], BF16) as SWDN, \
         nc.psum_tensor(f"ps7d0_{tg}", [128, 512], F32) as PD0, \
         nc.psum_tensor(f"ps7d1_{tg}", [128, 512], F32) as PD1, \
         nc.psum_tensor(f"ps7d2_{tg}", [128, 512], F32) as PD2, \
         nc.psum_tensor(f"ps7d3_{tg}", [128, 512], F32) as PD3:
        PDS = [PD0, PD1, PD2, PD3]
        P = Phase(C, f"dn{tg}")
        SW4 = SWDN.rearrange("p (r k c) -> p r k c", r=2, k=32)
        HID3 = r3(E["HID"][:, :], 64, 512)
        for dmg in range(4):
            for kh in range(2):
                half = dmg * 2 + kh
                P.t_dma("sync", SW4[:, half % 2],
                        E["d_wdb"].rearrange("(k p) c -> p k c", p=128
                                             )[:, kh * 32:(kh + 1) * 32,
                                               dmg * 512:(dmg + 1) * 512],
                        r=[("swdu", half % 2)] +
                          [("wcv", "d", kh * 4 + j4) for j4 in range(4)],
                        w=[("swdraw", half % 2)])
                P.add("scalar", lambda e, half=half: e.activation(
                    SWDN.rearrange("p (r c) -> p r c", r=2)[:, half % 2],
                    SWDN.rearrange("p (r c) -> p r c", r=2)[:, half % 2],
                    AF.Sign), r=[("swdraw", half % 2)], w=[("swd", half % 2)])
                for dmm in range(4):
                    def mmd(e, kh=kh, dmm=dmm, dmg=dmg, half=half):
                        ins = None
                        for kk in range(32):
                            k = kh * 32 + kk
                            ins = e.matmul(
                                PDS[dmm][:, :],
                                SW4[:, half % 2, kk, dmm * 128:(dmm + 1) * 128],
                                HID3[:, k, :],
                                start=(kh == 0 and kk == 0),
                                stop=(kh == 1 and kk == 31))
                        return ins
                    P.add("tensor", mmd,
                          r=[("swd", half % 2), ("pdr", dmm)] +
                            [("hid", k2) for k2 in range(kh * 32, kh * 32 + 32)],
                          w=[("pd", dmm)] + ([("swdu", half % 2)]
                                             if dmm == 3 else []))
            for dmm in range(4):
                dmc = dmg * 4 + dmm
                P.add("vector", lambda e, dmm=dmm, dmc=dmc: e.tensor_scalar(
                    FFN_O[:, dmc * 512:(dmc + 1) * 512], PDS[dmm][:, :],
                    E["dsc_sb"][:, dmc:dmc + 1], None, OP.mult),
                    r=[("pd", dmm), "dsc_sb"], w=[("ffo", dmc), ("pdr", dmm)])
        P.run()


def _p7_out(C, E, tg, FFN_O):
    nc = C.nc
    with nc.sbuf_tensor(f"YGL_{tg}", [128, 2 * DI], BF16) as YGL, \
         nc.sbuf_tensor(f"YGT_{tg}", [128, 32 * 512], BF16) as YGT, \
         nc.sbuf_tensor(f"WOS_{tg}", [128, 32 * 512], BF16) as WOS, \
         nc.sbuf_tensor(f"UTS_{tg}", [128, 4 * 2048], F32) as UTS, \
         nc.sbuf_tensor(f"OSB_{tg}", [128, 4 * 2048], F32) as OSB, \
         nc.sbuf_tensor(f"SSL_{tg}", [128, 512], F32) as SSL, \
         nc.psum_tensor(f"ps7oa_{tg}", [128, 512], F32) as POA, \
         nc.psum_tensor(f"ps7ob_{tg}", [128, 512], F32) as POB, \
         nc.psum_tensor(f"ps7ty_{tg}", [128, 1024], BF16) as PTY, \
         nc.psum_tensor(f"ps7ka_{tg}", [128, 512], F32) as PKA, \
         nc.psum_tensor(f"ps7kb_{tg}", [128, 512], F32) as PKB:
        P = Phase(C, f"out{tg}")
        YGT3 = r3(YGT[:, :], 32, 512)
        WOS3 = r3(WOS[:, :], 32, 512)
        for ts in range(4):
            row = 1152 + tg * 512 + ts * 128
            P.t_dma("sync", UTS[:, ts * 2048:(ts + 1) * 2048],
                    E["d_u_tok"][row:row + 128, :], w=[("uts", ts)])
        for i in range(4):
            P.t_dma("sync", YGL[:, (i % 2) * DI:(i % 2 + 1) * DI],
                    E["d_yg"][tg * 4 + i], r=[("ygl", i % 2), ("ygd", tg * 4 + i)],
                    w=[("ygld", i % 2)])
            for half in range(4):
                def ytr(e, i=i, half=half):
                    ins = None
                    for kk in range(8):
                        k = half * 8 + kk
                        ins = e.transpose(
                            PTY[:, kk * 128:(kk + 1) * 128],
                            YGL[:, (i % 2) * DI + k * 128:(i % 2) * DI + (k + 1) * 128],
                            E["id_bf"][:, :])
                    return ins
                P.add("tensor", ytr, r=[("ygld", i % 2), "id_bf", ("ptyr",)],
                      w=[("pty",)])

                def yev(e, i=i, half=half):
                    return e.tensor_copy(
                        YGT3[:, half * 8:(half + 1) * 8, i * 128:(i + 1) * 128],
                        r3(PTY[:, :], 8, 128))
                P.add("vector", yev, r=[("pty",)],
                      w=[("ygt", i, half), ("ptyr",)] +
                        ([("ygl", i % 2)] if half == 3 else []))
        for mg in range(4):
            P.t_dma("sync", WOS3,
                    E["d_wob"].rearrange("(k p) c -> p k c", p=128
                                         )[:, :, mg * 512:(mg + 1) * 512],
                    r=[("wosu",)] + [("wcv", "o", i4) for i4 in range(4)],
                    w=[("wos_raw",)])

            P.add("vector", lambda e: e.tensor_tensor(
                WOS3, WOS3, bfree(E["nsc_sb"][:, 0:32], 512), OP.mult),
                r=[("wos_raw",), "nsc_sb"], w=[("wos",)])
            for m in range(4):
                dmc = mg * 4 + m
                par = m % 2
                po = POA if par == 0 else POB

                def mmo(e, m=m, po=po):
                    ins = None
                    for k in range(32):
                        ins = e.matmul(po[:, :], WOS3[:, k, m * 128:(m + 1) * 128],
                                       YGT3[:, k, :], start=(k == 0),
                                       stop=(k == 31))
                    return ins
                P.add("tensor", mmo,
                      r=[("wos",), ("por", par)] +
                        [("ygt", i, half) for i in range(4) for half in range(4)],
                      w=[("po", par)] + ([("wosu",)] if m == 3 else []))
                P.add("vector", lambda e, po=po, dmc=dmc: e.tensor_add(
                    SSL[:, :], po[:, :], FFN_O[:, dmc * 512:(dmc + 1) * 512]),
                    r=[("po", par), ("ffo", dmc), ("sslr",)],
                    w=[("ssl",), ("por", par)])

                def ktr(e, dmc=dmc, par=par):
                    ins = None
                    pk = PKA if par == 0 else PKB
                    for ts in range(4):
                        ins = e.transpose(pk[:, ts * 128:(ts + 1) * 128],
                                          SSL[:, ts * 128:(ts + 1) * 128],
                                          E["id_f32"][:, :])
                    return ins
                P.add("tensor", ktr, r=[("ssl",), "id_f32", ("pkr", par)],
                      w=[("pk", par), ("sslr",)])

                def radd(e, dmc=dmc, par=par):
                    ins = None
                    pk = PKA if par == 0 else PKB
                    for ts in range(4):
                        ins = e.tensor_add(
                            OSB[:, ts * 2048 + dmc * 128:ts * 2048 + (dmc + 1) * 128],
                            pk[:, ts * 128:(ts + 1) * 128],
                            UTS[:, ts * 2048 + dmc * 128:ts * 2048 + (dmc + 1) * 128])
                    return ins
                P.add("vector", radd,
                      r=[("pk", par)] + [("uts", ts) for ts in range(4)],
                      w=[("osb", dmc), ("pkr", par)], fast=True)
        for ts in range(4):
            P.t_dma("gpsimd", E["d_out"][tg * 512 + ts * 128:tg * 512 + (ts + 1) * 128, :],
                    OSB[:, ts * 2048:(ts + 1) * 2048],
                    r=[("osb", dmc2) for dmc2 in range(16)], w=["outw"])
        P.add("sync", lambda e: None, r=["outw"])
        P.run()


_CACHED = {}


def _stage_inputs(inputs):
    u = np.ascontiguousarray(np.asarray(inputs["L0_attn_u"], dtype=np.float32))
    w_in = np.ascontiguousarray(np.asarray(inputs["L0_attn_in_proj_w"], np.float32))
    convw = np.asarray(inputs["L0_attn_conv_weight"], np.float32)
    convb = np.asarray(inputs["L0_attn_conv_bias"], np.float32)
    A_log = np.asarray(inputs["L0_attn_A_log"], np.float32)
    dt_bias = np.asarray(inputs["L0_attn_dt_bias"], np.float32)
    D_param = np.asarray(inputs["L0_attn_D_param"], np.float32)
    norm_scale = np.asarray(inputs["L0_attn_norm_scale"], np.float32)
    w_out = np.ascontiguousarray(np.asarray(inputs["L0_attn_out_proj_w"], np.float32))
    w_upT = np.ascontiguousarray(
        np.asarray(inputs["L0_ffn_up_weight"], np.float32).T)
    w_dnT = np.ascontiguousarray(
        np.asarray(inputs["L0_ffn_down_weight"], np.float32).T)
    up_scale = np.asarray(inputs["L0_ffn_up_scale"], np.float32)
    down_scale = np.asarray(inputs["L0_ffn_down_scale"], np.float32)
    out_scale = np.asarray(inputs["L0_ffn_output_scale"], np.float32)

    cw = np.ascontiguousarray(
        convw.T.reshape(32, 128, 4).transpose(1, 0, 2).reshape(128, 128))
    cb = np.ascontiguousarray(convb.reshape(32, 128).T)
    nsc = np.ascontiguousarray(norm_scale.reshape(32, 128).T)
    usc = np.ascontiguousarray(up_scale.reshape(64, 128).T)
    dsc = np.ascontiguousarray(down_scale.reshape(16, 128).T)
    vecs = np.ascontiguousarray(np.stack([A_log, dt_bias, D_param]))
    osc = out_scale.reshape(1, 1)

    shared = dict(w_in=w_in, w_upT=w_upT, w_dnT=w_dnT, w_out=w_out,
                  cw=cw, cb=cb, nsc=nsc, usc=usc, dsc=dsc, vecs=vecs, osc=osc)
    zeros = np.zeros((1024, DM), np.float32)
    halo = np.zeros((128, DM), np.float32)
    in_maps = []
    for core in range(8):
        b, half = core // 2, core % 2
        prefix = u[b, 0:1024] if half == 1 else zeros
        own = u[b, half * 1024:(half + 1) * 1024]
        u_tok = np.ascontiguousarray(np.concatenate([halo, prefix, own], 0))
        m = dict(shared)
        m["u_tok"] = u_tok
        m["flag"] = np.array([[1.0 if half == 1 else 0.0]], np.float32)
        in_maps.append(m)
    return in_maps


def kernel(**inputs):
    if "nc" not in _CACHED:
        _CACHED["nc"] = build_program()
    nc = _CACHED["nc"]
    in_maps = _stage_inputs(inputs)
    res = run_bass_kernel_spmd(nc, in_maps, core_ids=list(range(8)))
    out = np.empty((4, 2048, DM), np.float32)
    for core in range(8):
        b, half = core // 2, core % 2
        out[b, half * 1024:(half + 1) * 1024] = res.results[core]["out"]
    return out


# version 18
# speedup vs baseline: 1.0584x; 1.0098x over previous
import numpy as np
import ml_dtypes
import concourse.bass as bass
import concourse.mybir as mybir
from concourse.bass_utils import run_bass_kernel_spmd

F32 = mybir.dt.float32
BF16 = mybir.dt.bfloat16
AF = mybir.ActivationFunctionType
OP = mybir.AluOpType
AX = mybir.AxisListType
BF16NP = ml_dtypes.bfloat16

NT = 17
TTOT = NT * 128
DM = 2048
DI = 4096
DH = 8192
NH = 64

ENGINES = ("sync", "scalar", "vector", "tensor", "gpsimd")


def bfree(ap2d, inner):
    return ap2d.unsqueeze(2).broadcast_to([ap2d.shape[0], ap2d.shape[1], inner])


def r3(ap2d, a, b):
    return ap2d.rearrange("p (a b) -> p a b", a=a, b=b)


class Task:
    __slots__ = ("eng", "emit", "reads", "writes", "dma", "seq", "done",
                 "waits", "prewait", "fast")

    def __init__(self, eng, emit, reads, writes, dma, fast=False):
        self.eng, self.emit = eng, emit
        self.reads, self.writes = list(reads), list(writes)
        self.dma = dma
        self.fast = fast


class Phase:

    def __init__(self, C, name):
        self.C = C
        self.name = name
        self.tasks = []

    def add(self, eng, emit, r=(), w=(), dma=False, fast=False):
        if emit.__name__ == "<lambda>" and eng in ("vector", "scalar"):
            fast = True
        self.tasks.append(Task(eng, emit, r, w, dma, fast))

    def t_dma(self, eng, out_ap, in_ap, r=(), w=(), **kw):
        def emit(e):
            return e.dma_start(out_ap, in_ap, **kw)
        self.add(eng, emit, r=r, w=w, dma=True)

    NDMA = 6

    def run(self):
        C = self.C
        nc = C.nc
        cnt = {}
        last_writer = C.last_writer
        readers = C.readers
        for t in self.tasks:
            if t.dma:
                k = C.dma_idx.get(t.eng, 0)
                C.dma_idx[t.eng] = k + 1
                slot = k % self.NDMA
                use = k // self.NDMA
                sname = f"{t.eng}D{slot}"
                t.prewait = (sname, 16 * use) if use > 0 else None
                t.done = (sname, 16 * (use + 1))
            else:
                key = (t.eng, t.dma)
                base = C.semval(self._sname(t.eng, t.dma))
                cnt[key] = cnt.get(key, 0) + 1
                t.seq = cnt[key]
                t.done = (self._sname(t.eng, t.dma), base + t.seq)
                t.prewait = None
        for t in self.tasks:
            waits = {}

            def need(dep, raw=False):
                if dep is None or dep is t:
                    return
                if (raw and dep.eng == t.eng and not dep.dma and not t.dma):
                    if dep.eng == "tensor":
                        return
                    if dep.eng in ("vector", "scalar") and not dep.fast:
                        return
                sname, val = dep.done
                waits[sname] = max(waits.get(sname, 0), val)

            for tok in t.reads:
                need(last_writer.get(tok), raw=True)
            for tok in t.writes:
                need(last_writer.get(tok))
                for rd in readers.get(tok, ()):
                    need(rd)
            t.waits = waits
            for tok in t.reads:
                readers.setdefault(tok, []).append(t)
            for tok in t.writes:
                last_writer[tok] = t
                readers[tok] = []
        with nc.Block(self.name) as blk:
            for eng in ENGINES:
                mytasks = [t for t in self.tasks if t.eng == eng]
                if not mytasks:
                    continue

                def body(e, mytasks=mytasks, eng=eng):
                    hwm = {}
                    for t in mytasks:
                        allw = dict(t.waits)
                        if t.prewait is not None:
                            sn, v = t.prewait
                            allw[sn] = max(allw.get(sn, 0), v)
                        for sname, val in sorted(allw.items()):
                            if hwm.get(sname, -1) >= val:
                                continue
                            e.wait_ge(C.sem(sname), val)
                            hwm[sname] = val
                        safe = (not t.dma) and (
                            C.race_detect or
                            (eng in ("vector", "scalar") and not t.fast))
                        ee = SafeEng(e, C, eng) if safe else e
                        ins = t.emit(ee)
                        sname, val = t.done
                        inc = 16 if t.dma else 1
                        if ins is None or safe:
                            e.sem_inc(C.sem(sname), inc)
                        else:
                            ins.then_inc(C.sem(sname), inc)
                        C.setsem(sname, val)

                getattr(blk, eng)(body)

    @staticmethod
    def _sname(eng, dma):
        return f"{eng}{'D' if dma else ''}"


class SafeEng:

    def __init__(self, eng, C, name):
        self._e = eng
        self._C = C
        self._n = name

    def __getattr__(self, name):
        attr = getattr(self._e, name)
        if not callable(attr) or name in ("wait_ge", "sem_inc", "nop",
                                          "dma_start", "load_library"):
            return attr

        def wrap(*a, **k):
            r = attr(*a, **k)
            C = self._C
            sn = "safe_" + self._n
            r.then_inc(C.sem(sn), 1)
            v = C.inc_safe(sn)
            self._e.wait_ge(C.sem(sn), v)
            return r
        return wrap


class Ctx:
    def __init__(self, dbg=(), race_detect=False):
        self.race_detect = race_detect
        self.dbg = set(dbg)
        self.nc = bass.Bass("TRN2", target_bir_lowering=False, debug=False,
                            detect_race_conditions=race_detect)
        self._sems = {}
        self._semn = {}
        self.dma_idx = {}
        self._safe = {}
        self.last_writer = {}
        self.readers = {}

    def inc_safe(self, name):
        self._safe[name] = self._safe.get(name, 0) + 1
        return self._safe[name]

    def sem(self, name):
        if name not in self._sems:
            self._sems[name] = self.nc.alloc_semaphore(name)
            self._semn[name] = 0
        return self._sems[name]

    def semval(self, name):
        self.sem(name)
        return self._semn[name]

    def setsem(self, name, v):
        self._semn[name] = max(self._semn[name], v)


def build_program(dbg=(), race_detect=False):
    C = Ctx(dbg, race_detect)
    nc = C.nc
    E = {}

    def dram(name, shape, dtype, kind="Internal"):
        k = "ExternalOutput" if name in C.dbg else kind
        E["d_" + name] = nc.dram_tensor(name, shape, dtype, kind=k).ap()

    for name, shape in [
        ("u_tok", [TTOT, DM]), ("w_in", [DM, 8384]),
        ("w_upT", [DM, DH]), ("w_dnT", [DH, DM]), ("w_out", [DI, DM]),
        ("cw", [128, 128]), ("cb", [128, 32]), ("nsc", [128, 32]),
        ("usc", [128, 64]), ("dsc", [128, 16]), ("vecs", [3, 64]),
        ("osc", [1, 1]), ("flag", [1, 1]),
    ]:
        E["d_" + name] = nc.dram_tensor(name, shape, F32, kind="ExternalInput").ap()
    E["d_out"] = nc.dram_tensor("out", [1024, DM], F32, kind="ExternalOutput").ap()

    dram("xnf", [16, 128, TTOT], BF16)
    dram("wb", [DM, 8384], BF16)
    dram("wub", [DM, DH], BF16)
    dram("wdb", [DH, DM], BF16)
    dram("wob", [DI, DM], BF16)
    dram("yg", [8, 128, DI], BF16)
    dram("zs", [8, 128, DI], BF16)
    dram("csd", [32, NH * 64], F32)
    dram("trow", [1, 2048], F32)
    dram("nrow", [1, 64], F32)
    if "stats" in C.dbg:
        E["d_pstat"] = nc.dram_tensor("pstat", [128, 6 * NT], F32,
                                      kind="ExternalOutput").ap()
    if "probe2" in C.dbg:
        for nm, shape in [("py", [128, DI]), ("patt", [128, NH * 128]),
                          ("pG", [128, 64]), ("pD", [128, NH * 64]),
                          ("phb", [64, DI]), ("pqs", [128, 64]),
                          ("pkd", [128, 128]), ("pvs", [128, DI])]:
            E["d_" + nm] = nc.dram_tensor(nm, shape, F32,
                                          kind="ExternalOutput").ap()
    if "probe" in C.dbg:
        E["d_ph0"] = nc.dram_tensor("ph0", [64, DI], F32,
                                    kind="ExternalOutput").ap()
        E["d_pdecp"] = nc.dram_tensor("pdecp", [128, 8 * 64], F32,
                                      kind="ExternalOutput").ap()
        E["d_pkq"] = nc.dram_tensor("pkq", [128, 2 * 16 * 64], F32,
                                    kind="ExternalOutput").ap()
        E["d_pvt"] = nc.dram_tensor("pvt", [128, 4 * DI], F32,
                                    kind="ExternalOutput").ap()

    A = nc.alloc_sbuf_tensor
    for nm, shape, dt in [
        ("causal", [128, 64], BF16), ("id_bf", [128, 128], BF16),
        ("id_f32", [128, 128], F32), ("cw_sb", [128, 128], F32),
        ("cb_sb", [128, 32], F32), ("nsc_sb", [128, 32], F32),
        ("usc_sb", [128, 64], F32), ("dsc_sb", [128, 16], F32),
        ("arow", [1, 64], F32), ("brow", [1, 64], F32), ("drow", [1, 64], F32),
        ("oscv", [1, 1], F32), ("flagv", [1, 1], F32), ("OSCB", [128, 1], F32),
        ("NEGA", [128, 64], F32), ("DTB", [128, 64], F32), ("DP", [128, 64], F32),
        ("FLG", [64, 1], F32),
        ("mu", [128, NT], F32), ("rstd", [128, NT], F32), ("nmr", [128, NT], F32),
        ("musum", [128, NT], F32), ("sum2", [128, NT], F32),
        ("stat_t", [128, NT], F32),
    ]:
        E[nm] = A(nm, shape, dt)

    import contextlib
    early_stack = contextlib.ExitStack()
    for nm, shape, dt in [
        ("ones", [128, 2048], F32), ("reset", [64, 2048], F32),
        ("dtv", [128, 16 * 64], F32), ("CS", [64, 2048], F32),
        ("GCS", [64, 1024], F32),
    ]:
        E[nm] = early_stack.enter_context(nc.sbuf_tensor(nm, shape, dt,
                                                         side="right"))

    _p0_consts(C, E)
    _p1_stats(C, E)
    _p2_xnf(C, E)

    ssd_stack = contextlib.ExitStack()
    for nm, shape, dt in [
        ("k_tok", [128, 16 * 64], BF16), ("q_tok", [128, 16 * 64], BF16),
        ("CS_T", [128, 16 * 64], F32), ("ECS_T", [128, 16 * 64], F32),
        ("DEC_T", [128, 16 * 64], F32), ("DECP_T", [128, 8 * 64], F32),
        ("DBC64", [64, 2048], F32), ("h_st", [64, DI], F32),
        ("vt", [128, 4 * DI], BF16),
        ("xtail", [128, 32 * 12], F32), ("sqs_cv", [128, 512], F32),
    ]:
        E[nm] = ssd_stack.enter_context(nc.sbuf_tensor(nm, shape, dt))

    _p3_bcdt(C, E)
    _p4_csfam(C, E)
    early_stack.close()
    _p5_z(C, E)
    _p6_main(C, E)
    ssd_stack.close()
    _p7_ffn_out(C, E)
    return nc


def _p0_consts(C, E):
    nc = C.nc
    P = Phase(C, "p0")
    for dst, src in [
        (E["cw_sb"][:, :], E["d_cw"]), (E["cb_sb"][:, :], E["d_cb"]),
        (E["nsc_sb"][:, :], E["d_nsc"]), (E["usc_sb"][:, :], E["d_usc"]),
        (E["dsc_sb"][:, :], E["d_dsc"]), (E["oscv"][:, :], E["d_osc"]),
        (E["flagv"][:, :], E["d_flag"]),
        (E["arow"][:, :], E["d_vecs"][0:1, :]),
        (E["brow"][:, :], E["d_vecs"][1:2, :]),
        (E["drow"][:, :], E["d_vecs"][2:3, :]),
    ]:
        nm = src.name if hasattr(src, "name") else "x"
        P.t_dma("sync", dst, src, w=[id(dst.tensor) if hasattr(dst, "tensor") else nm])
    P.tasks = []
    loads = [
        ("cw_sb", E["d_cw"]), ("cb_sb", E["d_cb"]), ("nsc_sb", E["d_nsc"]),
        ("usc_sb", E["d_usc"]), ("dsc_sb", E["d_dsc"]), ("oscv", E["d_osc"]),
        ("flagv", E["d_flag"]),
    ]
    for nm, src in loads:
        P.t_dma("sync", E[nm][:, :], src, w=[nm])
    P.t_dma("sync", E["arow"][:, :], E["d_vecs"][0:1, :], w=["arow"])
    P.t_dma("sync", E["brow"][:, :], E["d_vecs"][1:2, :], w=["brow"])
    P.t_dma("sync", E["drow"][:, :], E["d_vecs"][2:3, :], w=["drow"])

    P.add("scalar", lambda e: e.activation(E["arow"][:, :], E["arow"][:, :], AF.Exp),
          r=["arow"], w=["arow"])
    P.add("vector", lambda e: e.tensor_scalar_mul(E["arow"][:, :], E["arow"][:, :], -1.0),
          r=["arow"], w=["arow"])
    P.t_dma("gpsimd", E["d_nrow"], E["arow"][:, :], r=["arow"],
            w=["nrow_dram"])

    P.add("gpsimd", lambda e: e.memset(E["ones"][:, :], 1.0), w=["ones"])
    P.add("gpsimd", lambda e: e.affine_select(
        E["reset"][:, :], E["ones"][0:64, :], [[0, 32], [1, 64]], OP.is_gt, 0.0,
        base=0, channel_multiplier=0), r=["ones"], w=["reset"])
    P.add("gpsimd", lambda e: e.affine_select(
        E["causal"][0:64, :], E["ones"][0:64, 0:64], [[1, 64]], OP.is_ge, 0.0,
        base=0, channel_multiplier=-1), r=["ones"], w=["causal"])
    P.add("gpsimd", lambda e: e.affine_select(
        E["causal"][64:128, :], E["ones"][0:64, 0:64], [[1, 64]], OP.is_ge, 0.0,
        base=0, channel_multiplier=-1), r=["ones"], w=["causal"])
    P.add("gpsimd", lambda e: e.affine_select(
        E["id_bf"][:, :], E["ones"][:, 0:128], [[-1, 128]], OP.is_equal, 0.0,
        base=0, channel_multiplier=1), r=["ones"], w=["id_bf"])
    P.add("gpsimd", lambda e: e.affine_select(
        E["id_f32"][:, :], E["ones"][:, 0:128], [[-1, 128]], OP.is_equal, 0.0,
        base=0, channel_multiplier=1), r=["ones"], w=["id_f32"])
    P.t_dma("gpsimd", E["NEGA"][:, :], E["d_nrow"].partition_broadcast(128),
            r=["nrow_dram"], w=["NEGA"])
    P.t_dma("gpsimd", E["DTB"][:, :],
            E["d_vecs"][1:2, :].partition_broadcast(128), w=["DTB"])
    P.t_dma("gpsimd", E["DP"][:, :],
            E["d_vecs"][2:3, :].partition_broadcast(128), w=["DP"])
    P.t_dma("gpsimd", E["FLG"][:, :], E["d_flag"].partition_broadcast(64),
            w=["FLG"])
    P.t_dma("gpsimd", E["OSCB"][:, :], E["d_osc"].partition_broadcast(128),
            w=["OSCB"])
    P.add("vector", lambda e: e.tensor_scalar_mul(
        E["dsc_sb"][:, :], E["dsc_sb"][:, :], E["OSCB"][:, 0:1]),
        r=["OSCB", "dsc_sb"], w=["dsc_sb"])
    P.t_dma("gpsimd", E["d_wb"][:, 8192:8384], E["d_w_in"][:, 8192:8384],
            w=[("wcv", "b")])
    P.run()


def _p1_stats(C, E):
    nc = C.nc
    with nc.sbuf_tensor("UT1", [128, 2 * DM], F32) as UT, \
         nc.sbuf_tensor("SQS1", [128, DM], F32) as SQS:
        P = Phase(C, "p1")
        for i in range(NT):
            buf = UT[:, (i % 2) * DM:(i % 2 + 1) * DM]
            P.t_dma("sync", buf, E["d_u_tok"][i * 128:(i + 1) * 128, :],
                    w=[("ut", i % 2)])
            P.add("vector", lambda e, b=buf, i=i: e.reduce_sum(
                E["musum"][:, i:i + 1], b, axis=AX.X), r=[("ut", i % 2)],
                w=["musum"])
            P.add("scalar", lambda e, b=buf, i=i: e.activation(
                SQS[:, :], b, AF.Square, accum_out=E["sum2"][:, i:i + 1]),
                r=[("ut", i % 2)], w=["sum2"])
        P.run()

    P = Phase(C, "p1b")

    def v1(e):
        MU, RS, NM, T = E["mu"], E["rstd"], E["nmr"], E["stat_t"]
        e.tensor_scalar_mul(MU[:, :], E["musum"][:, :], 1.0 / DM)
        e.tensor_scalar_mul(T[:, :], E["sum2"][:, :], 1.0 / DM)
        e.tensor_mul(RS[:, :], MU[:, :], MU[:, :])
        e.tensor_sub(T[:, :], T[:, :], RS[:, :])
        return e.tensor_scalar_add(T[:, :], T[:, :], 1e-5)
    P.add("vector", v1, r=["musum", "sum2"], w=["mu", "vareps"])
    P.add("scalar", lambda e: e.activation(E["rstd"][:, :], E["stat_t"][:, :],
                                           AF.Sqrt), r=["vareps"], w=["sq"])

    def v2(e):
        RS, T, MU, NM = E["rstd"], E["stat_t"], E["mu"], E["nmr"]
        R0, S2 = E["musum"], E["sum2"]
        e.reciprocal(R0[:, :], RS[:, :])
        e.tensor_mul(S2[:, :], R0[:, :], R0[:, :])
        e.tensor_mul(S2[:, :], S2[:, :], T[:, :])
        e.tensor_scalar(S2[:, :], S2[:, :], -0.5, 1.5, OP.mult, OP.add)
        e.tensor_mul(RS[:, :], R0[:, :], S2[:, :])
        e.tensor_mul(NM[:, :], MU[:, :], RS[:, :])
        return e.tensor_scalar_mul(NM[:, :], NM[:, :], -1.0)
    P.add("vector", v2, r=["sq", "mu", "vareps"], w=["rstd", "musum", "sum2"])
    if "stats" in C.dbg:
        for i, nm in enumerate(["mu", "rstd", "nmr", "musum", "sum2",
                                "stat_t"]):
            P.t_dma("gpsimd", E["d_pstat"][:, i * NT:(i + 1) * NT],
                    E[nm][:, :], r=["rstd", "mu"], w=["pst"])
        P.add("sync", lambda e: None, r=["pst"])
    P.run()


def _p2_xnf(C, E):
    nc = C.nc
    groups = [(0, 4), (4, 4), (8, 4), (12, 4), (16, 1)]
    with nc.sbuf_tensor("UT2", [128, 2 * DM], F32) as UT, \
         nc.sbuf_tensor("XT2", [128, 2 * DM], BF16) as XT, \
         nc.sbuf_tensor("XSL2", [128, 2 * 16 * 512], BF16) as XSL, \
         nc.psum_tensor("ps2a", [128, 1024], BF16) as PSA, \
         nc.psum_tensor("ps2b", [128, 1024], BF16) as PSB:
        P = Phase(C, "p2")
        XSL4 = XSL.rearrange("p (r k t) -> p r k t", r=2, k=16)
        for gi, (t0, gn) in enumerate(groups):
            for j in range(gn):
                i = t0 + j
                n = gi * 4 + j
                pr = n % 2
                ub = UT[:, pr * DM:(pr + 1) * DM]
                xb = XT[:, pr * DM:(pr + 1) * DM]
                ps = PSA if pr == 0 else PSB
                P.t_dma("sync", ub, E["d_u_tok"][i * 128:(i + 1) * 128, :],
                        w=[("u2", pr)])
                P.add("scalar", lambda e, ub=ub, xb=xb, i=i: e.activation(
                    xb, ub, AF.Identity, bias=E["nmr"][:, i:i + 1],
                    scale=E["rstd"][:, i:i + 1]),
                    r=[("u2", pr), "rstd"], w=[("xn", pr)])
                for half in range(2):
                    def trh(e, xb=xb, ps=ps, half=half):
                        ins = None
                        for k in range(half * 8, half * 8 + 8):
                            ins = e.transpose(
                                ps[:, (k % 8) * 128:(k % 8) * 128 + 128],
                                xb[:, k * 128:(k + 1) * 128], E["id_bf"][:, :])
                        return ins
                    P.add("tensor", trh, r=[("xn", pr), "id_bf"],
                          w=[("tph", pr)])
                    dst = XSL4[:, gi % 2, half * 8:(half + 1) * 8,
                               j * 128:(j + 1) * 128]
                    P.add("vector", lambda e, dst=dst, ps=ps: e.tensor_copy(
                        dst, r3(ps[:, :], 8, 128)),
                        r=[("tph", pr)],
                        w=[("slab", gi % 2)])
            dst = E["d_xnf"].rearrange("k d t -> d k t"
                                       )[:, :, t0 * 128:t0 * 128 + gn * 128]
            P.t_dma("scalar", dst, XSL4[:, gi % 2, :, 0:gn * 128],
                    r=[("slab", gi % 2)], w=[("xnfd", gi), ("slab", gi % 2)])
            P2W = {0: [("z", 0), ("z", 1)], 1: [("z", 2), ("z", 3)],
                   2: [("z", 4), ("z", 5)], 3: [("z", 6), ("z", 7)],
                   4: [("x", 0), ("x", 1)]}
            for fam, ci in P2W.get(gi, []):
                off = 4096 if fam == "x" else 0
                sl = slice(off + ci * 512, off + (ci + 1) * 512)
                P.t_dma("gpsimd", E["d_wb"][:, sl], E["d_w_in"][:, sl],
                        r=[("xnfd", gi)], w=[("wcv", fam, ci)])
        P.add("sync", lambda e: None,
              r=[("xnfd", g2) for g2 in range(len(groups))], w=["xnf_ready"])
        P.run()


def _p3_bcdt(C, E):
    nc = C.nc
    with nc.sbuf_tensor("WBCDT", [128, 16 * 192], BF16) as W, \
         nc.sbuf_tensor("XST3", [128, 2 * 16 * 512], BF16) as XST, \
         nc.sbuf_tensor("p3T1", [128, 2 * 64], F32) as T1, \
         nc.sbuf_tensor("p3T2", [128, 2 * 64], F32) as T2, \
         nc.sbuf_tensor("p3T3", [128, 2 * 64], F32) as T3, \
         nc.sbuf_tensor("p3MSK", [128, 2 * 64], mybir.dt.int32) as MSK, \
         nc.psum_tensor("ps3a", [128, 192], F32) as PA, \
         nc.psum_tensor("ps3b", [128, 192], F32) as PB:
        P = Phase(C, "p3")
        W3 = r3(W[:, :], 16, 192)
        XST4 = XST.rearrange("p (r k t) -> p r k t", r=2, k=16)
        P.t_dma("sync", W3,
                E["d_wb"].rearrange("(k p) c -> p k c", p=128)[:, :, 8192:8384],
                r=[("wcv", "b")], w=["W3"])
        for g in range(4):
            src = E["d_xnf"].rearrange("k d t -> d k t"
                                       )[:, :, 128 + g * 512:128 + (g + 1) * 512]
            P.t_dma("sync", XST4[:, g % 2], src,
                    r=[("xst_used", g - 2), "xnf_ready"], w=[("xst", g)])
            if g >= 1:
                ci = g + 1
                sl = slice(4096 + ci * 512, 4096 + (ci + 1) * 512)
                P.t_dma("gpsimd", E["d_wb"][:, sl], E["d_w_in"][:, sl],
                        r=[("xst", g)], w=[("wcv", "x", ci)])
            for j in range(4):
                idx = g * 4 + j
                par = idx % 2
                ps = PA if par == 0 else PB

                def mm(e, g=g, j=j, ps=ps):
                    ins = None
                    for k in range(16):
                        ins = e.matmul(ps[:, :], XST4[:, g % 2, k,
                                                      j * 128:(j + 1) * 128],
                                       W3[:, k, :], start=(k == 0),
                                       stop=(k == 15))
                    return ins
                P.add("tensor", mm, r=[("xst", g), "W3", ("ps3r", par)],
                      w=[("ps3", par), ("xst_used", g) if j == 3 else ("xju", g, j)])

                for col, dst in ((0, "k_tok"), (64, "q_tok")):
                    P.add("vector", lambda e, ps=ps, col=col, par=par:
                          e.tensor_scalar_min(T1[:, par * 64:(par + 1) * 64],
                                              ps[:, col:col + 64], 0.0),
                          r=[("ps3", par)], w=[("T1", par)])
                    P.add("scalar", lambda e, par=par: e.activation(
                        T2[:, par * 64:(par + 1) * 64],
                        T1[:, par * 64:(par + 1) * 64], AF.Exp),
                        r=[("T1", par)], w=[("T2", par)])
                    P.add("vector", lambda e, ps=ps, col=col, par=par, dst=dst,
                          idx=idx: e.scalar_tensor_tensor(
                              E[dst][:, idx * 64:(idx + 1) * 64],
                              ps[:, col:col + 64], 0.0,
                              T2[:, par * 64:(par + 1) * 64], OP.max, OP.add),
                          r=[("ps3", par), ("T2", par)], w=[dst, ("T1", par)])

                def vdt(e, ps=ps, par=par):
                    e.tensor_add(T1[:, par * 64:(par + 1) * 64],
                                 ps[:, 128:192], E["DTB"][:, 0:64])
                    e.tensor_scalar_min(T3[:, par * 64:(par + 1) * 64],
                                        T1[:, par * 64:(par + 1) * 64], 20.0)
                    return e.tensor_single_scalar(
                        MSK[:, par * 64:(par + 1) * 64],
                        T1[:, par * 64:(par + 1) * 64], 20.0, OP.is_gt)
                P.add("vector", vdt, r=[("ps3", par), "DTB"],
                      w=[("T1", par), ("T3", par), ("MSK", par)])

                def sdt(e, par=par):
                    e.activation(T2[:, par * 64:(par + 1) * 64],
                                 T3[:, par * 64:(par + 1) * 64], AF.Exp)
                    return e.activation(T2[:, par * 64:(par + 1) * 64],
                                        T2[:, par * 64:(par + 1) * 64],
                                        AF.Ln, bias=1.0)
                P.add("scalar", sdt, r=[("T3", par)], w=[("T2", par)])
                P.add("vector", lambda e, par=par, idx=idx: e.select(
                    E["dtv"][:, idx * 64:(idx + 1) * 64],
                    MSK[:, par * 64:(par + 1) * 64],
                    T1[:, par * 64:(par + 1) * 64],
                    T2[:, par * 64:(par + 1) * 64], add_drain=True),
                    r=[("T2", par), ("T1", par), ("MSK", par)],
                    w=["dtv", ("ps3r", par)])
        P.run()


def _p4_csfam(C, E):
    nc = C.nc
    with nc.sbuf_tensor("p4XX", [64, 2048], F32) as XX, \
         nc.sbuf_tensor("p4YY", [64, 1024], F32) as YY, \
         nc.sbuf_tensor("p4TOT", [64, 32], F32) as TOT, \
         nc.sbuf_tensor("p4TOTT", [32, 64], F32) as TOTT, \
         nc.psum_tensor("ps4a", [128, 512], F32) as PSA, \
         nc.psum_tensor("ps4b", [128, 512], F32) as PSB:
        P = Phase(C, "p4")
        P.add("vector", lambda e: e.tensor_tensor(
            r3(E["dtv"][:, :], 16, 64), r3(E["dtv"][:, :], 16, 64),
            E["NEGA"][:, :].unsqueeze(1).broadcast_to([128, 16, 64]), OP.mult),
            r=["dtv", "NEGA"], w=["dtv"])
        for g in range(4):
            ps = PSA if g % 2 == 0 else PSB

            def trs(e, g=g, ps=ps):
                ins = None
                for i in range(4):
                    ins = e.transpose(ps[0:64, i * 128:(i + 1) * 128],
                                      E["dtv"][:, (g * 4 + i) * 64:(g * 4 + i + 1) * 64],
                                      E["id_f32"][:, :])
                return ins
            P.add("tensor", trs, r=["dtv", "id_f32", ("ps4r", g % 2)],
                  w=[("ps4", g % 2)])
            P.add("vector", lambda e, g=g, ps=ps: e.tensor_copy(
                XX[:, g * 512:(g + 1) * 512], ps[0:64, 0:512]),
                r=[("ps4", g % 2)], w=["LAH", ("ps4r", g % 2)])

        def scans(e):
            e.tensor_tensor_scan(E["CS"][:, :], E["reset"][:, :], XX[:, :],
                                 0.0, OP.mult, OP.add)
            e.tensor_tensor_scan(E["GCS"][:, :], E["ones"][0:64, 0:1024],
                                 XX[:, 0:1024], 0.0, OP.mult, OP.add)
            e.tensor_copy(TOT[:, :],
                          E["CS"].rearrange("h (c s) -> h c s", c=32)[:, :, 63])
            e.tensor_tensor(r3(XX[:, :], 32, 64), bfree(TOT[:, :], 64),
                            r3(E["CS"][:, :], 32, 64), OP.subtract)
            return e.tensor_tensor(
                YY[:, :].unsqueeze(1),
                bfree(E["GCS"][:, 1023:1024], 1024).transpose([0, 2, 1]),
                E["GCS"][:, :].unsqueeze(1), OP.subtract)
        P.add("vector", scans, r=["LAH", "reset", "ones"],
              w=["CS", "GCS", "TOT", "XXd", "YYd", "LAH"])

        def exps(e):
            e.activation(XX[:, :], XX[:, :], AF.Exp)
            e.activation(YY[:, :], YY[:, :], AF.Exp)
            return e.activation(TOT[:, :], TOT[:, :], AF.Exp)
        P.add("scalar", exps, r=["XXd", "YYd", "TOT"],
              w=["XXd", "YYd", "TOT"])

        jobs = [(E["CS"], "CS_T", 16, "CS"), (XX, "DEC_T", 16, "XXd"),
                (YY, "DECP_T", 8, "YYd")]
        gidx = 0
        for src, dstn, cnt, tok in jobs:
            for g in range((cnt + 3) // 4):
                take = min(4, cnt - g * 4)
                ps = PSA if gidx % 2 == 0 else PSB

                def trs2(e, src=src, g=g, take=take, ps=ps):
                    ins = None
                    for i in range(take):
                        ins = e.transpose(ps[:, i * 64:(i + 1) * 64],
                                          src[:, (g * 4 + i) * 128:(g * 4 + i + 1) * 128],
                                          E["id_f32"][0:64, 0:64])
                    return ins
                P.add("tensor", trs2, r=[tok, "id_f32", ("ps4r", gidx % 2)],
                      w=[("ps4", gidx % 2)])
                dst = E[dstn][:, (g * 4) * 64:(g * 4) * 64 + take * 64]
                P.add("vector", lambda e, dst=dst, ps=ps, take=take:
                      e.tensor_copy(dst, ps[:, 0:take * 64]),
                      r=[("ps4", gidx % 2)], w=[dstn, ("ps4r", gidx % 2)])
                gidx += 1
        P.add("tensor", lambda e: e.transpose(PSA[0:32, 0:64], TOT[:, :],
                                              E["id_f32"][0:64, 0:64]),
              r=["TOT", "id_f32", ("ps4r", 0)], w=[("ps4", 0)])
        P.add("vector", lambda e: e.tensor_copy(TOTT[:, :], PSA[0:32, 0:64]),
              r=[("ps4", 0)], w=["TOTT", ("ps4r", 0)])
        P.add("scalar", lambda e: e.activation(E["ECS_T"][:, :], E["CS_T"][:, :],
                                               AF.Exp), r=["CS_T"], w=["ECS_T"])
        for ci in (5, 6, 7):
            sl = slice(4096 + ci * 512, 4096 + (ci + 1) * 512)
            P.t_dma("gpsimd", E["d_wb"][:, sl], E["d_w_in"][:, sl],
                    r=["CS"], w=[("wcv", "x", ci)])
        P.t_dma("scalar", E["d_csd"].rearrange("c (h s) -> h c s", h=64),
                r3(E["CS"][:, :], 32, 64), r=["CS"], w=["csd_dram"])
        P.t_dma("scalar", E["d_trow"].rearrange("o (c h) -> (o c) h", c=32),
                TOTT[:, :], r=["TOTT"], w=["trow_dram"])
        P.t_dma("gpsimd", E["DBC64"][:, :],
                E["d_trow"].partition_broadcast(64), r=["trow_dram"],
                w=["DBC64"])
        P.add("sync", lambda e: None, r=["csd_dram"])
        P.run()


def _p5_z(C, E):
    nc = C.nc
    with nc.sbuf_tensor("WZ5", [128, 2 * 16 * 512], BF16) as WZ, \
         nc.sbuf_tensor("XST5", [128, 2 * 16 * 512], BF16) as XST, \
         nc.sbuf_tensor("ZR5", [128, 4 * 512], BF16) as ZR, \
         nc.psum_tensor("ps5a", [128, 512], F32) as PA, \
         nc.psum_tensor("ps5b", [128, 512], F32) as PB:
        P = Phase(C, "p5")
        WZ4 = WZ.rearrange("p (r k c) -> p r k c", r=2, k=16)
        XST4 = XST.rearrange("p (r k t) -> p r k t", r=2, k=16)
        for tg in range(2):
            src = E["d_xnf"].rearrange("k d t -> d k t"
                                       )[:, :, 1152 + tg * 512:1152 + (tg + 1) * 512]
            P.t_dma("sync", XST4[:, tg], src,
                    r=["xnf_ready"], w=[("xst5", tg)])
        for zb in range(8):
            P.t_dma("sync", WZ4[:, zb % 2],
                    E["d_wb"].rearrange("(k p) c -> p k c", p=128
                                        )[:, :, zb * 512:(zb + 1) * 512],
                    r=[("wzu", zb - 2), ("wcv", "z", zb)], w=[("wz", zb)])
            for tg in range(2):
                for j in range(4):
                    idx = zb * 8 + tg * 4 + j
                    par = idx % 2
                    ps = PA if par == 0 else PB

                    def mm(e, zb=zb, tg=tg, j=j, ps=ps):
                        ins = None
                        for k in range(16):
                            ins = e.matmul(ps[:, :],
                                           XST4[:, tg, k, j * 128:(j + 1) * 128],
                                           WZ4[:, zb % 2, k, :],
                                           start=(k == 0), stop=(k == 15))
                        return ins
                    P.add("tensor", mm,
                          r=[("xst5", tg), ("wz", zb), ("ps5r", par)],
                          w=[("ps5", par)] + ([("wzu", zb)] if (tg == 1 and j == 3) else []))
                    P.add("scalar", lambda e, ps=ps, idx=idx: e.activation(
                        ZR[:, (idx % 4) * 512:(idx % 4 + 1) * 512], ps[:, :],
                        AF.Silu),
                        r=[("ps5", par), ("zru", idx - 3)],
                        w=[("zr", idx), ("ps5r", par)])
                    ti = tg * 4 + j
                    P.t_dma("scalar", E["d_zs"][ti, :, zb * 512:(zb + 1) * 512],
                            ZR[:, (idx % 4) * 512:(idx % 4 + 1) * 512],
                            r=[("zr", idx)], w=[("zru", idx), ("zsd", zb, ti)])
        P.run()


CONV_LEVEL = 99
CONV_SKIP = ""
CONV_NMG = 8

XBLOCKS = [(0, 512, (1, 2, 3)), (512, 512, (4, 5, 6, 7)),
           (1024, 512, (8, 9, 10, 11)), (1536, 512, (12, 13, 14, 15)),
           (2048, 128, (16,))]


def _p6_conv_blk(C, E, bi):
    nc = C.nc
    a, w, tiles = XBLOCKS[bi]
    with nc.sbuf_tensor(f"XMV6_{bi}", [128, 16 * 512], BF16) as XMV, \
         nc.sbuf_tensor(f"WXG6_{bi}", [128, 2 * 16 * 512], BF16) as WXG, \
         nc.sbuf_tensor(f"XJ6_{bi}", [128, 2 * 4 * 516], F32) as XJ, \
         nc.psum_tensor(f"ps6xa_{bi}", [128, 512], F32) as PXA, \
         nc.psum_tensor(f"ps6xb_{bi}", [128, 512], F32) as PXB, \
         nc.psum_tensor(f"ps6ca_{bi}", [128, 512], F32) as PCA, \
         nc.psum_tensor(f"ps6cb_{bi}", [128, 512], F32) as PCB:
        P = Phase(C, f"cv{bi}")
        WXG4 = WXG.rearrange("p (r k c) -> p r k c", r=2, k=16)
        XJ4 = XJ.rearrange("p (r j c) -> p r j c", r=2, j=4)
        P.t_dma("sync", r3(XMV[:, 0:16 * w], 16, w),
                E["d_xnf"].rearrange("k d t -> d k t")[:, :, a:a + w],
                r=["xnf_ready"], w=["XMV"])
        CVW = {1: [("u", 0), ("u", 1), ("u", 2), ("u", 3)],
               2: [("u", 4), ("u", 5), ("u", 6), ("u", 7)],
               3: [("d", 3), ("d", 4), ("d", 5), ("d", 6)]}
        for j, (fam, ci) in enumerate(CVW.get(bi, [])):
            st_, dt_, rows = {"u": (E["d_w_upT"], E["d_wub"], 256),
                              "d": (E["d_w_dnT"], E["d_wdb"], 1024)}[fam]
            sl = slice(ci * rows, (ci + 1) * rows)
            P.t_dma("gpsimd", dt_[sl, :], st_[sl, :],
                    r=[("v", tiles[j])], w=[("wcv", fam, ci)])
        for mg in range(CONV_NMG):
            P.t_dma("sync", WXG4[:, mg % 2],
                    E["d_wb"].rearrange("(k p) c -> p k c", p=128
                                        )[:, :, 4096 + mg * 512:4096 + (mg + 1) * 512],
                    r=[("wcv", "x", mg)], w=[("wx", mg % 2)])
            for mm in range(4):
                m = mg * 4 + mm
                par = m % 2
                psx = PXA if par == 0 else PXB
                psc = PCA if par == 0 else PCB

                def mmx(e, mg=mg, mm=mm, psx=psx):
                    ins = None
                    for k in range(16):
                        ins = e.matmul(psx[:, 0:w], WXG4[:, mg % 2, k,
                                                        mm * 128:(mm + 1) * 128],
                                       r3(XMV[:, 0:16 * w], 16, w)[:, k, :],
                                       start=(k == 0), stop=(k == 15))
                    return ins
                P.add("tensor", mmx, r=[("wx", mg % 2), "XMV"],
                      w=[("psx", par)])

                if bi > 0:
                    P.add("vector", lambda e, m=m, par=par: e.tensor_copy(
                        XJ4[:, par, :, 0:3],
                        E["xtail"][:, m * 12:(m + 1) * 12].rearrange(
                            "p (j c) -> p j c", j=4)),
                        r=[("xtail", m)], w=[("xj0", par)])

                if CONV_LEVEL < 2:
                    continue

                def cpj01(e, m=m, par=par, psx=psx):
                    if CONV_SKIP == "act1":
                        return e.activation(XJ4[:, par, 0, 3:3 + w], psx[:, 0:w],
                                            AF.Identity)
                    e.activation(XJ4[:, par, 0, 3:3 + w], psx[:, 0:w], AF.Identity,
                                 bias=E["cb_sb"][:, m:m + 1],
                                 scale=E["cw_sb"][:, 4 * m:4 * m + 1])
                    return e.activation(XJ4[:, par, 1, 3:3 + w], psx[:, 0:w],
                                        AF.Identity,
                                        scale=E["cw_sb"][:, 4 * m + 1:4 * m + 2])
                P.add("scalar", cpj01, r=[("psx", par), "cw_sb", "cb_sb"],
                      w=[("xj1", par)], fast=True)

                def cpj23(e, m=m, par=par, psx=psx):
                    if CONV_SKIP == "dve_scr":
                        e.tensor_tensor(
                            E["sqs_cv"][:, 0:w], psx[:, 0:w],
                            E["cw_sb"][:, 4 * m + 2:4 * m + 3
                                       ].broadcast_to([128, w]), OP.mult)
                        return e.tensor_tensor(
                            E["sqs_cv"][:, 0:w], psx[:, 0:w],
                            E["cw_sb"][:, 4 * m + 3:4 * m + 4
                                       ].broadcast_to([128, w]), OP.mult)
                    if CONV_SKIP == "dve_sbsrc":
                        e.tensor_tensor(
                            XJ4[:, par, 2, 3:3 + w], E["sqs_cv"][:, 0:w],
                            E["cw_sb"][:, 4 * m + 2:4 * m + 3
                                       ].broadcast_to([128, w]), OP.mult)
                        return e.tensor_tensor(
                            XJ4[:, par, 3, 3:3 + w], E["sqs_cv"][:, 0:w],
                            E["cw_sb"][:, 4 * m + 3:4 * m + 4
                                       ].broadcast_to([128, w]), OP.mult)
                    e.tensor_tensor(
                        XJ4[:, par, 2, 3:3 + w], psx[:, 0:w],
                        E["cw_sb"][:, 4 * m + 2:4 * m + 3
                                   ].broadcast_to([128, w]), OP.mult)
                    return e.tensor_tensor(
                        XJ4[:, par, 3, 3:3 + w], psx[:, 0:w],
                        E["cw_sb"][:, 4 * m + 3:4 * m + 4
                                   ].broadcast_to([128, w]), OP.mult)
                if CONV_SKIP != "dve":
                    P.add("vector", cpj23, r=[("psx", par), ("xj1", par),
                                              "cw_sb"],
                          w=[("xj2", par)], fast=True)
                else:
                    P.add("vector", lambda e, par=par: e.engine_nop(),
                          w=[("xj2", par)])

                if CONV_LEVEL >= 3:
                    P.add("vector", lambda e, m=m, par=par: e.tensor_copy(
                        E["xtail"][:, m * 12:(m + 1) * 12].rearrange(
                            "p (j c) -> p j c", j=4), XJ4[:, par, :, w:w + 3]),
                        r=[("xj1", par), ("xj2", par)], w=[("xtail", m)])

                def trc(e, m=m, par=par, psc=psc, tiles=tiles, a=a):
                    ins = None
                    first = True
                    for wl, ti in enumerate(tiles):
                        for j in range(4):
                            col = 128 * ti + j - a
                            ins = e.matmul(psc[:, wl * 128:(wl + 1) * 128],
                                           XJ4[:, par, j, col:col + 128],
                                           E["id_f32"][:, :],
                                           start=first, stop=False,
                                           is_transpose=True,
                                           skip_group_check=True)
                            first = False
                    return ins
                rd = [("xj1", par), ("xj2", par), "id_f32"]
                if bi > 0:
                    rd.append(("xj0", par))
                if CONV_LEVEL >= 4:
                    P.add("tensor", trc, r=rd, w=[("psc", par)])

                def sil(e, m=m, par=par, psc=psc, tiles=tiles):
                    ins = None
                    for wl, ti in enumerate(tiles):
                        slot = ti % 4
                        ins = e.activation(
                            E["vt"][:, slot * DI + m * 128:slot * DI + (m + 1) * 128],
                            psc[:, wl * 128:(wl + 1) * 128], AF.Silu)
                    return ins
                if CONV_LEVEL >= 5:
                    P.add("scalar", sil, r=[("psc", par)],
                          w=[("v", ti) for ti in tiles])
        P.run()


def _p6_prefix(C, E, pairs, first, last):
    nc = C.nc
    with nc.psum_tensor(f"ps6h_{pairs[0]}", [64, DI], F32) as PH, \
         nc.sbuf_tensor(f"vscl_p{pairs[0]}", [128, DI], BF16) as VSCL:
        E["vscl"] = VSCL
        P = Phase(C, f"pre{pairs[0]}")
        if first:
            P.add("vector", lambda e: e.memset(E["h_st"][:, :], 0.0),
                  w=["h_st"])
        for p in pairs:
            ti = p + 1
            P.add("vector", lambda e, p=p, ti=ti: e.tensor_tensor(
                r3(E["vscl"][:, :], 64, 64),
                r3(E["vt"][:, (ti % 4) * DI:(ti % 4 + 1) * DI], 64, 64),
                bfree(E["DECP_T"][:, p * 64:(p + 1) * 64], 64), OP.mult),
                r=[("v", ti), "DECP_T"], w=["vscl"])
            for b in range(8):
                P.add("tensor", lambda e, p=p, b=b: e.matmul(
                    PH[:, b * 512:(b + 1) * 512],
                    E["k_tok"][:, p * 64:(p + 1) * 64],
                    E["vscl"][:, b * 512:(b + 1) * 512],
                    start=(p == pairs[0]), stop=(p == pairs[-1]),
                    skip_group_check=True),
                    r=["vscl", "k_tok"], w=[("ph", b)])
        for b in range(8):
            P.add("vector", lambda e, b=b: e.tensor_add(
                E["h_st"][:, b * 512:(b + 1) * 512],
                E["h_st"][:, b * 512:(b + 1) * 512], PH[:, b * 512:(b + 1) * 512]),
                r=[("ph", b), "h_st"], w=["h_st"])
        if last:
            P.add("vector", lambda e: e.tensor_scalar_mul(
                E["h_st"][:, :], E["h_st"][:, :], E["FLG"][:, 0:1]),
                r=["h_st", "FLG"], w=["h_st"])
        if last and "probe" in C.dbg:
            P.t_dma("gpsimd", E["d_ph0"], E["h_st"][:, :], r=["h_st"],
                    w=["prb"])
            P.t_dma("gpsimd", E["d_pdecp"], E["DECP_T"][:, :], r=["DECP_T"],
                    w=["prb"])

            def cast2(e):
                e.tensor_copy(E["vscl"][:, 0:1024].bitcast(F32),
                              E["k_tok"][:, :].bitcast(mybir.dt.uint16
                                                       ).bitcast(BF16)) \
                    if False else None
                return None
            P.t_dma("gpsimd", E["d_pkq"].bitcast(BF16)[:, 0:16 * 64],
                    E["k_tok"][:, :], r=["k_tok"], w=["prb"])
            P.t_dma("gpsimd", E["d_pkq"].bitcast(BF16)[:, 16 * 64:2 * 16 * 64],
                    E["q_tok"][:, :], r=["q_tok"], w=["prb"])
            P.t_dma("gpsimd", E["d_pvt"].bitcast(BF16)[:, 0:4 * DI],
                    E["vt"][:, :], r=[("v", 8)], w=["prb"])
            P.add("sync", lambda e: None, r=["prb"])
        P.run()


def _p6_ssd(C, E, pairs):
    nc = C.nc
    import contextlib
    st = contextlib.ExitStack()
    sfx = f"_s{pairs[0]}"
    for nm, shape, dt in [
        ("h_bf", [64, DI], BF16), ("h_bf2", [64, DI], BF16),
        ("kdiag", [128, 128], BF16), ("qstack", [128, 64], BF16),
        ("qflat_o", [64, 64], BF16), ("att", [128, NH * 128], BF16),
        ("Dmask", [128, NH * 64], BF16), ("CSB", [128, NH * 64], F32),
        ("G_sb", [128, 64], BF16), ("vscl", [128, DI], BF16),
        ("y_sl", [128, DI], BF16), ("zs_sl", [128, DI], BF16),
        ("yg_sl", [128, DI], BF16), ("YEV", [128, 512], F32),
        ("YEV2", [128, 512], F32), ("DVS", [128, 512], BF16),
        ("sqc", [128, 8], F32), ("rstd_y", [128, 1], F32),
        ("nwt", [128, 1], F32), ("sqs", [128, 512], F32),
    ]:
        E[nm] = st.enter_context(nc.sbuf_tensor(nm + sfx, shape, dt))
    with nc.psum_tensor(f"ps6sa{sfx}", [64, 512], F32) as PSa, \
         nc.psum_tensor(f"ps6sb{sfx}", [64, 512], F32) as PSb, \
         nc.psum_tensor(f"ps6Aa{sfx}", [128, 512], F32) as PAa, \
         nc.psum_tensor(f"ps6Ab{sfx}", [128, 512], F32) as PAb, \
         nc.psum_tensor(f"ps6Ba{sfx}", [128, 512], F32) as PBa, \
         nc.psum_tensor(f"ps6Bb{sfx}", [128, 512], F32) as PBb, \
         nc.psum_tensor(f"ps6T{sfx}", [128, 1024], BF16) as PT, \
         nc.psum_tensor(f"ps6G{sfx}", [128, 64], F32) as PG:
        P = Phase(C, f"ssd{pairs[0]}")
        SSW = {8: [("d", 0), ("d", 1), ("d", 2)],
               11: [("d", 7), ("o", 0), ("o", 1), ("o", 2)],
               15: [("o", 3)]}
        for j, (fam, ci) in enumerate(SSW.get(pairs[0], [])):
            st_, dt_ = {"d": (E["d_w_dnT"], E["d_wdb"]),
                        "o": (E["d_w_out"], E["d_wob"])}[fam]
            sl = slice(ci * 1024, (ci + 1) * 1024)
            P.t_dma("gpsimd", dt_[sl, :], st_[sl, :],
                    r=[("ygd", pairs[min(j, len(pairs) - 1)] - 8)],
                    w=[("wcv", fam, ci)])
        P.add("vector", lambda e: e.memset(E["att"][:, :], 0.0), w=["att"])
        P.add("vector", lambda e: e.memset(E["kdiag"][:, :], 0.0), w=["kdiag"])
        for p in pairs:
            ti = p + 1
            slot = ti % 4
            vsl = E["vt"][:, slot * DI:(slot + 1) * DI]

            P.t_dma("gpsimd", E["CSB"][0:64, :],
                    E["d_csd"][2 * p:2 * p + 1, :].partition_broadcast(64),
                    w=["CSBa"])
            P.t_dma("gpsimd", E["CSB"][64:128, :],
                    E["d_csd"][2 * p + 1:2 * p + 2, :].partition_broadcast(64),
                    w=["CSBb"])
            P.t_dma("sync", E["zs_sl"][:, 0:DI], E["d_zs"][p - 8],
                    r=[("zsd", zb2, p - 8) for zb2 in range(8)], w=["zs_sl"])

            def xm(e, p=p):
                e.scalar_tensor_tensor(
                    r3(E["CSB"][:, :], 64, 64), r3(E["CSB"][:, :], 64, 64), -1.0,
                    bfree(E["CS_T"][:, p * 64:(p + 1) * 64], 64),
                    OP.mult, OP.add)
                return e.tensor_scalar_max(E["CSB"][:, :], E["CSB"][:, :], 0.0)
            P.add("vector", xm, r=["CSBa", "CSBb", "CS_T"],
                  w=["CSBa", "CSBb", "Xm"])
            P.add("scalar", lambda e: e.activation(
                E["Dmask"][:, :], E["CSB"][:, :], AF.Exp, scale=-1.0),
                r=["Xm"], w=["Dmask"])

            def qktr(e, p=p):
                e.transpose(PT[0:64, 0:128], E["q_tok"][:, p * 64:(p + 1) * 64],
                            E["id_bf"][:, :])
                e.transpose(PT[64:128, 128:192],
                            E["q_tok"][64:128, p * 64:(p + 1) * 64],
                            E["id_bf"][64:128, 64:128], tile_position=(64, 64))
                e.transpose(PT[0:64, 192:256],
                            E["k_tok"][0:64, p * 64:(p + 1) * 64],
                            E["id_bf"][0:64, 0:64])
                return e.transpose(PT[64:128, 256:320],
                                   E["k_tok"][64:128, p * 64:(p + 1) * 64],
                                   E["id_bf"][64:128, 64:128],
                                   tile_position=(64, 64))
            P.add("tensor", qktr, r=["q_tok", "k_tok", "id_bf"], w=["psT"])

            def qkev(e):
                e.tensor_copy(E["qstack"][0:64, :], PT[0:64, 0:64])
                e.tensor_copy(E["qflat_o"][:, :], PT[0:64, 64:128])
                e.tensor_copy(E["qstack"][64:128, :], PT[64:128, 128:192])
                e.tensor_copy(E["kdiag"][0:64, 0:64], PT[0:64, 192:256])
                return e.tensor_copy(E["kdiag"][64:128, 64:128],
                                     PT[64:128, 256:320])
            P.add("vector", qkev, r=["psT"], w=["qstack", "qflat_o", "kdiag"],
                  fast=True)

            P.add("tensor", lambda e: e.matmul(PG[:, :], E["kdiag"][:, :],
                                               E["qstack"][:, :], start=True,
                                               stop=True),
                  r=["kdiag", "qstack"], w=["psG"])
            P.add("vector", lambda e: e.tensor_tensor(
                E["G_sb"][:, :], PG[:, :], E["causal"][:, :], OP.mult),
                r=["psG", "causal"], w=["G_sb"])

            def attb(e):
                A3 = E["att"].rearrange("p (h s) -> p h s", h=64)
                e.tensor_tensor(
                    A3[0:64, :, 0:64],
                    E["G_sb"][0:64, :].unsqueeze(1).broadcast_to([64, 64, 64]),
                    r3(E["Dmask"][0:64, :], 64, 64), OP.mult)
                return e.tensor_tensor(
                    A3[64:128, :, 64:128],
                    E["G_sb"][64:128, :].unsqueeze(1).broadcast_to([64, 64, 64]),
                    r3(E["Dmask"][64:128, :], 64, 64), OP.mult)
            P.add("vector", attb, r=["G_sb", "Dmask"], w=["att"],
                  fast=True)

            P.add("vector", lambda e, vsl=vsl, p=p: e.tensor_tensor(
                r3(E["vscl"][:, :], 64, 64), r3(vsl, 64, 64),
                bfree(E["DEC_T"][:, p * 64:(p + 1) * 64], 64), OP.mult),
                r=[("v", ti), "DEC_T"], w=["vscl"])

            P.add("vector", lambda e: e.tensor_copy(E["h_bf"][:, :],
                                                    E["h_st"][:, :]),
                  r=["h_st"], w=["h_bf"])
            P.add("vector", lambda e, p=p: e.tensor_tensor(
                r3(E["h_st"][:, :], 64, 64), r3(E["h_st"][:, :], 64, 64),
                bfree(E["DBC64"][0:64, (2 * p) * 64:(2 * p + 1) * 64], 64),
                OP.mult), r=["h_st", "DBC64"], w=["h_st"])
            for b in range(8):
                P.add("tensor", lambda e, p=p, b=b: e.matmul(
                    (PSa if b % 2 == 0 else PSb)[:, :],
                    E["k_tok"][0:64, p * 64:(p + 1) * 64],
                    E["vscl"][0:64, b * 512:(b + 1) * 512],
                    start=True, stop=True),
                    r=["vscl", "k_tok", ("pssr", b % 2)], w=[("pss", b % 2)])
                P.add("vector", lambda e, b=b: e.tensor_add(
                    E["h_st"][:, b * 512:(b + 1) * 512],
                    E["h_st"][:, b * 512:(b + 1) * 512],
                    (PSa if b % 2 == 0 else PSb)[:, :]),
                    r=[("pss", b % 2), "h_st"], w=["h_st", ("pssr", b % 2)])
            P.add("vector", lambda e: e.tensor_copy(E["h_bf2"][:, :],
                                                    E["h_st"][:, :]),
                  r=["h_st"], w=["h_bf2"])
            P.add("vector", lambda e, p=p: e.tensor_tensor(
                r3(E["h_st"][:, :], 64, 64), r3(E["h_st"][:, :], 64, 64),
                bfree(E["DBC64"][0:64, (2 * p + 1) * 64:(2 * p + 2) * 64], 64),
                OP.mult), r=["h_st", "DBC64"], w=["h_st"])
            for b in range(8):
                P.add("tensor", lambda e, p=p, b=b: e.matmul(
                    (PSa if b % 2 == 0 else PSb)[:, :],
                    E["k_tok"][64:128, p * 64:(p + 1) * 64],
                    E["vscl"][64:128, b * 512:(b + 1) * 512],
                    start=True, stop=True, tile_position=(64, 0)),
                    r=["vscl", "k_tok", ("pssr", b % 2)], w=[("pss", b % 2)])
                P.add("vector", lambda e, b=b: e.tensor_add(
                    E["h_st"][:, b * 512:(b + 1) * 512],
                    E["h_st"][:, b * 512:(b + 1) * 512],
                    (PSa if b % 2 == 0 else PSb)[:, :]),
                    r=[("pss", b % 2), "h_st"], w=["h_st", ("pssr", b % 2)])
            for b in range(8):
                PA = PAa if b % 2 == 0 else PAb
                PB = PBa if b % 2 == 0 else PBb
                P.add("tensor", lambda e, b=b, PA=PA: e.matmul(
                    PA[0:64, :], E["qstack"][0:64, :],
                    E["h_bf"][:, b * 512:(b + 1) * 512],
                    start=True, stop=False, skip_group_check=True),
                    r=["h_bf", "qstack", ("psar", b % 2)], w=[("psa", b % 2)])
                P.add("tensor", lambda e, b=b, PA=PA: e.matmul(
                    PA[64:128, :], E["qflat_o"][:, :],
                    E["h_bf2"][:, b * 512:(b + 1) * 512],
                    start=True, stop=True, tile_position=(0, 64),
                    skip_group_check=True),
                    r=["h_bf2", "qflat_o"], w=[("psa", b % 2)])

                def intra(e, b=b, vsl=vsl, PB=PB):
                    ins = None
                    for hh in range(8):
                        h = b * 8 + hh
                        ins = e.matmul(
                            PB[:, hh * 64:(hh + 1) * 64],
                            E["att"][:, h * 128:(h + 1) * 128],
                            vsl[:, h * 64:(h + 1) * 64],
                            start=(hh == 0), stop=(hh == 7),
                            skip_group_check=True)
                    return ins
                P.add("tensor", intra, r=["att", ("v", ti), ("psbr", b % 2)],
                      w=[("psb", b % 2)])

                def yev_a(e, p=p, b=b, vsl=vsl, PA=PA):
                    e.tensor_tensor(
                        r3(E["YEV"][:, :], 8, 64), r3(PA[:, :], 8, 64),
                        bfree(E["ECS_T"][:, p * 64 + b * 8:p * 64 + b * 8 + 8], 64),
                        OP.mult)
                    return e.tensor_tensor(
                        r3(E["DVS"][:, :], 8, 64),
                        r3(vsl[:, b * 512:(b + 1) * 512], 8, 64),
                        bfree(E["DP"][:, b * 8:b * 8 + 8], 64),
                        OP.mult)
                P.add("vector", yev_a,
                      r=[("psa", b % 2), ("v", ti), "ECS_T", "DP"],
                      w=[("yevt", b % 2), ("psar", b % 2)], fast=True)

                def yev_b(e, b=b, PB=PB):
                    e.tensor_add(E["YEV2"][:, :], E["YEV"][:, :], PB[:, :])
                    return e.tensor_add(E["y_sl"][:, b * 512:(b + 1) * 512],
                                        E["YEV2"][:, :], E["DVS"][:, :])
                P.add("vector", yev_b,
                      r=[("yevt", b % 2), ("psb", b % 2)],
                      w=[("yevt", b % 2), ("ysl", b), ("psbr", b % 2)])
                P.add("scalar", lambda e, b=b: e.activation(
                    E["sqs"][:, :], E["y_sl"][:, b * 512:(b + 1) * 512],
                    AF.Square, accum_out=E["sqc"][:, b:b + 1]),
                    r=[("ysl", b)], w=[("sqc", b)])

            def rsy1(e):
                e.reduce_sum(E["nwt"][:, :], E["sqc"][:, :], axis=AX.X)
                return e.tensor_scalar(E["nwt"][:, :], E["nwt"][:, :],
                                       1.0 / DI, 1e-6, OP.mult, OP.add)
            P.add("vector", rsy1, r=[("sqc", b) for b in range(8)], w=["nwt"])
            P.add("scalar", lambda e: e.activation(E["rstd_y"][:, :],
                                                   E["nwt"][:, :], AF.Sqrt),
                  r=["nwt"], w=["rsq"])

            def rsy2(e):
                RS, T = E["rstd_y"], E["nwt"]
                R0 = E["sqc"][:, 1:2]
                e.reciprocal(R0, RS[:, :])
                e.tensor_mul(E["sqc"][:, 0:1], R0, R0)
                e.tensor_mul(E["sqc"][:, 0:1], E["sqc"][:, 0:1], T[:, :])
                e.tensor_scalar(E["sqc"][:, 0:1], E["sqc"][:, 0:1], -0.5, 1.5,
                                OP.mult, OP.add)
                return e.tensor_mul(RS[:, :], R0, E["sqc"][:, 0:1])
            P.add("vector", rsy2, r=["rsq"], w=["rstd_y", ("sqc", 0),
                                               ("sqc", 1)])

            if p == 8 and "probe2" in C.dbg:
                for nm, src_ap, rt in [
                        ("py", E["y_sl"][:, :].bitcast(mybir.dt.uint16), None),
                        ("patt", E["att"][:, :].bitcast(mybir.dt.uint16), "att"),
                        ("pG", E["G_sb"][:, :].bitcast(mybir.dt.uint16), "G_sb"),
                        ("pD", E["Dmask"][:, :].bitcast(mybir.dt.uint16), "Dmask"),
                        ("phb", E["h_bf"][:, :].bitcast(mybir.dt.uint16), "h_bf"),
                        ("pqs", E["qstack"][:, :].bitcast(mybir.dt.uint16), "qstack"),
                        ("pkd", E["kdiag"][:, :].bitcast(mybir.dt.uint16), "kdiag"),
                        ("pvs", E["vscl"][:, :].bitcast(mybir.dt.uint16), "vscl")]:
                    rr = [rt] if rt else [("ysl", b) for b in range(8)]
                    P.t_dma("gpsimd",
                            E["d_" + nm].bitcast(mybir.dt.uint16
                                                 )[:, 0:src_ap.shape[1]],
                            src_ap, r=rr, w=["prb2"])
                P.add("sync", lambda e: None, r=["prb2"])
            P.add("vector", lambda e: e.scalar_tensor_tensor(
                E["yg_sl"][:, 0:DI], E["y_sl"][:, :], E["rstd_y"][:, 0:1],
                E["zs_sl"][:, 0:DI], OP.mult, OP.mult),
                r=["rstd_y", "zs_sl"] + [("ysl", b) for b in range(8)],
                w=["yg_sl"])
            P.t_dma("gpsimd", E["d_yg"][p - 8], E["yg_sl"][:, 0:DI],
                    r=["yg_sl"], w=[("ygd", p - 8)])
        P.add("sync", lambda e: None, r=[("ygd", p2 - 8) for p2 in pairs])
        P.run()
    st.close()


def _p6_main(C, E):
    _p6_conv_blk(C, E, 0)
    _p6_prefix(C, E, (0, 1, 2), True, False)
    _p6_conv_blk(C, E, 1)
    _p6_prefix(C, E, (3, 4, 5, 6), False, False)
    _p6_conv_blk(C, E, 2)
    _p6_prefix(C, E, (7,), False, True)
    _p6_ssd(C, E, (8, 9, 10))
    _p6_conv_blk(C, E, 3)
    _p6_ssd(C, E, (11, 12, 13, 14))
    _p6_conv_blk(C, E, 4)
    _p6_ssd(C, E, (15,))


def _p7_ffn_out(C, E):
    nc = C.nc
    with nc.sbuf_tensor("FFN_O", [128, 16 * 512], F32) as FFN_O:
        for tg in range(2):
            with nc.sbuf_tensor(f"HID_{tg}", [128, 64 * 512], BF16) as HID:
                E["HID"] = HID
                _p7_up(C, E, tg)
                _p7_dn(C, E, tg, FFN_O)
            _p7_out(C, E, tg, FFN_O)


def _p7_up(C, E, tg):
    nc = C.nc
    with nc.sbuf_tensor(f"XMV7_{tg}", [128, 16 * 512], BF16) as XMV, \
         nc.sbuf_tensor(f"SWUP_{tg}", [128, 2 * 16 * 512], BF16) as SWUP, \
         nc.psum_tensor(f"ps7ua_{tg}", [128, 512], F32) as PUA, \
         nc.psum_tensor(f"ps7ub_{tg}", [128, 512], F32) as PUB:
        P = Phase(C, f"up{tg}")
        SW4 = SWUP.rearrange("p (r k c) -> p r k c", r=2, k=16)
        HID3 = r3(E["HID"][:, :], 64, 512)
        P.t_dma("sync", r3(XMV[:, :], 16, 512),
                E["d_xnf"].rearrange("k d t -> d k t"
                                     )[:, :, 1152 + tg * 512:1152 + (tg + 1) * 512],
                r=["xnf_ready"], w=["XMV"])
        for mg in range(16):
            P.t_dma("sync", SW4[:, mg % 2],
                    E["d_wub"].rearrange("(k p) c -> p k c", p=128
                                         )[:, :, mg * 512:(mg + 1) * 512],
                    r=[("swu", mg % 2)] + [("wcv", "u", i4) for i4 in range(8)],
                    w=[("swraw", mg % 2)])
            P.add("scalar", lambda e, mg=mg: e.activation(
                SWUP.rearrange("p (r c) -> p r c", r=2)[:, mg % 2],
                SWUP.rearrange("p (r c) -> p r c", r=2)[:, mg % 2], AF.Sign),
                r=[("swraw", mg % 2)], w=[("sw", mg % 2)])
            for mm in range(4):
                m = mg * 4 + mm
                par = m % 2
                ps = PUA if par == 0 else PUB

                def mmu(e, mg=mg, mm=mm, ps=ps):
                    ins = None
                    for k in range(16):
                        ins = e.matmul(ps[:, :],
                                       SW4[:, mg % 2, k, mm * 128:(mm + 1) * 128],
                                       r3(XMV[:, :], 16, 512)[:, k, :],
                                       start=(k == 0), stop=(k == 15))
                    return ins
                P.add("tensor", mmu,
                      r=[("sw", mg % 2), "XMV", ("pur", par)],
                      w=[("pu", par)] + ([("swu", mg % 2)] if mm == 3 else []))
                P.add("vector", lambda e, m=m, ps=ps: e.tensor_scalar(
                    HID3[:, m, :], ps[:, :], E["usc_sb"][:, m:m + 1], 0.0,
                    OP.mult, OP.max),
                    r=[("pu", par), "usc_sb"], w=[("hid", m), ("pur", par)])
        P.run()


def _p7_dn(C, E, tg, FFN_O):
    nc = C.nc
    with nc.sbuf_tensor(f"SWDN_{tg}", [128, 2 * 32 * 512], BF16) as SWDN, \
         nc.psum_tensor(f"ps7d0_{tg}", [128, 512], F32) as PD0, \
         nc.psum_tensor(f"ps7d1_{tg}", [128, 512], F32) as PD1, \
         nc.psum_tensor(f"ps7d2_{tg}", [128, 512], F32) as PD2, \
         nc.psum_tensor(f"ps7d3_{tg}", [128, 512], F32) as PD3:
        PDS = [PD0, PD1, PD2, PD3]
        P = Phase(C, f"dn{tg}")
        SW4 = SWDN.rearrange("p (r k c) -> p r k c", r=2, k=32)
        HID3 = r3(E["HID"][:, :], 64, 512)
        for dmg in range(4):
            for kh in range(2):
                half = dmg * 2 + kh
                P.t_dma("sync", SW4[:, half % 2],
                        E["d_wdb"].rearrange("(k p) c -> p k c", p=128
                                             )[:, kh * 32:(kh + 1) * 32,
                                               dmg * 512:(dmg + 1) * 512],
                        r=[("swdu", half % 2)] +
                          [("wcv", "d", kh * 4 + j4) for j4 in range(4)],
                        w=[("swdraw", half % 2)])
                P.add("scalar", lambda e, half=half: e.activation(
                    SWDN.rearrange("p (r c) -> p r c", r=2)[:, half % 2],
                    SWDN.rearrange("p (r c) -> p r c", r=2)[:, half % 2],
                    AF.Sign), r=[("swdraw", half % 2)], w=[("swd", half % 2)])
                for dmm in range(4):
                    def mmd(e, kh=kh, dmm=dmm, dmg=dmg, half=half):
                        ins = None
                        for kk in range(32):
                            k = kh * 32 + kk
                            ins = e.matmul(
                                PDS[dmm][:, :],
                                SW4[:, half % 2, kk, dmm * 128:(dmm + 1) * 128],
                                HID3[:, k, :],
                                start=(kh == 0 and kk == 0),
                                stop=(kh == 1 and kk == 31))
                        return ins
                    P.add("tensor", mmd,
                          r=[("swd", half % 2), ("pdr", dmm)] +
                            [("hid", k2) for k2 in range(kh * 32, kh * 32 + 32)],
                          w=[("pd", dmm)] + ([("swdu", half % 2)]
                                             if dmm == 3 else []))
            for dmm in range(4):
                dmc = dmg * 4 + dmm
                P.add("vector", lambda e, dmm=dmm, dmc=dmc: e.tensor_scalar(
                    FFN_O[:, dmc * 512:(dmc + 1) * 512], PDS[dmm][:, :],
                    E["dsc_sb"][:, dmc:dmc + 1], None, OP.mult),
                    r=[("pd", dmm), "dsc_sb"], w=[("ffo", dmc), ("pdr", dmm)])
        P.run()


def _p7_out(C, E, tg, FFN_O):
    nc = C.nc
    with nc.sbuf_tensor(f"YGL_{tg}", [128, 2 * DI], BF16) as YGL, \
         nc.sbuf_tensor(f"YGT_{tg}", [128, 32 * 512], BF16) as YGT, \
         nc.sbuf_tensor(f"WOS_{tg}", [128, 32 * 512], BF16) as WOS, \
         nc.sbuf_tensor(f"UTS_{tg}", [128, 4 * 2048], F32) as UTS, \
         nc.sbuf_tensor(f"OSB_{tg}", [128, 4 * 2048], F32) as OSB, \
         nc.sbuf_tensor(f"SSL_{tg}", [128, 512], F32) as SSL, \
         nc.psum_tensor(f"ps7oa_{tg}", [128, 512], F32) as POA, \
         nc.psum_tensor(f"ps7ob_{tg}", [128, 512], F32) as POB, \
         nc.psum_tensor(f"ps7ty_{tg}", [128, 1024], BF16) as PTY, \
         nc.psum_tensor(f"ps7ka_{tg}", [128, 512], F32) as PKA, \
         nc.psum_tensor(f"ps7kb_{tg}", [128, 512], F32) as PKB:
        P = Phase(C, f"out{tg}")
        YGT3 = r3(YGT[:, :], 32, 512)
        WOS3 = r3(WOS[:, :], 32, 512)
        for ts in range(4):
            row = 1152 + tg * 512 + ts * 128
            P.t_dma("sync", UTS[:, ts * 2048:(ts + 1) * 2048],
                    E["d_u_tok"][row:row + 128, :], w=[("uts", ts)])
        for i in range(4):
            P.t_dma("sync", YGL[:, (i % 2) * DI:(i % 2 + 1) * DI],
                    E["d_yg"][tg * 4 + i], r=[("ygl", i % 2), ("ygd", tg * 4 + i)],
                    w=[("ygld", i % 2)])
            for half in range(4):
                def ytr(e, i=i, half=half):
                    ins = None
                    for kk in range(8):
                        k = half * 8 + kk
                        ins = e.transpose(
                            PTY[:, kk * 128:(kk + 1) * 128],
                            YGL[:, (i % 2) * DI + k * 128:(i % 2) * DI + (k + 1) * 128],
                            E["id_bf"][:, :])
                    return ins
                P.add("tensor", ytr, r=[("ygld", i % 2), "id_bf", ("ptyr",)],
                      w=[("pty",)])

                def yev(e, i=i, half=half):
                    return e.tensor_copy(
                        YGT3[:, half * 8:(half + 1) * 8, i * 128:(i + 1) * 128],
                        r3(PTY[:, :], 8, 128))
                P.add("vector", yev, r=[("pty",)],
                      w=[("ygt", i, half), ("ptyr",)] +
                        ([("ygl", i % 2)] if half == 3 else []))
        for mg in range(4):
            P.t_dma("sync", WOS3,
                    E["d_wob"].rearrange("(k p) c -> p k c", p=128
                                         )[:, :, mg * 512:(mg + 1) * 512],
                    r=[("wosu",)] + [("wcv", "o", i4) for i4 in range(4)],
                    w=[("wos_raw",)])

            P.add("vector", lambda e: e.tensor_tensor(
                WOS3, WOS3, bfree(E["nsc_sb"][:, 0:32], 512), OP.mult),
                r=[("wos_raw",), "nsc_sb"], w=[("wos",)])
            for m in range(4):
                dmc = mg * 4 + m
                par = m % 2
                po = POA if par == 0 else POB

                def mmo(e, m=m, po=po):
                    ins = None
                    for k in range(32):
                        ins = e.matmul(po[:, :], WOS3[:, k, m * 128:(m + 1) * 128],
                                       YGT3[:, k, :], start=(k == 0),
                                       stop=(k == 31))
                    return ins
                P.add("tensor", mmo,
                      r=[("wos",), ("por", par)] +
                        [("ygt", i, half) for i in range(4) for half in range(4)],
                      w=[("po", par)] + ([("wosu",)] if m == 3 else []))
                P.add("vector", lambda e, po=po, dmc=dmc: e.tensor_add(
                    SSL[:, :], po[:, :], FFN_O[:, dmc * 512:(dmc + 1) * 512]),
                    r=[("po", par), ("ffo", dmc), ("sslr",)],
                    w=[("ssl",), ("por", par)])

                def ktr(e, dmc=dmc, par=par):
                    ins = None
                    pk = PKA if par == 0 else PKB
                    for ts in range(4):
                        ins = e.transpose(pk[:, ts * 128:(ts + 1) * 128],
                                          SSL[:, ts * 128:(ts + 1) * 128],
                                          E["id_f32"][:, :])
                    return ins
                P.add("tensor", ktr, r=[("ssl",), "id_f32", ("pkr", par)],
                      w=[("pk", par), ("sslr",)])

                def radd(e, dmc=dmc, par=par):
                    ins = None
                    pk = PKA if par == 0 else PKB
                    for ts in range(4):
                        ins = e.tensor_add(
                            OSB[:, ts * 2048 + dmc * 128:ts * 2048 + (dmc + 1) * 128],
                            pk[:, ts * 128:(ts + 1) * 128],
                            UTS[:, ts * 2048 + dmc * 128:ts * 2048 + (dmc + 1) * 128])
                    return ins
                P.add("vector", radd,
                      r=[("pk", par)] + [("uts", ts) for ts in range(4)],
                      w=[("osb", dmc), ("pkr", par)], fast=True)
        for ts in range(4):
            P.t_dma("gpsimd", E["d_out"][tg * 512 + ts * 128:tg * 512 + (ts + 1) * 128, :],
                    OSB[:, ts * 2048:(ts + 1) * 2048],
                    r=[("osb", dmc2) for dmc2 in range(16)], w=["outw"])
        P.add("sync", lambda e: None, r=["outw"])
        P.run()


_CACHED = {}


def _stage_inputs(inputs):
    u = np.ascontiguousarray(np.asarray(inputs["L0_attn_u"], dtype=np.float32))
    w_in = np.ascontiguousarray(np.asarray(inputs["L0_attn_in_proj_w"], np.float32))
    convw = np.asarray(inputs["L0_attn_conv_weight"], np.float32)
    convb = np.asarray(inputs["L0_attn_conv_bias"], np.float32)
    A_log = np.asarray(inputs["L0_attn_A_log"], np.float32)
    dt_bias = np.asarray(inputs["L0_attn_dt_bias"], np.float32)
    D_param = np.asarray(inputs["L0_attn_D_param"], np.float32)
    norm_scale = np.asarray(inputs["L0_attn_norm_scale"], np.float32)
    w_out = np.ascontiguousarray(np.asarray(inputs["L0_attn_out_proj_w"], np.float32))
    w_upT = np.ascontiguousarray(
        np.asarray(inputs["L0_ffn_up_weight"], np.float32).T)
    w_dnT = np.ascontiguousarray(
        np.asarray(inputs["L0_ffn_down_weight"], np.float32).T)
    up_scale = np.asarray(inputs["L0_ffn_up_scale"], np.float32)
    down_scale = np.asarray(inputs["L0_ffn_down_scale"], np.float32)
    out_scale = np.asarray(inputs["L0_ffn_output_scale"], np.float32)

    cw = np.ascontiguousarray(
        convw.T.reshape(32, 128, 4).transpose(1, 0, 2).reshape(128, 128))
    cb = np.ascontiguousarray(convb.reshape(32, 128).T)
    nsc = np.ascontiguousarray(norm_scale.reshape(32, 128).T)
    usc = np.ascontiguousarray(up_scale.reshape(64, 128).T)
    dsc = np.ascontiguousarray(down_scale.reshape(16, 128).T)
    vecs = np.ascontiguousarray(np.stack([A_log, dt_bias, D_param]))
    osc = out_scale.reshape(1, 1)

    shared = dict(w_in=w_in, w_upT=w_upT, w_dnT=w_dnT, w_out=w_out,
                  cw=cw, cb=cb, nsc=nsc, usc=usc, dsc=dsc, vecs=vecs, osc=osc)
    zeros = np.zeros((1024, DM), np.float32)
    halo = np.zeros((128, DM), np.float32)
    in_maps = []
    for core in range(8):
        b, half = core // 2, core % 2
        prefix = u[b, 0:1024] if half == 1 else zeros
        own = u[b, half * 1024:(half + 1) * 1024]
        u_tok = np.ascontiguousarray(np.concatenate([halo, prefix, own], 0))
        m = dict(shared)
        m["u_tok"] = u_tok
        m["flag"] = np.array([[1.0 if half == 1 else 0.0]], np.float32)
        in_maps.append(m)
    return in_maps


def kernel(**inputs):
    if "nc" not in _CACHED:
        _CACHED["nc"] = build_program()
    nc = _CACHED["nc"]
    in_maps = _stage_inputs(inputs)
    res = run_bass_kernel_spmd(nc, in_maps, core_ids=list(range(8)))
    out = np.empty((4, 2048, DM), np.float32)
    for core in range(8):
        b, half = core // 2, core % 2
        out[b, half * 1024:(half + 1) * 1024] = res.results[core]["out"]
    return out
